# version 09
# baseline (speedup 1.0000x reference)
import math
import contextlib
import numpy as np
import ml_dtypes

import concourse.bass as bass
import concourse.mybir as mybir
from concourse.bass_utils import run_bass_kernel_spmd

AF = mybir.ActivationFunctionType
ALU = mybir.AluOpType
F32 = mybir.dt.float32
BF16 = mybir.dt.bfloat16
U8 = mybir.dt.uint8
I32 = mybir.dt.int32
BF = ml_dtypes.bfloat16

D = 4096
II = 11008
NH = 32
HD = 128
S = 2048
BATCH = 4
M = 1024
MC = 512
NCORES = 8
NEG = -60000.0
SCALE_Q = float(HD) ** -0.5
EPS = 1e-5
MASK32 = 0x0F0F0F0F
TAU = [0, 2, 4, 6, 1, 3, 5, 7]


def _pi(n):
    q = np.arange(n)
    return 8 * (q // 8) + np.array(TAU)[q % 8]


PI_D = _pi(D)
PI_I = _pi(II)
SLOPES = [(2.0 ** (-2.0 ** (-(math.log2(NH) - 3)))) ** (i + 1) for i in range(NH)]



def _qtiles(parity):
    return [2 * t + parity for t in range(8)]


def _bias_table(parity):
    out = np.full((8, 128, 2048), NEG, np.float32)
    myt = _qtiles(parity)
    pat = _qtiles(1 - parity)
    for t in range(8):
        posq = myt[t] * 128 + np.arange(128)
        for j in range(t + 1):
            dA = (myt[j] * 128 + np.arange(128))[None, :] - posq[:, None]
            out[t, :, j * 128:(j + 1) * 128] = np.where(dA <= 0, dA, NEG)
            dB = (pat[j] * 128 + np.arange(128))[None, :] - posq[:, None]
            out[t, :, 1024 + j * 128:1024 + (j + 1) * 128] = np.where(dB <= 0, dB, NEG)
    return out


def _pretile_packed(qw, kt):
    r, w = qw.shape
    assert r == kt * 128 and w % 32 == 0
    nchunks = w // 32
    a = qw.reshape(kt, 128, nchunks, 32)
    return np.ascontiguousarray(a.transpose(2, 1, 0, 3))


def prep_inputs(inp):
    pc = [{} for _ in range(NCORES)]
    x = np.asarray(inp["L0_attn_hidden_states"])
    ident = np.eye(128, dtype=BF)
    for c in range(NCORES):
        p, j = c // 2, c % 2
        toks = np.concatenate([np.arange(g * 128, (g + 1) * 128) for g in _qtiles(j)])
        xc = x[p][toks][:, PI_D]
        pc[c]["x_in"] = np.ascontiguousarray(xc.T.astype(np.float32))
        pc[c]["bias_tab"] = np.ascontiguousarray(_bias_table(j))
        pc[c]["ident"] = ident

    sh = {}
    for L in range(2):
        pre = f"L{L}_"
        if L == 0:
            wqkv = np.asarray(inp["L0_attn_Wqkv"])
            wo = np.asarray(inp["L0_attn_Wo"])
        else:
            wqkv = np.concatenate(
                [np.asarray(inp["L1_attn_Wq"]), np.asarray(inp["L1_attn_Wkv"])], axis=1)
            wo = np.asarray(inp["L1_attn_Wo"])
        sh[pre + "wqkv"] = np.ascontiguousarray(wqkv[PI_D, :].astype(np.float32))
        sh[pre + "wo"] = np.ascontiguousarray(wo[:, PI_D].astype(np.float32))
        f = f"L{L}_ffn_"
        sh[pre + "pk_g"] = _pretile_packed(np.asarray(inp[f + "gate_qweight"])[PI_D, :], 32)
        sh[pre + "pk_u"] = _pretile_packed(np.asarray(inp[f + "up_qweight"])[PI_D, :], 32)
        sh[pre + "pk_d"] = _pretile_packed(np.asarray(inp[f + "down_qweight"])[PI_I, :], 86)
        sh[pre + "sc_g"] = np.ascontiguousarray(np.asarray(inp[f + "gate_scales"])[:, PI_I].astype(np.float32))
        sh[pre + "sc_u"] = np.ascontiguousarray(np.asarray(inp[f + "up_scales"])[:, PI_I].astype(np.float32))
        sh[pre + "sc_d"] = np.ascontiguousarray(np.asarray(inp[f + "down_scales"])[:, PI_D].astype(np.float32))
        sh[pre + "qz_g"] = np.ascontiguousarray(np.asarray(inp[f + "gate_qzeros"], np.int32))
        sh[pre + "qz_u"] = np.ascontiguousarray(np.asarray(inp[f + "up_qzeros"], np.int32))
        sh[pre + "qz_d"] = np.ascontiguousarray(np.asarray(inp[f + "down_qzeros"], np.int32))
    return sh, pc


def unprep_output(outs_by_core):
    res = np.empty((BATCH, S, D), np.float32)
    inv = np.argsort(PI_D)
    for c in range(NCORES):
        p, j = c // 2, c % 2
        toks = np.concatenate([np.arange(g * 128, (g + 1) * 128) for g in _qtiles(j)])
        res[p][toks] = np.asarray(outs_by_core[c], np.float32).T[:, inv]
    return res



class Prog:
    def __init__(self, name):
        self.nc = bass.Bass("TRN2", target_bir_lowering=False)
        self.es = contextlib.ExitStack()
        self.cur = self.es
        self.name = name

    def sb(self, name, shape, dt):
        return self.cur.enter_context(self.nc.sbuf_tensor(name, shape, dt))

    def ps(self, name, shape, dt=F32):
        return self.cur.enter_context(self.nc.psum_tensor(name, shape, dt))

    def sem(self, name):
        return self.cur.enter_context(self.nc.semaphore(name))

    @contextlib.contextmanager
    def stage(self):
        old = self.cur
        with contextlib.ExitStack() as st_es:
            self.cur = st_es
            yield
        self.cur = old
        self.nc.reset()

    def din(self, name, shape, dt):
        return self.nc.dram_tensor(name, list(shape), dt, kind="ExternalInput").ap()

    def dout(self, name, shape, dt):
        return self.nc.dram_tensor(name, list(shape), dt, kind="ExternalOutput").ap()

    def dint(self, name, shape, dt, shared=False):
        kw = {"addr_space": "Shared"} if shared else {}
        return self.nc.dram_tensor(name, list(shape), dt, kind="Internal", **kw).ap()


def make_consts(P, values, tag="cst"):
    tiles = {v: P.sb(f"{tag}_{i}", [128, 1], F32) for i, v in enumerate(values)}
    with P.nc.Block() as blk:
        @blk.vector
        def _(e):
            for v, t in tiles.items():
                e.memset(t[:], float(v))
    return tiles


def load_rows(e, dst_sb, src_dram, kt_n, sem, grp=8):
    n = 0
    for k0 in range(0, kt_n, grp):
        k1 = min(kt_n, k0 + grp)
        e.dma_start(
            out=dst_sb[:, k0 * M:k1 * M],
            in_=src_dram.rearrange("(kt p) m -> p kt m", p=128)[:, k0:k1, :],
        ).then_inc(sem, 16)
        n += 1
    return n



def emit_ln(P, src_dram, dst_dram, res_dram=None, tag="ln"):
    nc = P.nc
    kt_n = 32
    src_is_bf = (src_dram.dtype == BF16)
    inv_n = 1.0 / float(kt_n * 128)
    out_f32 = res_dram is not None

    x_sb = P.sb(tag + "_x", [128, kt_n * M], BF16 if src_is_bf else F32)
    xb_sb = None if src_is_bf else P.sb(tag + "_xb", [128, 2, M], BF16)
    sq_sb = P.sb(tag + "_sq", [128, 2, M], BF16)
    ones_col = P.sb(tag + "_oc", [128, 1], BF16)
    ones_row = P.sb(tag + "_or", [1, 128], F32)
    vq = P.sb(tag + "_vq", [1, M], F32)
    vmean = P.sb(tag + "_vm", [1, M], F32)
    vrstd = P.sb(tag + "_vr", [1, M], F32)
    vmr = P.sb(tag + "_vmr", [1, M], F32)
    vtmp = P.sb(tag + "_vt", [1, M], F32)
    rstd_rep = P.sb(tag + "_rr", [128, M], F32)
    mr_rep = P.sb(tag + "_mrr", [128, M], F32)
    res_sb = P.sb(tag + "_res", [128, 2, M], F32) if res_dram is not None else None
    out_sb = P.sb(tag + "_out", [128, 2, M], F32 if out_f32 else BF16)
    tmp_sb = P.sb(tag + "_tmp", [128, M], F32)

    ps_s = P.ps(tag + "_ps", [1, M])
    ps_q = P.ps(tag + "_pq", [1, M])
    ps_r1 = P.ps(tag + "_pr1", [128, M])
    ps_r2 = P.ps(tag + "_pr2", [128, M])

    ld = P.sem(tag + "_ld")
    cv = P.sem(tag + "_cv")
    cq = P.sem(tag + "_cq")
    mm = P.sem(tag + "_mm")
    ev = P.sem(tag + "_ev")
    st = P.sem(tag + "_st")
    rl = P.sem(tag + "_rl")
    nv = P.sem(tag + "_nv")

    with nc.Block() as blk:
        @blk.sync
        def _(e):
            load_rows(e, x_sb, src_dram, kt_n, ld)

        if not src_is_bf:
            @blk.vector
            def _(e):
                e.memset(ones_col[:], 1.0)
                e.memset(ones_row[:], 1.0)
                for kt in range(kt_n):
                    e.wait_ge(ld, 16 * (kt // 8 + 1))
                    if kt >= 2:
                        e.wait_ge(mm, kt - 1)
                    e.tensor_copy(xb_sb[:, kt % 2, :],
                                  x_sb[:, kt * M:(kt + 1) * M]).then_inc(cv, 1)
        else:
            @blk.vector
            def _(e):
                e.memset(ones_col[:], 1.0)
                e.memset(ones_row[:], 1.0).then_inc(cv, 1)

        @blk.scalar
        def _(e):
            for kt in range(kt_n):
                e.wait_ge(ld, 16 * (kt // 8 + 1))
                if kt >= 2:
                    e.wait_ge(mm, kt - 1)
                e.activation(out=sq_sb[:, kt % 2, :], in_=x_sb[:, kt * M:(kt + 1) * M],
                             func=AF.Square).then_inc(cq, 1)

        @blk.tensor
        def _(e):
            for kt in range(kt_n):
                e.wait_ge(cv, 1 if src_is_bf else kt + 1)
                e.wait_ge(cq, kt + 1)
                xsrc = (x_sb[:, kt * M:(kt + 1) * M] if src_is_bf
                        else xb_sb[:, kt % 2, :])
                for mc in range(2):
                    e.matmul(ps_s[:, mc * MC:(mc + 1) * MC], ones_col[:],
                             xsrc[:, mc * MC:(mc + 1) * MC],
                             start=(kt == 0), stop=(kt == kt_n - 1))
                    i2 = e.matmul(ps_q[:, mc * MC:(mc + 1) * MC], ones_col[:],
                                  sq_sb[:, kt % 2, mc * MC:(mc + 1) * MC],
                                  start=(kt == 0), stop=(kt == kt_n - 1))
                i2.then_inc(mm, 1)

    with nc.Block() as blk:
        @blk.scalar
        def _(e):
            e.copy(vmean[:], ps_s[:])
            e.copy(vq[:], ps_q[:]).then_inc(ev, 1)
            e.wait_ge(ev, 2)
            e.activation(out=vq[:], in_=vq[:], func=AF.Sqrt).then_inc(ev, 1)

        @blk.vector
        def _(e):
            e.wait_ge(ev, 1)
            e.tensor_scalar_mul(vmean[:], vmean[:], inv_n)
            e.tensor_scalar_mul(vq[:], vq[:], inv_n)
            e.drain()
            e.tensor_tensor(out=vtmp[:], in0=vmean[:], in1=vmean[:], op=ALU.mult)
            e.drain()
            e.tensor_sub(vq[:], vq[:], vtmp[:])
            e.drain()
            e.tensor_scalar_add(vq[:], vq[:], EPS).then_inc(ev, 1)
            e.wait_ge(ev, 3)
            e.reciprocal(vrstd[:], vq[:])
            e.drain()
            e.tensor_mul(vmr[:], vmean[:], vrstd[:]).then_inc(ev, 1)

        @blk.tensor
        def _(e):
            e.wait_ge(ev, 4)
            for mc in range(2):
                e.matmul(ps_r1[:, mc * MC:(mc + 1) * MC], ones_row[:],
                         vrstd[:, mc * MC:(mc + 1) * MC], start=True, stop=True)
                i2 = e.matmul(ps_r2[:, mc * MC:(mc + 1) * MC], ones_row[:],
                              vmr[:, mc * MC:(mc + 1) * MC], start=True, stop=True)
            i2.then_inc(mm, 1)

    with nc.Block() as blk:
        @blk.scalar
        def _(e):
            e.copy(rstd_rep[:], ps_r1[:])
            e.copy(mr_rep[:], ps_r2[:]).then_inc(ev, 1)

        @blk.sync
        def _(e):
            if res_dram is not None:
                for kt in range(kt_n):
                    if kt >= 2:
                        e.wait_ge(nv, kt - 1)
                    e.dma_start(out=res_sb[:, kt % 2, :],
                                in_=res_dram[kt * 128:(kt + 1) * 128, :]).then_inc(rl, 16)

        @blk.vector
        def _(e):
            e.wait_ge(ev, 5)
            for kt in range(kt_n):
                xt = x_sb[:, kt * M:(kt + 1) * M]
                if kt >= 2:
                    e.wait_ge(st, 16 * (kt - 1))
                e.tensor_mul(tmp_sb[:], xt, rstd_rep[:])
                e.drain()
                if res_dram is not None:
                    e.wait_ge(rl, 16 * (kt + 1))
                    e.tensor_sub(tmp_sb[:], tmp_sb[:], mr_rep[:])
                    e.drain()
                    e.tensor_add(out_sb[:, kt % 2, :], tmp_sb[:],
                                 res_sb[:, kt % 2, :]).then_inc(nv, 1)
                else:
                    e.tensor_sub(out_sb[:, kt % 2, :], tmp_sb[:],
                                 mr_rep[:]).then_inc(nv, 1)

        @blk.gpsimd
        def _(e):
            for kt in range(kt_n):
                e.wait_ge(nv, kt + 1)
                e.dma_start(out=dst_dram[kt * 128:(kt + 1) * 128, :],
                            in_=out_sb[:, kt % 2, :]).then_inc(st, 16)
            e.wait_ge(st, 16 * kt_n)



def emit_proj_f32w(P, w_dram, rhs_sb, out_dram, kt_n, nt_n, qscale_tiles=0, tag="pj",
                   scale_ap=None):
    nc = P.nc
    NCB = 256
    ncb_n = nt_n * 128 // NCB

    wland = P.sb(tag + "_wl", [128, 2 * NCB], F32)
    w_sb = P.sb(tag + "_w", [128, 2, kt_n * NCB], BF16)
    out_sb = P.sb(tag + "_o", [128, 4, MC], BF16)
    psum = [P.ps(tag + f"_p{i}", [128, MC]) for i in range(8)]

    wld = [P.sem(tag + f"_wld{i}") for i in range(2)]
    wcv = P.sem(tag + "_wcv")
    mmg = P.sem(tag + "_mmg")
    ev = P.sem(tag + "_ev")
    st = [P.sem(tag + f"_st{i}") for i in range(4)]

    G_N = ncb_n * 4

    with nc.Block() as blk:
        @blk.sync
        def _(e):
            for c in range(ncb_n):
                for kt in range(kt_n):
                    i = c * kt_n + kt
                    if i >= 2:
                        e.wait_ge(wcv, i - 1)
                    e.dma_start(
                        out=wland[:, (i % 2) * NCB:(i % 2 + 1) * NCB],
                        in_=w_dram[kt * 128:(kt + 1) * 128, c * NCB:(c + 1) * NCB],
                    ).then_inc(wld[i % 2], 16)

        @blk.vector
        def _(e):
            for c in range(ncb_n):
                buf = c % 2
                if c >= 2:
                    e.wait_ge(mmg, (c - 1) * 4)
                for kt in range(kt_n):
                    i = c * kt_n + kt
                    e.wait_ge(wld[i % 2], 16 * (i // 2 + 1))
                    e.tensor_copy(w_sb[:, buf, kt * NCB:(kt + 1) * NCB],
                                  wland[:, (i % 2) * NCB:(i % 2 + 1) * NCB]).then_inc(wcv, 1)

        @blk.tensor
        def _(e):
            for c in range(ncb_n):
                buf = c % 2
                for mc in range(2):
                    for nt in range(2):
                        G = c * 4 + mc * 2 + nt
                        bank = G % 8
                        if G >= 8:
                            e.wait_ge(ev, G - 7)
                        for kt in range(kt_n):
                            if mc == 0 and nt == 0:
                                e.wait_ge(wcv, c * kt_n + kt + 1)
                            mi = e.matmul(
                                psum[bank][:],
                                w_sb[:, buf, kt * NCB + nt * 128: kt * NCB + (nt + 1) * 128],
                                rhs_sb[:, kt * M + mc * MC: kt * M + (mc + 1) * MC],
                                start=(kt == 0), stop=(kt == kt_n - 1))
                        mi.then_inc(mmg, 1)

        @blk.scalar
        def _(e):
            for c in range(ncb_n):
                for mc in range(2):
                    for nt in range(2):
                        G = c * 4 + mc * 2 + nt
                        bank = G % 8
                        e.wait_ge(mmg, G + 1)
                        if G >= 4:
                            e.wait_ge(st[G % 4], 16 * (G // 4))
                        if (2 * c + nt) < qscale_tiles:
                            e.activation(out=out_sb[:, G % 4, :], in_=psum[bank][:],
                                         func=AF.Copy, scale=scale_ap[:]).then_inc(ev, 1)
                        else:
                            e.copy(out_sb[:, G % 4, :], psum[bank][:]).then_inc(ev, 1)

        @blk.gpsimd
        def _(e):
            for c in range(ncb_n):
                for mc in range(2):
                    for nt in range(2):
                        G = c * 4 + mc * 2 + nt
                        e.wait_ge(ev, G + 1)
                        r0 = (2 * c + nt) * 128
                        e.dma_start(out=out_dram[r0:r0 + 128, mc * MC:(mc + 1) * MC],
                                    in_=out_sb[:, G % 4, :]).then_inc(st[G % 4], 16)
            for i in range(4):
                e.wait_ge(st[i], 16 * (G_N // 4 + (1 if i < G_N % 4 else 0)))


def build_ln_prog(post):
    P = Prog("postln" if post else "preln")
    with P.es:
        if post:
            z = P.din("z_in", [D, M], BF16)
            res = P.din("res_in", [D, M], F32)
            xo = P.dout("x_out", [D, M], F32)
            emit_ln(P, z, xo, res_dram=res)
        else:
            x = P.din("x_in", [D, M], F32)
            y = P.dout("y_out", [D, M], BF16)
            emit_ln(P, x, y)
    return P


def build_qkv_prog():
    P = Prog("qkv")
    with P.es:
        y = P.din("y_in", [D, M], BF16)
        w = P.din("w_in", [D, 4352], F32)
        qk = P.dout("qk_out", [4352, M], BF16)
        y_sb = P.sb("y", [128, 32 * M], BF16)
        ld = P.sem("yld")
        with P.nc.Block() as blk:
            @blk.sync
            def _(e):
                load_rows(e, y_sb, y, 32, ld)

            @blk.vector
            def _(e):
                e.wait_ge(ld, 16 * 4)
        cst = make_consts(P, [SCALE_Q], tag="qc")
        emit_proj_f32w(P, w, y_sb, qk, 32, 34, qscale_tiles=32, scale_ap=cst[SCALE_Q])
    return P


def build_wo_prog():
    P = Prog("wo")
    with P.es:
        at = P.din("at_in", [D, M], BF16)
        w = P.din("w_in", [D, D], F32)
        z = P.dout("z_out", [D, M], BF16)
        a_sb = P.sb("a", [128, 32 * M], BF16)
        ld = P.sem("ald")
        with P.nc.Block() as blk:
            @blk.sync
            def _(e):
                load_rows(e, a_sb, at, 32, ld)

            @blk.vector
            def _(e):
                e.wait_ge(ld, 16 * 4)
        emit_proj_f32w(P, w, a_sb, z, 32, 32)
    return P



def emit_attn(P, qk_dram, kp_dram, vp_dram, bias_dram, id_dram, out_dram, tag="at"):
    nc = P.nc
    qk_sb = P.sb(tag + "_qk", [128, 34 * M], BF16)
    kp_sb = P.sb(tag + "_kp", [128, M], BF16)
    vp_sb = P.sb(tag + "_vp", [128, M], BF16)
    id_sb = P.sb(tag + "_id", [128, 128], BF16)
    vt_sb = P.sb(tag + "_vt", [128, 16 * 128], BF16)
    bias_sb = P.sb(tag + "_bt", [128, 2, 2048], F32)
    srow = P.sb(tag + "_sr", [128, 2, 2048], F32)
    probs = P.sb(tag + "_pb", [128, 2, 2048], BF16)
    ptb = P.sb(tag + "_pt", [128, 2, 2048], BF16)
    mA = P.sb(tag + "_mA", [128, 1], F32)
    mB = P.sb(tag + "_mB", [128, 1], F32)
    negm = P.sb(tag + "_nm", [128, 2, 1], F32)
    sA = P.sb(tag + "_sA", [128, 2, 1], F32)
    sB = P.sb(tag + "_sB", [128, 2, 1], F32)
    rsum = P.sb(tag + "_rs", [128, 1], F32)
    out_sb = P.sb(tag + "_ao", [128, 32 * M], BF16)

    ps_sc = P.ps(tag + "_psc", [128, 2048])
    ps_tp = P.ps(tag + "_ptp", [128, 2048], BF16)
    ps_pv = P.ps(tag + "_ppv", [128, 128])

    ld = P.sem(tag + "_ld")
    bl = [P.sem(tag + f"_bl{i}") for i in range(2)]
    s_mm = P.sem(tag + "_smm")
    s_sr = P.sem(tag + "_ssr")
    s_ex = P.sem(tag + "_sex")
    s_no = P.sem(tag + "_sno")
    s_tp = P.sem(tag + "_stp")
    s_pt = P.sem(tag + "_spt")
    s_pv = P.sem(tag + "_spv")
    s_ae = P.sem(tag + "_sae")
    st = P.sem(tag + "_st")

    with nc.Block() as blk:
        @blk.sync
        def _(e):
            n = load_rows(e, qk_sb, qk_dram, 34, ld)
            e.dma_start(out=kp_sb[:], in_=kp_dram[:]).then_inc(ld, 16)
            e.dma_start(out=vp_sb[:], in_=vp_dram[:]).then_inc(ld, 16)
            e.dma_start(out=id_sb[:], in_=id_dram[:]).then_inc(ld, 16)

        @blk.tensor
        def _(e):
            e.wait_ge(ld, 16 * 8)
            mi = None
            for j in range(8):
                mi = e.transpose(ps_tp[:, j * 128:(j + 1) * 128].bitcast(BF16),
                                 qk_sb[:, 33 * M + j * 128: 33 * M + (j + 1) * 128], id_sb[:])
            mi.then_inc(s_tp, 1)
            for j in range(8):
                mi = e.transpose(ps_tp[:, 1024 + j * 128: 1024 + (j + 1) * 128].bitcast(BF16),
                                 vp_sb[:, j * 128:(j + 1) * 128], id_sb[:])
            mi.then_inc(s_tp, 1)

        @blk.scalar
        def _(e):
            e.wait_ge(s_tp, 1)
            e.copy(vt_sb[:, 0:1024], ps_tp[:, 0:1024])
            e.wait_ge(s_tp, 2)
            e.copy(vt_sb[:, 1024:2048], ps_tp[:, 1024:2048]).then_inc(s_tp, 1)

    with nc.Block() as blk:
        @blk.sync
        def _(e):
            for t in range(8):
                if t >= 2:
                    e.wait_ge(s_sr, (t - 1) * 32)
                e.dma_start(out=bias_sb[:, t % 2, :], in_=bias_dram[t]).then_inc(bl[t % 2], 16)

        def emit_transp(e, j):
            tj = j // 32
            e.wait_ge(s_no, j + 1)
            if j > 0:
                e.wait_ge(s_pt, j)
            pb = probs[:, j % 2, :]
            for jj in range(tj + 1):
                e.transpose(ps_tp[:, jj * 128:(jj + 1) * 128],
                            pb[:, jj * 128:(jj + 1) * 128], id_sb[:])
                mi = e.transpose(ps_tp[:, 1024 + jj * 128:1024 + (jj + 1) * 128],
                                 pb[:, 1024 + jj * 128:1024 + (jj + 1) * 128], id_sb[:])
            mi.then_inc(s_tp, 1)

        def emit_pv(e, j):
            tj = j // 32
            e.wait_ge(s_pt, j + 1)
            if j > 0:
                e.wait_ge(s_ae, j)
            pt = ptb[:, j % 2, :]
            for jj in range(tj + 1):
                e.matmul(ps_pv[:], vt_sb[:, jj * 128:(jj + 1) * 128],
                         pt[:, jj * 128:(jj + 1) * 128],
                         start=(jj == 0), stop=False)
                mi = e.matmul(ps_pv[:], vt_sb[:, 1024 + jj * 128:1024 + (jj + 1) * 128],
                              pt[:, 1024 + jj * 128:1024 + (jj + 1) * 128],
                              start=False, stop=(jj == tj))
            mi.then_inc(s_pv, 1)

        @blk.tensor
        def _(e):
            for k in range(258):
                if k <= 255:
                    t, h = divmod(k, 32)
                    w = (t + 1) * 128
                    if k > 0:
                        e.wait_ge(s_sr, k)
                    q_ap = qk_sb[:, h * M + t * 128: h * M + (t + 1) * 128]
                    for c0 in range(0, w, MC):
                        cw = min(MC, w - c0)
                        e.matmul(ps_sc[:, c0:c0 + cw], q_ap,
                                 qk_sb[:, 32 * M + c0: 32 * M + c0 + cw],
                                 start=True, stop=True)
                        mi = e.matmul(ps_sc[:, 1024 + c0:1024 + c0 + cw], q_ap,
                                      kp_sb[:, c0:c0 + cw], start=True, stop=True)
                    mi.then_inc(s_mm, 1)
                if 1 <= k <= 256:
                    emit_transp(e, k - 1)
                if k >= 2:
                    emit_pv(e, k - 2)

        def sm_front(e, it):
            t, h = divmod(it, 32)
            w = (t + 1) * 128
            e.wait_ge(s_mm, it + 1)
            if h == 0:
                e.wait_ge(bl[t % 2], 16 * (t // 2 + 1))
            bt = bias_sb[:, t % 2, :]
            sr = srow[:, it % 2, :]
            sl = float(SLOPES[h])
            e.scalar_tensor_tensor(out=sr[:, 0:w], in0=bt[:, 0:w], scalar=sl,
                                   in1=ps_sc[:, 0:w], op0=ALU.mult, op1=ALU.add)
            e.scalar_tensor_tensor(out=sr[:, 1024:1024 + w], in0=bt[:, 1024:1024 + w],
                                   scalar=sl, in1=ps_sc[:, 1024:1024 + w],
                                   op0=ALU.mult, op1=ALU.add)
            e.drain()
            e.reduce_max(mA[:], sr[:, 0:w], axis=mybir.AxisListType.X)
            e.reduce_max(mB[:], sr[:, 1024:1024 + w], axis=mybir.AxisListType.X)
            e.drain()
            e.tensor_max(mA[:], mA[:], mB[:])
            e.drain()
            e.tensor_scalar_mul(negm[:, it % 2, :], mA[:], -1.0).then_inc(s_sr, 1)

        def sm_back(e, it):
            t, h = divmod(it, 32)
            w = (t + 1) * 128
            e.wait_ge(s_ex, it + 1)
            e.tensor_add(sA[:, it % 2, :], sA[:, it % 2, :], sB[:, it % 2, :])
            e.drain()
            e.reciprocal(rsum[:], sA[:, it % 2, :])
            e.drain()
            pb = probs[:, it % 2, :]
            e.tensor_scalar(out=pb[:, 0:w], in0=pb[:, 0:w],
                            scalar1=rsum[:], scalar2=None, op0=ALU.mult)
            e.tensor_scalar(out=pb[:, 1024:1024 + w], in0=pb[:, 1024:1024 + w],
                            scalar1=rsum[:], scalar2=None, op0=ALU.mult).then_inc(s_no, 1)

        @blk.vector
        def _(e):
            sm_front(e, 0)
            for it in range(1, 256):
                sm_front(e, it)
                sm_back(e, it - 1)
            sm_back(e, 255)

        @blk.scalar
        def _(e):
            for k in range(258):
                if k <= 255:
                    t, h = divmod(k, 32)
                    w = (t + 1) * 128
                    it = k
                    e.wait_ge(s_sr, it + 1)
                    if it >= 2:
                        e.wait_ge(s_tp, 3 + it - 1)
                        e.wait_ge(s_no, it - 1)
                    pb = probs[:, it % 2, :]
                    sr = srow[:, it % 2, :]
                    e.activation(out=pb[:, 0:w], in_=sr[:, 0:w], func=AF.Exp,
                                 bias=negm[:, it % 2, :], accum_out=sA[:, it % 2, :])
                    e.activation(out=pb[:, 1024:1024 + w], in_=sr[:, 1024:1024 + w],
                                 func=AF.Exp, bias=negm[:, it % 2, :],
                                 accum_out=sB[:, it % 2, :]).then_inc(s_ex, 1)
                if 1 <= k <= 256:
                    j = k - 1
                    tj = j // 32
                    wj = (tj + 1) * 128
                    e.wait_ge(s_tp, 3 + j + 1)
                    if j >= 2:
                        e.wait_ge(s_pv, j - 1)
                    pt = ptb[:, j % 2, :]
                    e.copy(pt[:, 0:wj], ps_tp[:, 0:wj])
                    e.copy(pt[:, 1024:1024 + wj], ps_tp[:, 1024:1024 + wj]).then_inc(s_pt, 1)
                if k >= 2:
                    j = k - 2
                    tj, hj = divmod(j, 32)
                    e.wait_ge(s_pv, j + 1)
                    e.copy(out_sb[:, hj * M + tj * 128: hj * M + (tj + 1) * 128],
                           ps_pv[:]).then_inc(s_ae, 1)

    with nc.Block() as blk:
        @blk.gpsimd
        def _(e):
            for h in range(32):
                e.dma_start(out=out_dram[h * 128:(h + 1) * 128, :],
                            in_=out_sb[:, h * M:(h + 1) * M]).then_inc(st, 16)
            e.wait_ge(st, 16 * 32)


def build_attn_prog():
    P = Prog("attn")
    with P.es:
        qk = P.din("qk_in", [4352, M], BF16)
        kp = P.din("kp_in", [128, M], BF16)
        vp = P.din("vp_in", [128, M], BF16)
        bias = P.din("bias_tab", [8, 128, 2048], F32)
        ident = P.din("ident", [128, 128], BF16)
        ao = P.dout("at_out", [D, M], BF16)
        emit_attn(P, qk, kp, vp, bias, ident, ao)
    return P



def emit_ffn_prep(P, sc_dram, qz_dram, cm_dram, nzs_dram, groups, nchunks, words, tag):
    nc = P.nc
    cols = nchunks * 256
    with contextlib.ExitStack() as tes:
        s_sb = tes.enter_context(nc.sbuf_tensor(tag + "_s", [groups, cols], F32))
        sb_bf = tes.enter_context(nc.sbuf_tensor(tag + "_sb", [groups, cols], BF16))
        qz_sb = tes.enter_context(nc.sbuf_tensor(tag + "_qz", [groups, words], I32))
        hi32 = tes.enter_context(nc.sbuf_tensor(tag + "_hi", [groups, words], I32))
        nzs = tes.enter_context(nc.sbuf_tensor(tag + "_z", [groups, cols], BF16))
        ld = P.sem(tag + "_ld")
        dn = P.sem(tag + "_dn")
        with nc.Block() as blk:
            @blk.sync
            def _(e):
                e.dma_start(out=s_sb[:], in_=sc_dram[:]).then_inc(ld, 16)
                e.dma_start(out=qz_sb[:], in_=qz_dram[:]).then_inc(ld, 16)

            @blk.vector
            def _(e):
                e.wait_ge(ld, 32)
                e.tensor_copy(sb_bf[:], s_sb[:]).then_inc(dn, 1)
                e.drain()
                e.tensor_scalar(out=hi32[:], in0=qz_sb[:], scalar1=4, scalar2=MASK32,
                                op0=ALU.logical_shift_right, op1=ALU.bitwise_and)
                e.tensor_single_scalar(out=qz_sb[:], in_=qz_sb[:], scalar=MASK32,
                                       op=ALU.bitwise_and)
                e.drain()
                e.tensor_tensor(out=nzs[:].rearrange("g (w two) -> g w two", two=2)[:, :, 0],
                                in0=qz_sb[:].bitcast(U8),
                                in1=sb_bf[:].rearrange("g (w two) -> g w two", two=2)[:, :, 0],
                                op=ALU.mult)
                e.tensor_tensor(out=nzs[:].rearrange("g (w two) -> g w two", two=2)[:, :, 1],
                                in0=hi32[:].bitcast(U8),
                                in1=sb_bf[:].rearrange("g (w two) -> g w two", two=2)[:, :, 1],
                                op=ALU.mult)
                e.drain()
                e.tensor_scalar_mul(nzs[:], nzs[:], -1.0).then_inc(dn, 1)

            @blk.gpsimd
            def _(e):
                e.wait_ge(dn, 1)
                e.dma_start(
                    out=cm_dram.rearrange("nc g w -> g nc w"),
                    in_=sb_bf[:].rearrange("g (nc w) -> g nc w", w=256),
                ).then_inc(dn, 16)
                e.wait_ge(dn, 18)
                e.dma_start(out=nzs_dram[:], in_=nzs[:]).then_inc(dn, 16)
                e.wait_ge(dn, 34)


def emit_gateup(P, y_sb, pkg_dram, pku_dram, sgcm_dram, sucm_dram, nzg_dram, nzu_dram,
                xsum_sb, h_dram, hsum_dram, tag="gu"):
    nc = P.nc
    NCH = 43
    kt_n = 32

    pk = {m: P.sb(tag + f"_pk{m}", [128, 2, 1024], I32) for m in "gu"}
    hi = {m: P.sb(tag + f"_hi{m}", [128, 1024], I32) for m in "gu"}
    nz = {m: P.sb(tag + f"_nz{m}", [32, 2, 256], BF16) for m in "gu"}
    srep = {m: P.sb(tag + f"_sr{m}", [128, 8192], BF16) for m in "gu"}
    qs = {m: P.sb(tag + f"_qs{m}", [128, 8192], BF16) for m in "gu"}
    hg = P.sb(tag + "_hg", [128, 2, MC], F32)
    hu = P.sb(tag + "_hu", [128, 2, MC], BF16)
    htile = P.sb(tag + "_ht", [128, 4, MC], BF16)
    stair = P.sb(tag + "_stair", [128, 171], BF16)
    hsum_sb = P.sb(tag + "_hs", [86, M], BF16)

    ps_g = [P.ps(tag + f"_pg{i}", [128, MC]) for i in range(4)]
    ps_u = [P.ps(tag + f"_pu{i}", [128, MC]) for i in range(2)]
    ps_hs = P.ps(tag + "_phs", [86, M])

    s_pk = [P.sem(tag + f"_spk{i}") for i in range(2)]
    s_srg = P.sem(tag + "_ssrg")
    s_sru = P.sem(tag + "_ssru")
    s_dqg = P.sem(tag + "_sdqg")
    s_dqu = P.sem(tag + "_sdqu")
    s_peg = P.sem(tag + "_speg")
    s_peu = P.sem(tag + "_speu")
    s_sig = P.sem(tag + "_ssig")
    s_cu = P.sem(tag + "_scu")
    s_ht = P.sem(tag + "_sht")
    s_hsm = P.sem(tag + "_shsm")
    s_hst = [P.sem(tag + f"_shst{i}") for i in range(4)]
    s_fin = P.sem(tag + "_sfin")

    with nc.Block() as blk:
        @blk.vector
        def _(e):
            e.memset(stair[:], 0.0)
            e.drain()
            e.memset(stair[:, 85:86], 1.0)
            e.drain()
            for c in range(NCH):
                e.wait_ge(s_srg, 16 * (c + 1))
                e.wait_ge(s_pk[c % 2], 64 * (c // 2 + 1))
                if c >= 1:
                    e.wait_ge(s_peg, 4 * c)
                for m, s_sr_sem, s_dq_sem, s_pe_sem in (
                        ("g", s_srg, s_dqg, s_peg), ("u", s_sru, s_dqu, s_peu)):
                    if m == "u":
                        e.wait_ge(s_sru, 16 * (c + 1))
                        if c >= 1:
                            e.wait_ge(s_peu, 4 * c)
                    e.tensor_scalar(out=hi[m][:], in0=pk[m][:, c % 2, :], scalar1=4,
                                    scalar2=MASK32, op0=ALU.logical_shift_right,
                                    op1=ALU.bitwise_and)
                    e.drain()
                    e.tensor_single_scalar(out=pk[m][:, c % 2, :], in_=pk[m][:, c % 2, :],
                                           scalar=MASK32, op=ALU.bitwise_and)
                    e.drain()
                    qv = qs[m][:].rearrange("p (kt i two) -> p kt i two", kt=32, two=2)
                    sv = srep[m][:].rearrange("p (kt i two) -> p kt i two", kt=32, two=2)
                    e.tensor_tensor(out=qv[:, :, :, 0],
                                    in0=pk[m][:, c % 2, :].bitcast(U8).rearrange(
                                        "p (kt i) -> p kt i", kt=32),
                                    in1=sv[:, :, :, 0], op=ALU.mult)
                    e.tensor_tensor(out=qv[:, :, :, 1],
                                    in0=hi[m][:].bitcast(U8).rearrange(
                                        "p (kt i) -> p kt i", kt=32),
                                    in1=sv[:, :, :, 1], op=ALU.mult
                                    ).then_inc(s_dqg if m == "g" else s_dqu, 1)

        @blk.sync
        def _(e):
            for c in range(NCH):
                if c >= 2:
                    e.wait_ge(s_dqu, c - 1)
                    e.wait_ge(s_peu, 4 * (c - 1))
                e.dma_start(out=pk["g"][:, c % 2, :],
                            in_=pkg_dram[c].rearrange("p kt w -> p (kt w)")
                            ).then_inc(s_pk[c % 2], 16)
                e.dma_start(out=pk["u"][:, c % 2, :],
                            in_=pku_dram[c].rearrange("p kt w -> p (kt w)")
                            ).then_inc(s_pk[c % 2], 16)
                e.dma_start(out=nz["g"][:, c % 2, :], in_=nzg_dram[:, c * 256:(c + 1) * 256]
                            ).then_inc(s_pk[c % 2], 16)
                e.dma_start(out=nz["u"][:, c % 2, :], in_=nzu_dram[:, c * 256:(c + 1) * 256]
                            ).then_inc(s_pk[c % 2], 16)
                if c >= 1:
                    e.wait_ge(s_dqg, c)
                e.dma_start(out=srep["g"][:],
                            in_=sgcm_dram[c].rearrange("g w -> (g w)")
                            .unsqueeze(0).partition_broadcast(128).squeeze(1)
                            ).then_inc(s_srg, 16)
                if c >= 1:
                    e.wait_ge(s_dqu, c)
                e.dma_start(out=srep["u"][:],
                            in_=sucm_dram[c].rearrange("g w -> (g w)")
                            .unsqueeze(0).partition_broadcast(128).squeeze(1)
                            ).then_inc(s_sru, 16)

        @blk.scalar
        def _(e):
            for c in range(NCH):
                for k in range(4):
                    G = c * 4 + k
                    e.wait_ge(s_peg, G + 1)
                    if G >= 2:
                        e.wait_ge(s_ht, G - 1)
                    e.activation(out=hg[:, k % 2, :], in_=ps_g[k][:],
                                 func=AF.Silu).then_inc(s_sig, 1)
                    e.wait_ge(s_peu, G + 1)
                    e.copy(hu[:, k % 2, :], ps_u[k % 2][:]).then_inc(s_cu, 1)

        @blk.tensor
        def _(e):
            for c in range(NCH):
                e.wait_ge(s_dqg, c + 1)
                for k in range(4):
                    mc, nt = divmod(k, 2)
                    G = c * 4 + k
                    if c >= 1:
                        e.wait_ge(s_sig, 4 * (c - 1) + k + 1)
                    for kt in range(kt_n):
                        e.matmul(ps_g[k][:],
                                 qs["g"][:, kt * 256 + nt * 128: kt * 256 + (nt + 1) * 128],
                                 y_sb[:, kt * M + mc * MC: kt * M + (mc + 1) * MC],
                                 start=(kt == 0), stop=False)
                    e.matmul(ps_g[k][:], nz["g"][:, c % 2, nt * 128:(nt + 1) * 128],
                             xsum_sb[:, mc * MC:(mc + 1) * MC], start=False, stop=True
                             ).then_inc(s_peg, 1)
                e.wait_ge(s_dqu, c + 1)
                for k in range(4):
                    mc, nt = divmod(k, 2)
                    G = c * 4 + k
                    if G >= 2:
                        e.wait_ge(s_cu, G - 1)
                    for kt in range(kt_n):
                        e.matmul(ps_u[k % 2][:],
                                 qs["u"][:, kt * 256 + nt * 128: kt * 256 + (nt + 1) * 128],
                                 y_sb[:, kt * M + mc * MC: kt * M + (mc + 1) * MC],
                                 start=(kt == 0), stop=False)
                    e.matmul(ps_u[k % 2][:], nz["u"][:, c % 2, nt * 128:(nt + 1) * 128],
                             xsum_sb[:, mc * MC:(mc + 1) * MC], start=False, stop=True
                             ).then_inc(s_peu, 1)
                e.wait_ge(s_ht, 4 * (c + 1))
                for k in range(4):
                    mc, nt = divmod(k, 2)
                    e.matmul(ps_hs[:, mc * MC:(mc + 1) * MC],
                             stair[:, 85 - (2 * c + nt): 171 - (2 * c + nt)][:, 0:86],
                             htile[:, k, :],
                             start=(c == 0 and nt == 0), stop=(c == NCH - 1 and nt == 1))
                e.nop().then_inc(s_hsm, 1)

        @blk.gpsimd
        def _(e):
            for c in range(NCH):
                for k in range(4):
                    mc, nt = divmod(k, 2)
                    G = c * 4 + k
                    e.wait_ge(s_sig, G + 1)
                    e.wait_ge(s_cu, G + 1)
                    if c >= 1:
                        e.wait_ge(s_hsm, c)
                        e.wait_ge(s_hst[k], 16 * c)
                    e.tensor_tensor(out=htile[:, k, :], in0=hg[:, k % 2, :],
                                    in1=hu[:, k % 2, :], op=ALU.mult).then_inc(s_ht, 1)
                    e.wait_ge(s_ht, c * 4 + k + 1)
                    e.dma_start(out=h_dram[(2 * c + nt) * 128:(2 * c + nt + 1) * 128,
                                           mc * MC:(mc + 1) * MC],
                                in_=htile[:, k, :]).then_inc(s_hst[k], 16)
            for k in range(4):
                e.wait_ge(s_hst[k], 16 * NCH)

    with nc.Block() as blk:
        @blk.scalar
        def _(e):
            e.copy(hsum_sb[:], ps_hs[:]).then_inc(s_fin, 1)

        @blk.gpsimd
        def _(e):
            e.wait_ge(s_fin, 1)
            e.dma_start(out=hsum_dram[:], in_=hsum_sb[:]).then_inc(s_fin, 16)
            e.wait_ge(s_fin, 17)


def build_gateup_prog():
    P = Prog("gateup")
    with P.es:
        y = P.din("y_in", [D, M], BF16)
        pkg = P.din("pk_g", [43, 128, 32, 32], I32)
        pku = P.din("pk_u", [43, 128, 32, 32], I32)
        scg = P.din("sc_g", [32, II], F32)
        scu = P.din("sc_u", [32, II], F32)
        qzg = P.din("qz_g", [32, 1376], I32)
        qzu = P.din("qz_u", [32, 1376], I32)
        h = P.dout("h_out", [II, M], BF16)
        hsum = P.dout("hsum_out", [86, M], BF16)
        sgcm = P.dint("sg_cm", [43, 32, 256], BF16)
        sucm = P.dint("su_cm", [43, 32, 256], BF16)
        nzg = P.dint("nzg", [32, II], BF16)
        nzu = P.dint("nzu", [32, II], BF16)

        y_sb = P.sb("y", [128, 32 * M], BF16)
        xsum_sb = P.sb("xs", [32, M], BF16)
        stair32 = P.sb("st32", [128, 63], BF16)
        xs_es = contextlib.ExitStack()
        ps_xs = xs_es.enter_context(P.nc.psum_tensor("pxs", [32, M], F32))
        ld = P.sem("yld")
        xs = P.sem("xsem")

        with P.nc.Block() as blk:
            @blk.sync
            def _(e):
                load_rows(e, y_sb, y, 32, ld)

            @blk.vector
            def _(e):
                e.memset(stair32[:], 0.0)
                e.drain()
                e.memset(stair32[:, 31:32], 1.0).then_inc(xs, 1)

            @blk.tensor
            def _(e):
                e.wait_ge(xs, 1)
                e.wait_ge(ld, 16 * 4)
                for kt in range(32):
                    for mc in range(2):
                        mi = e.matmul(ps_xs[:, mc * MC:(mc + 1) * MC],
                                      stair32[:, 31 - kt: 63 - kt][:, 0:32],
                                      y_sb[:, kt * M + mc * MC: kt * M + (mc + 1) * MC],
                                      start=(kt == 0), stop=(kt == 31))
                mi.then_inc(xs, 1)

            @blk.scalar
            def _(e):
                e.wait_ge(xs, 2)
                e.copy(xsum_sb[:], ps_xs[:]).then_inc(xs, 1)
        xs_es.close()

        emit_ffn_prep(P, scg, qzg, sgcm, nzg, 32, 43, 1376, "spg")
        emit_ffn_prep(P, scu, qzu, sucm, nzu, 32, 43, 1376, "spu")
        emit_gateup(P, y_sb, pkg, pku, sgcm, sucm, nzg, nzu, xsum_sb, h, hsum)
    return P



def emit_down(P, h_dram, hsum_sb, pkd_dram, sdcm_dram, nzd_dram, z_dram, tag="dn"):
    nc = P.nc
    NDC = 16
    kt_n = 86
    KH = 43

    hh_sb = P.sb(tag + "_hh", [128, kt_n * MC], BF16)
    pk_sb = P.sb(tag + "_pk", [128, kt_n * 32], I32)
    hi = P.sb(tag + "_hi", [128, KH * 32], I32)
    srep = P.sb(tag + "_sr", [128, kt_n * 256], BF16)
    qs = P.sb(tag + "_qs", [128, kt_n * 256], BF16)
    nzd = P.sb(tag + "_nz", [86, 256], BF16)
    out_sb = P.sb(tag + "_o", [128, 4, MC], BF16)

    psum = [P.ps(tag + f"_p{i}", [128, MC]) for i in range(4)]

    s_hl = P.sem(tag + "_shl")
    s_pkh = [P.sem(tag + f"_spk{i}") for i in range(2)]
    s_nz = P.sem(tag + "_snz")
    s_srh = [P.sem(tag + f"_ssr{i}") for i in range(2)]
    s_dq = P.sem(tag + "_sdq")
    s_ph = P.sem(tag + "_sph")
    s_pe = P.sem(tag + "_spe")
    s_ev = P.sem(tag + "_sev")
    s_st = [P.sem(tag + f"_sst{i}") for i in range(4)]

    def deq_half(e, i, h):
        e.wait_ge(s_srh[h], 16 * (i + 1))
        e.wait_ge(s_pkh[h], 16 * (i + 1))
        if h == 1:
            e.wait_ge(s_nz, 16 * (i + 1))
        if i >= 1:
            if h == 0:
                e.wait_ge(s_ph, i)
            else:
                e.wait_ge(s_pe, 2 * i)
        pkh = pk_sb[:, h * KH * 32:(h + 1) * KH * 32]
        e.tensor_scalar(out=hi[:], in0=pkh, scalar1=4, scalar2=MASK32,
                        op0=ALU.logical_shift_right, op1=ALU.bitwise_and)
        e.drain()
        e.tensor_single_scalar(out=pkh, in_=pkh, scalar=MASK32, op=ALU.bitwise_and)
        e.drain()
        qv = qs[:, h * KH * 256:(h + 1) * KH * 256].rearrange(
            "p (kt i two) -> p kt i two", kt=KH, two=2)
        sv = srep[:, h * KH * 256:(h + 1) * KH * 256].rearrange(
            "p (kt i two) -> p kt i two", kt=KH, two=2)
        e.tensor_tensor(out=qv[:, :, :, 0],
                        in0=pkh.bitcast(U8).rearrange("p (kt i) -> p kt i", kt=KH),
                        in1=sv[:, :, :, 0], op=ALU.mult)
        e.tensor_tensor(out=qv[:, :, :, 1],
                        in0=hi[:].bitcast(U8).rearrange("p (kt i) -> p kt i", kt=KH),
                        in1=sv[:, :, :, 1], op=ALU.mult).then_inc(s_dq, 1)

    with nc.Block() as blk:
        @blk.sync
        def _(e):
            for mc in range(2):
                if mc == 1:
                    e.wait_ge(s_pe, NDC * 2)
                for k0 in range(0, kt_n, 8):
                    k1 = min(kt_n, k0 + 8)
                    e.dma_start(
                        out=hh_sb[:, k0 * MC:k1 * MC],
                        in_=h_dram.rearrange("(kt p) m -> p kt m", p=128)[
                            :, k0:k1, mc * MC:(mc + 1) * MC],
                    ).then_inc(s_hl, 16)
                for dc in range(NDC):
                    i = mc * NDC + dc
                    if i >= 1:
                        e.wait_ge(s_dq, 2 * i)
                    for h in range(2):
                        e.dma_start(out=pk_sb[:, h * KH * 32:(h + 1) * KH * 32],
                                    in_=pkd_dram[dc][:, h * KH:(h + 1) * KH, :]
                                    .rearrange("p kt w -> p (kt w)")).then_inc(s_pkh[h], 16)
                    for h in range(2):
                        e.dma_start(
                            out=srep[:, h * KH * 256:(h + 1) * KH * 256],
                            in_=sdcm_dram[dc][h * KH:(h + 1) * KH, :]
                            .rearrange("g w -> (g w)")
                            .unsqueeze(0).partition_broadcast(128).squeeze(1)
                        ).then_inc(s_srh[h], 16)
                    if i >= 1:
                        e.wait_ge(s_pe, 2 * i)
                    e.dma_start(out=nzd[:], in_=nzd_dram[:, dc * 256:(dc + 1) * 256]
                                ).then_inc(s_nz, 16)

        @blk.scalar
        def _(e):
            for mc in range(2):
                for dc in range(NDC):
                    i = mc * NDC + dc
                    for nt in range(2):
                        G = i * 2 + nt
                        e.wait_ge(s_pe, G + 1)
                        if G >= 4:
                            e.wait_ge(s_st[G % 4], 16 * (G // 4))
                        e.copy(out_sb[:, G % 4, :], psum[G % 4][:]).then_inc(s_ev, 1)

        @blk.vector
        def _(e):
            for mc in range(2):
                for dc in range(NDC):
                    i = mc * NDC + dc
                    deq_half(e, i, 0)
                    deq_half(e, i, 1)

        @blk.tensor
        def _(e):
            for mc in range(2):
                for dc in range(NDC):
                    i = mc * NDC + dc
                    e.wait_ge(s_dq, 2 * i + 1)
                    if mc == 0 and dc == 0:
                        e.wait_ge(s_hl, 16 * 11)
                    if mc == 1 and dc == 0:
                        e.wait_ge(s_hl, 16 * 22)
                    for nt in range(2):
                        G = i * 2 + nt
                        if G >= 4:
                            e.wait_ge(s_ev, G - 3)
                        for kt in range(KH):
                            mi = e.matmul(psum[G % 4][:],
                                          qs[:, kt * 256 + nt * 128: kt * 256 + (nt + 1) * 128],
                                          hh_sb[:, kt * MC:(kt + 1) * MC],
                                          start=(kt == 0), stop=False)
                        if nt == 1:
                            mi.then_inc(s_ph, 1)
                    e.wait_ge(s_dq, 2 * i + 2)
                    for nt in range(2):
                        G = i * 2 + nt
                        for kt in range(KH, kt_n):
                            e.matmul(psum[G % 4][:],
                                     qs[:, kt * 256 + nt * 128: kt * 256 + (nt + 1) * 128],
                                     hh_sb[:, kt * MC:(kt + 1) * MC],
                                     start=False, stop=False)
                        e.matmul(psum[G % 4][:],
                                 nzd[:, nt * 128:(nt + 1) * 128],
                                 hsum_sb[:, mc * MC:(mc + 1) * MC], start=False, stop=True
                                 ).then_inc(s_pe, 1)

        @blk.gpsimd
        def _(e):
            for mc in range(2):
                for dc in range(NDC):
                    for nt in range(2):
                        G = (mc * NDC + dc) * 2 + nt
                        e.wait_ge(s_ev, G + 1)
                        r0 = (2 * dc + nt) * 128
                        e.dma_start(out=z_dram[r0:r0 + 128, mc * MC:(mc + 1) * MC],
                                    in_=out_sb[:, G % 4, :]).then_inc(s_st[G % 4], 16)
            for i in range(4):
                e.wait_ge(s_st[i], 16 * NDC)


def build_down_prog():
    P = Prog("down")
    with P.es:
        h = P.din("h_in", [II, M], BF16)
        hsum = P.din("hsum_in", [86, M], BF16)
        pkd = P.din("pk_d", [16, 128, 86, 32], I32)
        scd = P.din("sc_d", [86, D], F32)
        qzd = P.din("qz_d", [86, 512], I32)
        z = P.dout("z_out", [D, M], BF16)
        sdcm = P.dint("sd_cm", [16, 86, 256], BF16)
        nzd = P.dint("nzd", [86, D], BF16)

        hsum_sb = P.sb("hs", [86, M], BF16)
        ld = P.sem("hld")
        with P.nc.Block() as blk:
            @blk.sync
            def _(e):
                e.dma_start(out=hsum_sb[:], in_=hsum[:]).then_inc(ld, 16)

            @blk.vector
            def _(e):
                e.wait_ge(ld, 16)
        emit_ffn_prep(P, scd, qzd, sdcm, nzd, 86, 16, 512, "spd")
        emit_down(P, h, hsum_sb, pkd, sdcm, nzd, z)
    return P



_PROGS = {}


def _prog(name, builder):
    if name not in _PROGS:
        _PROGS[name] = builder()
    return _PROGS[name]


def _run(P, per_core_inputs):
    res = run_bass_kernel_spmd(P.nc, per_core_inputs, core_ids=list(range(NCORES)))
    return res.results


def kernel_multilaunch(**inputs):
    sh, pc = prep_inputs(inputs)
    x = [pc[c]["x_in"] for c in range(NCORES)]

    p_ln = _prog("preln", lambda: build_ln_prog(False))
    p_post = _prog("postln", lambda: build_ln_prog(True))
    p_qkv = _prog("qkv", build_qkv_prog)
    p_attn = _prog("attn", build_attn_prog)
    p_wo = _prog("wo", build_wo_prog)
    p_gu = _prog("gateup", build_gateup_prog)
    p_dn = _prog("down", build_down_prog)

    for L in range(2):
        pre = f"L{L}_"
        r = _run(p_ln, [{"x_in": x[c]} for c in range(NCORES)])
        y = [r[c]["y_out"] for c in range(NCORES)]
        r = _run(p_qkv, [{"y_in": y[c], "w_in": sh[pre + "wqkv"]} for c in range(NCORES)])
        qk = [r[c]["qk_out"] for c in range(NCORES)]
        ins = []
        for c in range(NCORES):
            part = c ^ 1
            ins.append({
                "qk_in": qk[c],
                "kp_in": np.ascontiguousarray(qk[part][32 * 128:33 * 128, :]),
                "vp_in": np.ascontiguousarray(qk[part][33 * 128:34 * 128, :]),
                "bias_tab": pc[c]["bias_tab"],
                "ident": pc[c]["ident"],
            })
        r = _run(p_attn, ins)
        at = [r[c]["at_out"] for c in range(NCORES)]
        r = _run(p_wo, [{"at_in": at[c], "w_in": sh[pre + "wo"]} for c in range(NCORES)])
        z = [r[c]["z_out"] for c in range(NCORES)]
        r = _run(p_post, [{"z_in": z[c], "res_in": x[c]} for c in range(NCORES)])
        x = [r[c]["x_out"] for c in range(NCORES)]
        r = _run(p_ln, [{"x_in": x[c]} for c in range(NCORES)])
        y2 = [r[c]["y_out"] for c in range(NCORES)]
        r = _run(p_gu, [{"y_in": y2[c], "pk_g": sh[pre + "pk_g"], "pk_u": sh[pre + "pk_u"],
                         "sc_g": sh[pre + "sc_g"], "sc_u": sh[pre + "sc_u"],
                         "qz_g": sh[pre + "qz_g"], "qz_u": sh[pre + "qz_u"]}
                        for c in range(NCORES)])
        h = [r[c]["h_out"] for c in range(NCORES)]
        hs = [r[c]["hsum_out"] for c in range(NCORES)]
        r = _run(p_dn, [{"h_in": h[c], "hsum_in": hs[c], "pk_d": sh[pre + "pk_d"],
                         "sc_d": sh[pre + "sc_d"], "qz_d": sh[pre + "qz_d"]}
                        for c in range(NCORES)])
        z2 = [r[c]["z_out"] for c in range(NCORES)]
        r = _run(p_post, [{"z_in": z2[c], "res_in": x[c]} for c in range(NCORES)])
        x = [r[c]["x_out"] for c in range(NCORES)]

    return unprep_output(x)



MAGIC0 = 0x5EED0001
MAGIC1 = 0x5EED0002


def stage_preln(P, src_dram, dst_dram, tag):
    with P.stage():
        emit_ln(P, src_dram, dst_dram, tag=tag)


def stage_postln(P, z_dram, res_dram, dst_dram, tag):
    with P.stage():
        emit_ln(P, z_dram, dst_dram, res_dram=res_dram, tag=tag)


def stage_proj(P, rhs_dram, w_dram, out_dram, nt_n, qscale_tiles, tag):
    with P.stage():
        rhs_sb = P.sb(tag + "_rhs", [128, 32 * M], BF16)
        ld = P.sem(tag + "_rld")
        with P.nc.Block() as blk:
            @blk.sync
            def _(e):
                load_rows(e, rhs_sb, rhs_dram, 32, ld)

            @blk.vector
            def _(e):
                e.wait_ge(ld, 16 * 4)
        sc_ap = None
        if qscale_tiles:
            cst = make_consts(P, [SCALE_Q], tag=tag + "_c")
            sc_ap = cst[SCALE_Q]
        emit_proj_f32w(P, w_dram, rhs_sb, out_dram, 32, nt_n,
                       qscale_tiles=qscale_tiles, tag=tag, scale_ap=sc_ap)


def stage_exchange(P, qk_dram, kp_dram, vp_dram, shkv, shfl, magic, tag):
    nc = P.nc
    with P.stage():
        kv_sb = P.sb(tag + "_kv", [128, 2048], BF16)
        fl_sb = P.sb(tag + "_fl", [1, 8], I32)
        fr_sb = P.sb(tag + "_fr", [1, 8], I32)
        ld = P.sem(tag + "_ld")
        wr = P.sem(tag + "_wr")
        ps_ = P.sem(tag + "_ps")
        with nc.Block() as blk:
            @blk.sync
            def _(e):
                e.dma_start(out=kv_sb[:].rearrange("p (kt m) -> p kt m", kt=2),
                            in_=qk_dram[32 * 128:34 * 128, :]
                            .rearrange("(kt p) m -> p kt m", p=128)).then_inc(ld, 16)

            @blk.vector
            def _(e):
                e.memset(fl_sb[:], magic)
                e.wait_ge(ld, 16)
                e.tensor_copy(kv_sb[0:1, 0:1], kv_sb[0:1, 0:1]).then_inc(ld, 1)

            @blk.scalar
            def _(e):
                e.wait_ge(ld, 17)
                for arm in nc.switch_core_id(engines=[e], n=128):
                    s = int(arm) % 2
                    e.dma_start(out=shkv[s], in_=kv_sb[:]).then_inc(wr, 16)
                    e.wait_ge(wr, 16)
                    e.dma_start(out=shfl[s], in_=fl_sb[:]).then_inc(wr, 16)
                    e.wait_ge(wr, 32)
                    with e.register(f"{tag}acc{int(arm)}") as racc, \
                         e.register(f"{tag}fl{int(arm)}") as rflag, \
                         e.register(f"{tag}cd{int(arm)}") as rcond:
                        e.reg_mov(racc, 0)

                        def cond(e=e, racc=racc, rflag=rflag, rcond=rcond, s=s):
                            e.reg_alu(racc, racc, 16, mybir.AluOpType.add)
                            e.dma_start(out=fr_sb[:], in_=shfl[1 - s]).then_inc(ps_, 16)
                            e.wait_ge(ps_, racc)
                            e.load(rflag, fr_sb[0:1, 0:1])
                            e.reg_alu(rcond, rflag, magic, mybir.AluOpType.not_equal)
                            return rcond
                        with e.While(cond):
                            e.nop()
                    e.dma_start(out=kp_dram[:],
                                in_=shkv[1 - s][:, 0:1024]).then_inc(wr, 16)
                    e.dma_start(out=vp_dram[:],
                                in_=shkv[1 - s][:, 1024:2048]).then_inc(wr, 16)
                    e.wait_ge(wr, 64)


def stage_attn(P, qk_dram, kp_dram, vp_dram, bias_dram, id_dram, out_dram, tag):
    with P.stage():
        emit_attn(P, qk_dram, kp_dram, vp_dram, bias_dram, id_dram, out_dram, tag=tag)


def stage_gateup(P, y_dram, pkg, pku, scg, scu, qzg, qzu, h_dram, hsum_dram,
                 sgcm, sucm, nzg, nzu, tag):
    with P.stage():
        y_sb = P.sb(tag + "_y", [128, 32 * M], BF16)
        xsum_sb = P.sb(tag + "_xs", [32, M], BF16)
        stair32 = P.sb(tag + "_s32", [128, 63], BF16)
        xs_es = contextlib.ExitStack()
        ps_xs = xs_es.enter_context(P.nc.psum_tensor(tag + "_pxs", [32, M], F32))
        ld = P.sem(tag + "_yld")
        xs = P.sem(tag + "_xsem")
        with P.nc.Block() as blk:
            @blk.sync
            def _(e):
                load_rows(e, y_sb, y_dram, 32, ld)

            @blk.vector
            def _(e):
                e.memset(stair32[:], 0.0)
                e.drain()
                e.memset(stair32[:, 31:32], 1.0).then_inc(xs, 1)

            @blk.tensor
            def _(e):
                e.wait_ge(xs, 1)
                e.wait_ge(ld, 16 * 4)
                for kt in range(32):
                    for mc in range(2):
                        mi = e.matmul(ps_xs[:, mc * MC:(mc + 1) * MC],
                                      stair32[:, 31 - kt: 63 - kt][:, 0:32],
                                      y_sb[:, kt * M + mc * MC: kt * M + (mc + 1) * MC],
                                      start=(kt == 0), stop=(kt == 31))
                mi.then_inc(xs, 1)

            @blk.scalar
            def _(e):
                e.wait_ge(xs, 2)
                e.copy(xsum_sb[:], ps_xs[:]).then_inc(xs, 1)
        xs_es.close()

        emit_ffn_prep(P, scg, qzg, sgcm, nzg, 32, 43, 1376, tag + "g")
        emit_ffn_prep(P, scu, qzu, sucm, nzu, 32, 43, 1376, tag + "u")
        emit_gateup(P, y_sb, pkg, pku, sgcm, sucm, nzg, nzu, xsum_sb, h_dram,
                    hsum_dram, tag=tag)


def stage_down(P, h_dram, hsum_dram, pkd, scd, qzd, z_dram, sdcm, nzd, tag):
    with P.stage():
        hsum_sb = P.sb(tag + "_hs", [86, M], BF16)
        ld = P.sem(tag + "_hld")
        with P.nc.Block() as blk:
            @blk.sync
            def _(e):
                e.dma_start(out=hsum_sb[:], in_=hsum_dram[:]).then_inc(ld, 16)

            @blk.vector
            def _(e):
                e.wait_ge(ld, 16)
        emit_ffn_prep(P, scd, qzd, sdcm, nzd, 86, 16, 512, tag + "d")
        emit_down(P, h_dram, hsum_sb, pkd, sdcm, nzd, z_dram, tag=tag)


def build_mega():
    P = Prog("mega")
    with P.es:
        nc = P.nc
        x_in = P.din("x_in", [D, M], F32)
        bias = P.din("bias_tab", [8, 128, 2048], F32)
        ident = P.din("ident", [128, 128], BF16)
        x_out = P.dout("x_out", [D, M], F32)

        w = {}
        for L in range(2):
            pre = f"L{L}_"
            w[pre + "wqkv"] = P.din(pre + "wqkv", [D, 4352], F32)
            w[pre + "wo"] = P.din(pre + "wo", [D, D], F32)
            w[pre + "pk_g"] = P.din(pre + "pk_g", [43, 128, 32, 32], I32)
            w[pre + "pk_u"] = P.din(pre + "pk_u", [43, 128, 32, 32], I32)
            w[pre + "pk_d"] = P.din(pre + "pk_d", [16, 128, 86, 32], I32)
            w[pre + "sc_g"] = P.din(pre + "sc_g", [32, II], F32)
            w[pre + "sc_u"] = P.din(pre + "sc_u", [32, II], F32)
            w[pre + "sc_d"] = P.din(pre + "sc_d", [86, D], F32)
            w[pre + "qz_g"] = P.din(pre + "qz_g", [32, 1376], I32)
            w[pre + "qz_u"] = P.din(pre + "qz_u", [32, 1376], I32)
            w[pre + "qz_d"] = P.din(pre + "qz_d", [86, 512], I32)

        x_st = P.dint("x_st", [D, M], F32)
        y_d = P.dint("y_d", [D, M], BF16)
        qk_d = P.dint("qk_d", [4352, M], BF16)
        kp_d = P.dint("kp_d", [128, M], BF16)
        vp_d = P.dint("vp_d", [128, M], BF16)
        at_d = P.dint("at_d", [D, M], BF16)
        z_d = P.dint("z_d", [D, M], BF16)
        h_d = P.dint("h_d", [II, M], BF16)
        hs_d = P.dint("hs_d", [86, M], BF16)
        sgcm = P.dint("sgcm", [43, 32, 256], BF16)
        sucm = P.dint("sucm", [43, 32, 256], BF16)
        sdcm = P.dint("sdcm", [16, 86, 256], BF16)
        nzg = P.dint("nzg", [32, II], BF16)
        nzu = P.dint("nzu", [32, II], BF16)
        nzd = P.dint("nzd", [86, D], BF16)
        shkv = [[P.dint(f"shkv{L}_{s}", [128, 2048], BF16, shared=True)
                 for s in range(2)] for L in range(2)]
        shfl = [[P.dint(f"shfl{L}_{s}", [1, 8], I32, shared=True)
                 for s in range(2)] for L in range(2)]

        for L in range(2):
            pre = f"L{L}_"
            xsrc = x_in if L == 0 else x_st
            stage_preln(P, xsrc, y_d, tag=f"ln1_{L}")
            stage_proj(P, y_d, w[pre + "wqkv"], qk_d, 34, 32, tag=f"qkv_{L}")
            stage_exchange(P, qk_d, kp_d, vp_d, shkv[L], shfl[L],
                           MAGIC0 + L, tag=f"xc_{L}")
            stage_attn(P, qk_d, kp_d, vp_d, bias, ident, at_d, tag=f"at_{L}")
            stage_proj(P, at_d, w[pre + "wo"], z_d, 32, 0, tag=f"wo_{L}")
            stage_postln(P, z_d, xsrc, x_st, tag=f"pl1_{L}")
            stage_preln(P, x_st, y_d, tag=f"ln3_{L}")
            stage_gateup(P, y_d, w[pre + "pk_g"], w[pre + "pk_u"],
                         w[pre + "sc_g"], w[pre + "sc_u"],
                         w[pre + "qz_g"], w[pre + "qz_u"],
                         h_d, hs_d, sgcm, sucm, nzg, nzu, tag=f"gu_{L}")
            stage_down(P, h_d, hs_d, w[pre + "pk_d"], w[pre + "sc_d"],
                       w[pre + "qz_d"], z_d, sdcm, nzd, tag=f"dn_{L}")
            dst = x_st if L == 0 else x_out
            stage_postln(P, z_d, x_st, dst, tag=f"pl2_{L}")
    return P


_MEGA = [None]


def kernel(**inputs):
    sh, pc = prep_inputs(inputs)
    if _MEGA[0] is None:
        _MEGA[0] = build_mega()
    P = _MEGA[0]
    in_maps = []
    for c in range(NCORES):
        m = {"x_in": pc[c]["x_in"], "bias_tab": pc[c]["bias_tab"],
             "ident": pc[c]["ident"]}
        for L in range(2):
            pre = f"L{L}_"
            for k in ["wqkv", "wo", "pk_g", "pk_u", "pk_d", "sc_g", "sc_u",
                      "sc_d", "qz_g", "qz_u", "qz_d"]:
                m[pre + k] = sh[pre + k]
        in_maps.append(m)
    res = run_bass_kernel_spmd(P.nc, in_maps, core_ids=list(range(NCORES)))
    return unprep_output([res.results[c]["x_out"] for c in range(NCORES)])


# version 10
# speedup vs baseline: 1.1615x; 1.1615x over previous
import math
import contextlib
import numpy as np
import ml_dtypes

import concourse.bass as bass
import concourse.mybir as mybir
from concourse.bass_utils import run_bass_kernel_spmd

AF = mybir.ActivationFunctionType
ALU = mybir.AluOpType
F32 = mybir.dt.float32
BF16 = mybir.dt.bfloat16
U8 = mybir.dt.uint8
I32 = mybir.dt.int32
BF = ml_dtypes.bfloat16

D = 4096
II = 11008
NH = 32
HD = 128
S = 2048
BATCH = 4
M = 1024
MC = 512
NCORES = 8
NEG = -60000.0
SCALE_Q = float(HD) ** -0.5
EPS = 1e-5
MASK32 = 0x0F0F0F0F
TAU = [0, 2, 4, 6, 1, 3, 5, 7]


def _pi(n):
    q = np.arange(n)
    return 8 * (q // 8) + np.array(TAU)[q % 8]


PI_D = _pi(D)
PI_I = _pi(II)
SLOPES = [(2.0 ** (-2.0 ** (-(math.log2(NH) - 3)))) ** (i + 1) for i in range(NH)]



def _qtiles(parity):
    return [2 * t + parity for t in range(8)]


def _bias_table(parity):
    out = np.full((128, 9216), NEG, np.float32)
    myt = _qtiles(parity)
    pat = _qtiles(1 - parity)
    for t in range(8):
        off = 128 * t * (t + 1)
        w = (t + 1) * 128
        posq = myt[t] * 128 + np.arange(128)
        for j in range(t + 1):
            dA = (myt[j] * 128 + np.arange(128))[None, :] - posq[:, None]
            out[:, off + j * 128:off + (j + 1) * 128] = np.where(dA <= 0, dA, NEG)
            dB = (pat[j] * 128 + np.arange(128))[None, :] - posq[:, None]
            out[:, off + w + j * 128:off + w + (j + 1) * 128] = np.where(dB <= 0, dB, NEG)
    return out


def _pretile_packed(qw, kt):
    r, w = qw.shape
    assert r == kt * 128 and w % 32 == 0
    nchunks = w // 32
    a = qw.reshape(kt, 128, nchunks, 32)
    return np.ascontiguousarray(a.transpose(2, 1, 0, 3))


def prep_inputs(inp):
    pc = [{} for _ in range(NCORES)]
    x = np.asarray(inp["L0_attn_hidden_states"])
    ident = np.eye(128, dtype=BF)
    for c in range(NCORES):
        p, j = c // 2, c % 2
        toks = np.concatenate([np.arange(g * 128, (g + 1) * 128) for g in _qtiles(j)])
        xc = x[p][toks][:, PI_D]
        pc[c]["x_in"] = np.ascontiguousarray(xc.T.astype(np.float32))
        pc[c]["bias_tab"] = np.ascontiguousarray(_bias_table(j))
        pc[c]["ident"] = ident

    sh = {}
    for L in range(2):
        pre = f"L{L}_"
        if L == 0:
            wqkv = np.asarray(inp["L0_attn_Wqkv"])
            wo = np.asarray(inp["L0_attn_Wo"])
        else:
            wqkv = np.concatenate(
                [np.asarray(inp["L1_attn_Wq"]), np.asarray(inp["L1_attn_Wkv"])], axis=1)
            wo = np.asarray(inp["L1_attn_Wo"])
        sh[pre + "wqkv"] = np.ascontiguousarray(wqkv[PI_D, :].astype(np.float32))
        sh[pre + "wo"] = np.ascontiguousarray(wo[:, PI_D].astype(np.float32))
        f = f"L{L}_ffn_"
        sh[pre + "pk_g"] = _pretile_packed(np.asarray(inp[f + "gate_qweight"])[PI_D, :], 32)
        sh[pre + "pk_u"] = _pretile_packed(np.asarray(inp[f + "up_qweight"])[PI_D, :], 32)
        sh[pre + "pk_d"] = _pretile_packed(np.asarray(inp[f + "down_qweight"])[PI_I, :], 86)
        sh[pre + "sc_g"] = np.ascontiguousarray(np.asarray(inp[f + "gate_scales"])[:, PI_I].astype(np.float32))
        sh[pre + "sc_u"] = np.ascontiguousarray(np.asarray(inp[f + "up_scales"])[:, PI_I].astype(np.float32))
        sh[pre + "sc_d"] = np.ascontiguousarray(np.asarray(inp[f + "down_scales"])[:, PI_D].astype(np.float32))
        sh[pre + "qz_g"] = np.ascontiguousarray(np.asarray(inp[f + "gate_qzeros"], np.int32))
        sh[pre + "qz_u"] = np.ascontiguousarray(np.asarray(inp[f + "up_qzeros"], np.int32))
        sh[pre + "qz_d"] = np.ascontiguousarray(np.asarray(inp[f + "down_qzeros"], np.int32))
    return sh, pc


def unprep_output(outs_by_core):
    res = np.empty((BATCH, S, D), np.float32)
    inv = np.argsort(PI_D)
    for c in range(NCORES):
        p, j = c // 2, c % 2
        toks = np.concatenate([np.arange(g * 128, (g + 1) * 128) for g in _qtiles(j)])
        res[p][toks] = np.asarray(outs_by_core[c], np.float32).T[:, inv]
    return res



class Prog:
    def __init__(self, name):
        self.nc = bass.Bass("TRN2", target_bir_lowering=False)
        self.es = contextlib.ExitStack()
        self.cur = self.es
        self.name = name

    def sb(self, name, shape, dt):
        return self.cur.enter_context(self.nc.sbuf_tensor(name, shape, dt))

    def ps(self, name, shape, dt=F32):
        return self.cur.enter_context(self.nc.psum_tensor(name, shape, dt))

    def sem(self, name):
        return self.cur.enter_context(self.nc.semaphore(name))

    @contextlib.contextmanager
    def stage(self):
        old = self.cur
        with contextlib.ExitStack() as st_es:
            self.cur = st_es
            yield
        self.cur = old
        self.nc.reset()

    def din(self, name, shape, dt):
        return self.nc.dram_tensor(name, list(shape), dt, kind="ExternalInput").ap()

    def dout(self, name, shape, dt):
        return self.nc.dram_tensor(name, list(shape), dt, kind="ExternalOutput").ap()

    def dint(self, name, shape, dt, shared=False):
        kw = {"addr_space": "Shared"} if shared else {}
        return self.nc.dram_tensor(name, list(shape), dt, kind="Internal", **kw).ap()


def make_consts(P, values, tag="cst"):
    tiles = {v: P.sb(f"{tag}_{i}", [128, 1], F32) for i, v in enumerate(values)}
    with P.nc.Block() as blk:
        @blk.vector
        def _(e):
            for v, t in tiles.items():
                e.memset(t[:], float(v))
    return tiles


def load_rows(e, dst_sb, src_dram, kt_n, sem, grp=8):
    n = 0
    for k0 in range(0, kt_n, grp):
        k1 = min(kt_n, k0 + grp)
        e.dma_start(
            out=dst_sb[:, k0 * M:k1 * M],
            in_=src_dram.rearrange("(kt p) m -> p kt m", p=128)[:, k0:k1, :],
        ).then_inc(sem, 16)
        n += 1
    return n



def emit_ln(P, src_dram, dst_dram, res_dram=None, tag="ln"):
    nc = P.nc
    kt_n = 32
    src_is_bf = (src_dram.dtype == BF16)
    inv_n = 1.0 / float(kt_n * 128)
    out_f32 = res_dram is not None

    x_sb = P.sb(tag + "_x", [128, kt_n * M], BF16 if src_is_bf else F32)
    xb_sb = None if src_is_bf else P.sb(tag + "_xb", [128, 2, M], BF16)
    sq_sb = P.sb(tag + "_sq", [128, 2, M], BF16)
    ones_col = P.sb(tag + "_oc", [128, 1], BF16)
    ones_row = P.sb(tag + "_or", [1, 128], F32)
    vq = P.sb(tag + "_vq", [1, M], F32)
    vmean = P.sb(tag + "_vm", [1, M], F32)
    vrstd = P.sb(tag + "_vr", [1, M], F32)
    vmr = P.sb(tag + "_vmr", [1, M], F32)
    vtmp = P.sb(tag + "_vt", [1, M], F32)
    rstd_rep = P.sb(tag + "_rr", [128, M], F32)
    mr_rep = P.sb(tag + "_mrr", [128, M], F32)
    res_sb = P.sb(tag + "_res", [128, 2, M], F32) if res_dram is not None else None
    out_sb = P.sb(tag + "_out", [128, 2, M], F32 if out_f32 else BF16)
    tmp_sb = P.sb(tag + "_tmp", [128, M], F32)

    ps_s = P.ps(tag + "_ps", [1, M])
    ps_q = P.ps(tag + "_pq", [1, M])
    ps_r1 = P.ps(tag + "_pr1", [128, M])
    ps_r2 = P.ps(tag + "_pr2", [128, M])

    ld = P.sem(tag + "_ld")
    cv = P.sem(tag + "_cv")
    cq = P.sem(tag + "_cq")
    mm = P.sem(tag + "_mm")
    ev = P.sem(tag + "_ev")
    st = P.sem(tag + "_st")
    rl = P.sem(tag + "_rl")
    nv = P.sem(tag + "_nv")

    with nc.Block() as blk:
        @blk.sync
        def _(e):
            load_rows(e, x_sb, src_dram, kt_n, ld)

        if not src_is_bf:
            @blk.vector
            def _(e):
                e.memset(ones_col[:], 1.0)
                e.memset(ones_row[:], 1.0)
                for kt in range(kt_n):
                    e.wait_ge(ld, 16 * (kt // 8 + 1))
                    if kt >= 2:
                        e.wait_ge(mm, kt - 1)
                    e.tensor_copy(xb_sb[:, kt % 2, :],
                                  x_sb[:, kt * M:(kt + 1) * M]).then_inc(cv, 1)
        else:
            @blk.vector
            def _(e):
                e.memset(ones_col[:], 1.0)
                e.memset(ones_row[:], 1.0).then_inc(cv, 1)

        @blk.scalar
        def _(e):
            for kt in range(kt_n):
                e.wait_ge(ld, 16 * (kt // 8 + 1))
                if kt >= 2:
                    e.wait_ge(mm, kt - 1)
                e.activation(out=sq_sb[:, kt % 2, :], in_=x_sb[:, kt * M:(kt + 1) * M],
                             func=AF.Square).then_inc(cq, 1)

        @blk.tensor
        def _(e):
            for kt in range(kt_n):
                e.wait_ge(cv, 1 if src_is_bf else kt + 1)
                e.wait_ge(cq, kt + 1)
                xsrc = (x_sb[:, kt * M:(kt + 1) * M] if src_is_bf
                        else xb_sb[:, kt % 2, :])
                for mc in range(2):
                    e.matmul(ps_s[:, mc * MC:(mc + 1) * MC], ones_col[:],
                             xsrc[:, mc * MC:(mc + 1) * MC],
                             start=(kt == 0), stop=(kt == kt_n - 1))
                    i2 = e.matmul(ps_q[:, mc * MC:(mc + 1) * MC], ones_col[:],
                                  sq_sb[:, kt % 2, mc * MC:(mc + 1) * MC],
                                  start=(kt == 0), stop=(kt == kt_n - 1))
                i2.then_inc(mm, 1)

    with nc.Block() as blk:
        @blk.scalar
        def _(e):
            e.copy(vmean[:], ps_s[:])
            e.copy(vq[:], ps_q[:]).then_inc(ev, 1)
            e.wait_ge(ev, 2)
            e.activation(out=vq[:], in_=vq[:], func=AF.Sqrt).then_inc(ev, 1)

        @blk.vector
        def _(e):
            e.wait_ge(ev, 1)
            e.tensor_scalar_mul(vmean[:], vmean[:], inv_n)
            e.tensor_scalar_mul(vq[:], vq[:], inv_n)
            e.drain()
            e.tensor_tensor(out=vtmp[:], in0=vmean[:], in1=vmean[:], op=ALU.mult)
            e.drain()
            e.tensor_sub(vq[:], vq[:], vtmp[:])
            e.drain()
            e.tensor_scalar_add(vq[:], vq[:], EPS).then_inc(ev, 1)
            e.wait_ge(ev, 3)
            e.reciprocal(vrstd[:], vq[:])
            e.drain()
            e.tensor_mul(vmr[:], vmean[:], vrstd[:]).then_inc(ev, 1)

        @blk.tensor
        def _(e):
            e.wait_ge(ev, 4)
            for mc in range(2):
                e.matmul(ps_r1[:, mc * MC:(mc + 1) * MC], ones_row[:],
                         vrstd[:, mc * MC:(mc + 1) * MC], start=True, stop=True)
                i2 = e.matmul(ps_r2[:, mc * MC:(mc + 1) * MC], ones_row[:],
                              vmr[:, mc * MC:(mc + 1) * MC], start=True, stop=True)
            i2.then_inc(mm, 1)

    with nc.Block() as blk:
        @blk.scalar
        def _(e):
            e.copy(rstd_rep[:], ps_r1[:])
            e.copy(mr_rep[:], ps_r2[:]).then_inc(ev, 1)

        @blk.sync
        def _(e):
            if res_dram is not None:
                for kt in range(kt_n):
                    if kt >= 2:
                        e.wait_ge(nv, kt - 1)
                    e.dma_start(out=res_sb[:, kt % 2, :],
                                in_=res_dram[kt * 128:(kt + 1) * 128, :]).then_inc(rl, 16)

        @blk.vector
        def _(e):
            e.wait_ge(ev, 5)
            for kt in range(kt_n):
                xt = x_sb[:, kt * M:(kt + 1) * M]
                if kt >= 2:
                    e.wait_ge(st, 16 * (kt - 1))
                e.tensor_mul(tmp_sb[:], xt, rstd_rep[:])
                e.drain()
                if res_dram is not None:
                    e.wait_ge(rl, 16 * (kt + 1))
                    e.tensor_sub(tmp_sb[:], tmp_sb[:], mr_rep[:])
                    e.drain()
                    e.tensor_add(out_sb[:, kt % 2, :], tmp_sb[:],
                                 res_sb[:, kt % 2, :]).then_inc(nv, 1)
                else:
                    e.tensor_sub(out_sb[:, kt % 2, :], tmp_sb[:],
                                 mr_rep[:]).then_inc(nv, 1)

        @blk.gpsimd
        def _(e):
            for kt in range(kt_n):
                e.wait_ge(nv, kt + 1)
                e.dma_start(out=dst_dram[kt * 128:(kt + 1) * 128, :],
                            in_=out_sb[:, kt % 2, :]).then_inc(st, 16)
            e.wait_ge(st, 16 * kt_n)



def emit_proj_f32w(P, w_dram, rhs_sb, out_dram, kt_n, nt_n, qscale_tiles=0, tag="pj",
                   scale_ap=None):
    nc = P.nc
    NCB = 256
    ncb_n = nt_n * 128 // NCB

    wland = P.sb(tag + "_wl", [128, 2 * NCB], F32)
    w_sb = P.sb(tag + "_w", [128, 2, kt_n * NCB], BF16)
    out_sb = P.sb(tag + "_o", [128, 4, MC], BF16)
    psum = [P.ps(tag + f"_p{i}", [128, MC]) for i in range(8)]

    wld = [P.sem(tag + f"_wld{i}") for i in range(2)]
    wcv = P.sem(tag + "_wcv")
    mmg = P.sem(tag + "_mmg")
    ev = P.sem(tag + "_ev")
    st = [P.sem(tag + f"_st{i}") for i in range(4)]

    G_N = ncb_n * 4

    with nc.Block() as blk:
        @blk.sync
        def _(e):
            for c in range(ncb_n):
                for kt in range(kt_n):
                    i = c * kt_n + kt
                    if i >= 2:
                        e.wait_ge(wcv, i - 1)
                    e.dma_start(
                        out=wland[:, (i % 2) * NCB:(i % 2 + 1) * NCB],
                        in_=w_dram[kt * 128:(kt + 1) * 128, c * NCB:(c + 1) * NCB],
                    ).then_inc(wld[i % 2], 16)

        @blk.vector
        def _(e):
            for c in range(ncb_n):
                buf = c % 2
                if c >= 2:
                    e.wait_ge(mmg, (c - 1) * 4)
                for kt in range(kt_n):
                    i = c * kt_n + kt
                    e.wait_ge(wld[i % 2], 16 * (i // 2 + 1))
                    e.tensor_copy(w_sb[:, buf, kt * NCB:(kt + 1) * NCB],
                                  wland[:, (i % 2) * NCB:(i % 2 + 1) * NCB]).then_inc(wcv, 1)

        @blk.tensor
        def _(e):
            for c in range(ncb_n):
                buf = c % 2
                for mc in range(2):
                    for nt in range(2):
                        G = c * 4 + mc * 2 + nt
                        bank = G % 8
                        if G >= 8:
                            e.wait_ge(ev, G - 7)
                        for kt in range(kt_n):
                            if mc == 0 and nt == 0:
                                e.wait_ge(wcv, c * kt_n + kt + 1)
                            mi = e.matmul(
                                psum[bank][:],
                                w_sb[:, buf, kt * NCB + nt * 128: kt * NCB + (nt + 1) * 128],
                                rhs_sb[:, kt * M + mc * MC: kt * M + (mc + 1) * MC],
                                start=(kt == 0), stop=(kt == kt_n - 1))
                        mi.then_inc(mmg, 1)

        @blk.scalar
        def _(e):
            for c in range(ncb_n):
                for mc in range(2):
                    for nt in range(2):
                        G = c * 4 + mc * 2 + nt
                        bank = G % 8
                        e.wait_ge(mmg, G + 1)
                        if G >= 4:
                            e.wait_ge(st[G % 4], 16 * (G // 4))
                        if (2 * c + nt) < qscale_tiles:
                            e.activation(out=out_sb[:, G % 4, :], in_=psum[bank][:],
                                         func=AF.Copy, scale=scale_ap[:]).then_inc(ev, 1)
                        else:
                            e.copy(out_sb[:, G % 4, :], psum[bank][:]).then_inc(ev, 1)

        @blk.gpsimd
        def _(e):
            for c in range(ncb_n):
                for mc in range(2):
                    for nt in range(2):
                        G = c * 4 + mc * 2 + nt
                        e.wait_ge(ev, G + 1)
                        r0 = (2 * c + nt) * 128
                        e.dma_start(out=out_dram[r0:r0 + 128, mc * MC:(mc + 1) * MC],
                                    in_=out_sb[:, G % 4, :]).then_inc(st[G % 4], 16)
            for i in range(4):
                e.wait_ge(st[i], 16 * (G_N // 4 + (1 if i < G_N % 4 else 0)))


def build_ln_prog(post):
    P = Prog("postln" if post else "preln")
    with P.es:
        if post:
            z = P.din("z_in", [D, M], BF16)
            res = P.din("res_in", [D, M], F32)
            xo = P.dout("x_out", [D, M], F32)
            emit_ln(P, z, xo, res_dram=res)
        else:
            x = P.din("x_in", [D, M], F32)
            y = P.dout("y_out", [D, M], BF16)
            emit_ln(P, x, y)
    return P


def build_qkv_prog():
    P = Prog("qkv")
    with P.es:
        y = P.din("y_in", [D, M], BF16)
        w = P.din("w_in", [D, 4352], F32)
        qk = P.dout("qk_out", [4352, M], BF16)
        y_sb = P.sb("y", [128, 32 * M], BF16)
        ld = P.sem("yld")
        with P.nc.Block() as blk:
            @blk.sync
            def _(e):
                load_rows(e, y_sb, y, 32, ld)

            @blk.vector
            def _(e):
                e.wait_ge(ld, 16 * 4)
        cst = make_consts(P, [SCALE_Q], tag="qc")
        emit_proj_f32w(P, w, y_sb, qk, 32, 34, qscale_tiles=32, scale_ap=cst[SCALE_Q])
    return P


def build_wo_prog():
    P = Prog("wo")
    with P.es:
        at = P.din("at_in", [D, M], BF16)
        w = P.din("w_in", [D, D], F32)
        z = P.dout("z_out", [D, M], BF16)
        a_sb = P.sb("a", [128, 32 * M], BF16)
        ld = P.sem("ald")
        with P.nc.Block() as blk:
            @blk.sync
            def _(e):
                load_rows(e, a_sb, at, 32, ld)

            @blk.vector
            def _(e):
                e.wait_ge(ld, 16 * 4)
        emit_proj_f32w(P, w, a_sb, z, 32, 32)
    return P



OFF_T = [128 * t * (t + 1) for t in range(8)]


def emit_attn(P, qk_dram, kp_dram, vp_dram, bias_dram, id_dram, out_dram, tag="at"):
    nc = P.nc
    qk_sb = P.sb(tag + "_qk", [128, 34 * M], BF16)
    kp_sb = P.sb(tag + "_kp", [128, M], BF16)
    vp_sb = P.sb(tag + "_vp", [128, M], BF16)
    id_sb = P.sb(tag + "_id", [128, 128], BF16)
    vt_sb = P.sb(tag + "_vt", [128, 16 * 128], BF16)
    btab = P.sb(tag + "_bt", [128, 9216], F32)
    srow = P.sb(tag + "_sr", [128, 2, 2048], F32)
    probs = P.sb(tag + "_pb", [128, 2, 2048], BF16)
    ptb = P.sb(tag + "_pt", [128, 2, 2048], BF16)
    mA = P.sb(tag + "_mA", [128, 1], F32)
    negm = P.sb(tag + "_nm", [128, 2, 1], F32)
    sA = P.sb(tag + "_sA", [128, 2, 1], F32)
    rsum = P.sb(tag + "_rs", [128, 1], F32)
    out_sb = P.sb(tag + "_ao", [128, 4, 128], BF16)

    ps_sc = P.ps(tag + "_psc", [128, 2048])
    ps_tp = P.ps(tag + "_ptp", [128, 2048], BF16)
    ps_pv = P.ps(tag + "_ppv", [128, 128])

    ld = P.sem(tag + "_ld")
    s_mm = P.sem(tag + "_smm")
    s_sr = P.sem(tag + "_ssr")
    s_ex = P.sem(tag + "_sex")
    s_no = P.sem(tag + "_sno")
    s_tp = P.sem(tag + "_stp")
    s_pt = P.sem(tag + "_spt")
    s_pv = P.sem(tag + "_spv")
    s_ae = P.sem(tag + "_sae")
    s_ost = [P.sem(tag + f"_so{i}") for i in range(4)]

    with nc.Block() as blk:
        @blk.sync
        def _(e):
            e.dma_start(out=id_sb[:], in_=id_dram[:]).then_inc(ld, 16)
            e.dma_start(out=qk_sb[:, 32 * M:34 * M],
                        in_=qk_dram.rearrange("(kt p) m -> p kt m", p=128)[:, 32:34, :]
                        ).then_inc(ld, 16)
            e.dma_start(out=kp_sb[:], in_=kp_dram[:]).then_inc(ld, 16)
            e.dma_start(out=vp_sb[:], in_=vp_dram[:]).then_inc(ld, 16)
            e.dma_start(out=btab[:], in_=bias_dram[:]).then_inc(ld, 16)
            for g in range(4):
                e.dma_start(out=qk_sb[:, g * 8 * M:(g + 1) * 8 * M],
                            in_=qk_dram.rearrange("(kt p) m -> p kt m", p=128)
                            [:, g * 8:(g + 1) * 8, :]).then_inc(ld, 16)

        @blk.tensor
        def _(e):
            e.wait_ge(ld, 16 * 4)
            mi = None
            for j in range(8):
                mi = e.transpose(ps_tp[:, j * 128:(j + 1) * 128],
                                 qk_sb[:, 33 * M + j * 128: 33 * M + (j + 1) * 128], id_sb[:])
            mi.then_inc(s_tp, 1)
            for j in range(8):
                mi = e.transpose(ps_tp[:, 1024 + j * 128: 1024 + (j + 1) * 128],
                                 vp_sb[:, j * 128:(j + 1) * 128], id_sb[:])
            mi.then_inc(s_tp, 1)

        @blk.scalar
        def _(e):
            e.wait_ge(s_tp, 1)
            e.copy(vt_sb[:, 0:1024], ps_tp[:, 0:1024])
            e.wait_ge(s_tp, 2)
            e.copy(vt_sb[:, 1024:2048], ps_tp[:, 1024:2048]).then_inc(s_tp, 1)

    with nc.Block() as blk:
        def emit_transp(e, j):
            tj = j % 8
            e.wait_ge(s_no, j + 1)
            if j > 0:
                e.wait_ge(s_pt, j)
            pb = probs[:, j % 2, :]
            mi = None
            for jj in range(2 * (tj + 1)):
                mi = e.transpose(ps_tp[:, jj * 128:(jj + 1) * 128],
                                 pb[:, jj * 128:(jj + 1) * 128], id_sb[:])
            mi.then_inc(s_tp, 1)

        def emit_pv(e, j):
            tj = j % 8
            e.wait_ge(s_pt, j + 1)
            if j > 0:
                e.wait_ge(s_ae, j)
            pt = ptb[:, j % 2, :]
            for jj in range(tj + 1):
                e.matmul(ps_pv[:], vt_sb[:, jj * 128:(jj + 1) * 128],
                         pt[:, jj * 128:(jj + 1) * 128],
                         start=(jj == 0), stop=False)
                mi = e.matmul(ps_pv[:], vt_sb[:, 1024 + jj * 128:1024 + (jj + 1) * 128],
                              pt[:, (tj + 1 + jj) * 128:(tj + 2 + jj) * 128],
                              start=False, stop=(jj == tj))
            mi.then_inc(s_pv, 1)

        @blk.tensor
        def _(e):
            for k in range(258):
                if k <= 255:
                    h, t = divmod(k, 8)
                    w = (t + 1) * 128
                    if t == 0 and h % 8 == 0:
                        e.wait_ge(ld, 16 * (6 + h // 8))
                    if k > 0:
                        e.wait_ge(s_sr, k)
                    q_ap = qk_sb[:, h * M + t * 128: h * M + (t + 1) * 128]
                    for c0 in range(0, w, MC):
                        cw = min(MC, w - c0)
                        e.matmul(ps_sc[:, c0:c0 + cw], q_ap,
                                 qk_sb[:, 32 * M + c0: 32 * M + c0 + cw],
                                 start=True, stop=True)
                    a = w
                    while a < 2 * w:
                        b = min(2 * w, (a // MC + 1) * MC)
                        mi = e.matmul(ps_sc[:, a:b], q_ap,
                                      kp_sb[:, a - w: b - w], start=True, stop=True)
                        a = b
                    mi.then_inc(s_mm, 1)
                if 1 <= k <= 256:
                    emit_transp(e, k - 1)
                if k >= 2:
                    emit_pv(e, k - 2)

        def sm_front(e, it):
            h, t = divmod(it, 8)
            w = (t + 1) * 128
            e.wait_ge(s_mm, it + 1)
            if it == 0:
                e.wait_ge(ld, 16 * 5)
            sr = srow[:, it % 2, :]
            off = OFF_T[t]
            sl = float(SLOPES[h])
            e.scalar_tensor_tensor(out=sr[:, 0:2 * w], in0=btab[:, off:off + 2 * w],
                                   scalar=sl, in1=ps_sc[:, 0:2 * w],
                                   op0=ALU.mult, op1=ALU.add)
            e.drain()
            e.reduce_max(mA[:], sr[:, 0:2 * w], axis=mybir.AxisListType.X)
            e.drain()
            e.tensor_scalar_mul(negm[:, it % 2, :], mA[:], -1.0).then_inc(s_sr, 1)

        def sm_back(e, it):
            h, t = divmod(it, 8)
            w = (t + 1) * 128
            e.wait_ge(s_ex, it + 1)
            e.reciprocal(rsum[:], sA[:, it % 2, :])
            e.drain()
            pb = probs[:, it % 2, :]
            e.tensor_scalar(out=pb[:, 0:2 * w], in0=pb[:, 0:2 * w],
                            scalar1=rsum[:], scalar2=None, op0=ALU.mult).then_inc(s_no, 1)

        @blk.vector
        def _(e):
            sm_front(e, 0)
            for it in range(1, 256):
                sm_front(e, it)
                sm_back(e, it - 1)
            sm_back(e, 255)

        @blk.scalar
        def _(e):
            for k in range(258):
                if k <= 255:
                    it = k
                    h, t = divmod(it, 8)
                    w = (t + 1) * 128
                    e.wait_ge(s_sr, it + 1)
                    if it >= 2:
                        e.wait_ge(s_tp, 3 + it - 1)
                        e.wait_ge(s_no, it - 1)
                    pb = probs[:, it % 2, :]
                    sr = srow[:, it % 2, :]
                    e.activation(out=pb[:, 0:2 * w], in_=sr[:, 0:2 * w], func=AF.Exp,
                                 bias=negm[:, it % 2, :],
                                 accum_out=sA[:, it % 2, :]).then_inc(s_ex, 1)
                if 1 <= k <= 256:
                    j = k - 1
                    tj = j % 8
                    wj2 = 2 * (tj + 1) * 128
                    e.wait_ge(s_tp, 3 + j + 1)
                    if j >= 2:
                        e.wait_ge(s_pv, j - 1)
                    pt = ptb[:, j % 2, :]
                    e.copy(pt[:, 0:wj2], ps_tp[:, 0:wj2]).then_inc(s_pt, 1)
                if k >= 2:
                    j = k - 2
                    e.wait_ge(s_pv, j + 1)
                    if j >= 4:
                        e.wait_ge(s_ost[j % 4], 16 * (j // 4))
                    e.copy(out_sb[:, j % 4, :], ps_pv[:]).then_inc(s_ae, 1)

        @blk.gpsimd
        def _(e):
            for j in range(256):
                hj, tj = divmod(j, 8)
                e.wait_ge(s_ae, j + 1)
                e.dma_start(out=out_dram[hj * 128:(hj + 1) * 128,
                                         tj * 128:(tj + 1) * 128],
                            in_=out_sb[:, j % 4, :]).then_inc(s_ost[j % 4], 16)
            for i in range(4):
                e.wait_ge(s_ost[i], 16 * 64)


def build_attn_prog():
    P = Prog("attn")
    with P.es:
        qk = P.din("qk_in", [4352, M], BF16)
        kp = P.din("kp_in", [128, M], BF16)
        vp = P.din("vp_in", [128, M], BF16)
        bias = P.din("bias_tab", [128, 9216], F32)
        ident = P.din("ident", [128, 128], BF16)
        ao = P.dout("at_out", [D, M], BF16)
        emit_attn(P, qk, kp, vp, bias, ident, ao)
    return P



def emit_ffn_prep(P, sc_dram, qz_dram, cm_dram, nzs_dram, groups, nchunks, words, tag):
    nc = P.nc
    cols = nchunks * 256
    with contextlib.ExitStack() as tes:
        s_sb = tes.enter_context(nc.sbuf_tensor(tag + "_s", [groups, cols], F32))
        sb_bf = tes.enter_context(nc.sbuf_tensor(tag + "_sb", [groups, cols], BF16))
        qz_sb = tes.enter_context(nc.sbuf_tensor(tag + "_qz", [groups, words], I32))
        hi32 = tes.enter_context(nc.sbuf_tensor(tag + "_hi", [groups, words], I32))
        nzs = tes.enter_context(nc.sbuf_tensor(tag + "_z", [groups, cols], BF16))
        ld = P.sem(tag + "_ld")
        dn = P.sem(tag + "_dn")
        with nc.Block() as blk:
            @blk.sync
            def _(e):
                e.dma_start(out=s_sb[:], in_=sc_dram[:]).then_inc(ld, 16)
                e.dma_start(out=qz_sb[:], in_=qz_dram[:]).then_inc(ld, 16)

            @blk.vector
            def _(e):
                e.wait_ge(ld, 32)
                e.tensor_copy(sb_bf[:], s_sb[:]).then_inc(dn, 1)
                e.drain()
                e.tensor_scalar(out=hi32[:], in0=qz_sb[:], scalar1=4, scalar2=MASK32,
                                op0=ALU.logical_shift_right, op1=ALU.bitwise_and)
                e.tensor_single_scalar(out=qz_sb[:], in_=qz_sb[:], scalar=MASK32,
                                       op=ALU.bitwise_and)
                e.drain()
                e.tensor_tensor(out=nzs[:].rearrange("g (w two) -> g w two", two=2)[:, :, 0],
                                in0=qz_sb[:].bitcast(U8),
                                in1=sb_bf[:].rearrange("g (w two) -> g w two", two=2)[:, :, 0],
                                op=ALU.mult)
                e.tensor_tensor(out=nzs[:].rearrange("g (w two) -> g w two", two=2)[:, :, 1],
                                in0=hi32[:].bitcast(U8),
                                in1=sb_bf[:].rearrange("g (w two) -> g w two", two=2)[:, :, 1],
                                op=ALU.mult)
                e.drain()
                e.tensor_scalar_mul(nzs[:], nzs[:], -1.0).then_inc(dn, 1)

            @blk.gpsimd
            def _(e):
                e.wait_ge(dn, 1)
                e.dma_start(
                    out=cm_dram.rearrange("nc g w -> g nc w"),
                    in_=sb_bf[:].rearrange("g (nc w) -> g nc w", w=256),
                ).then_inc(dn, 16)
                e.wait_ge(dn, 18)
                e.dma_start(out=nzs_dram[:], in_=nzs[:]).then_inc(dn, 16)
                e.wait_ge(dn, 34)


def emit_gateup(P, y_sb, pkg_dram, pku_dram, sgcm_dram, sucm_dram, nzg_dram, nzu_dram,
                xsum_sb, h_dram, hsum_dram, tag="gu"):
    nc = P.nc
    NCH = 43
    kt_n = 32

    pk = {m: P.sb(tag + f"_pk{m}", [128, 2, 1024], I32) for m in "gu"}
    hi = {m: P.sb(tag + f"_hi{m}", [128, 1024], I32) for m in "gu"}
    nz = {m: P.sb(tag + f"_nz{m}", [32, 2, 256], BF16) for m in "gu"}
    srep = {m: P.sb(tag + f"_sr{m}", [128, 8192], BF16) for m in "gu"}
    qs = {m: P.sb(tag + f"_qs{m}", [128, 8192], BF16) for m in "gu"}
    hg = P.sb(tag + "_hg", [128, 2, MC], F32)
    hu = P.sb(tag + "_hu", [128, 2, MC], BF16)
    htile = P.sb(tag + "_ht", [128, 4, MC], BF16)
    stair = P.sb(tag + "_stair", [128, 171], BF16)
    hsum_sb = P.sb(tag + "_hs", [86, M], BF16)

    ps_g = [P.ps(tag + f"_pg{i}", [128, MC]) for i in range(4)]
    ps_u = [P.ps(tag + f"_pu{i}", [128, MC]) for i in range(2)]
    ps_hs = P.ps(tag + "_phs", [86, M])

    s_pk = [P.sem(tag + f"_spk{i}") for i in range(2)]
    s_srg = P.sem(tag + "_ssrg")
    s_sru = P.sem(tag + "_ssru")
    s_dqg = P.sem(tag + "_sdqg")
    s_dqu = P.sem(tag + "_sdqu")
    s_peg = P.sem(tag + "_speg")
    s_peu = P.sem(tag + "_speu")
    s_sig = P.sem(tag + "_ssig")
    s_cu = P.sem(tag + "_scu")
    s_ht = P.sem(tag + "_sht")
    s_hsm = P.sem(tag + "_shsm")
    s_hst = [P.sem(tag + f"_shst{i}") for i in range(4)]
    s_fin = P.sem(tag + "_sfin")

    with nc.Block() as blk:
        @blk.vector
        def _(e):
            e.memset(stair[:], 0.0)
            e.drain()
            e.memset(stair[:, 85:86], 1.0)
            e.drain()
            for c in range(NCH):
                e.wait_ge(s_srg, 16 * (c + 1))
                e.wait_ge(s_pk[c % 2], 64 * (c // 2 + 1))
                if c >= 1:
                    e.wait_ge(s_peg, 4 * c)
                for m, s_sr_sem, s_dq_sem, s_pe_sem in (
                        ("g", s_srg, s_dqg, s_peg), ("u", s_sru, s_dqu, s_peu)):
                    if m == "u":
                        e.wait_ge(s_sru, 16 * (c + 1))
                        if c >= 1:
                            e.wait_ge(s_peu, 4 * c)
                    e.tensor_scalar(out=hi[m][:], in0=pk[m][:, c % 2, :], scalar1=4,
                                    scalar2=MASK32, op0=ALU.logical_shift_right,
                                    op1=ALU.bitwise_and)
                    e.drain()
                    e.tensor_single_scalar(out=pk[m][:, c % 2, :], in_=pk[m][:, c % 2, :],
                                           scalar=MASK32, op=ALU.bitwise_and)
                    e.drain()
                    qv = qs[m][:].rearrange("p (kt i two) -> p kt i two", kt=32, two=2)
                    sv = srep[m][:].rearrange("p (kt i two) -> p kt i two", kt=32, two=2)
                    e.tensor_tensor(out=qv[:, :, :, 0],
                                    in0=pk[m][:, c % 2, :].bitcast(U8).rearrange(
                                        "p (kt i) -> p kt i", kt=32),
                                    in1=sv[:, :, :, 0], op=ALU.mult)
                    e.tensor_tensor(out=qv[:, :, :, 1],
                                    in0=hi[m][:].bitcast(U8).rearrange(
                                        "p (kt i) -> p kt i", kt=32),
                                    in1=sv[:, :, :, 1], op=ALU.mult
                                    ).then_inc(s_dqg if m == "g" else s_dqu, 1)

        @blk.sync
        def _(e):
            for c in range(NCH):
                if c >= 2:
                    e.wait_ge(s_dqu, c - 1)
                    e.wait_ge(s_peu, 4 * (c - 1))
                e.dma_start(out=pk["g"][:, c % 2, :],
                            in_=pkg_dram[c].rearrange("p kt w -> p (kt w)")
                            ).then_inc(s_pk[c % 2], 16)
                e.dma_start(out=pk["u"][:, c % 2, :],
                            in_=pku_dram[c].rearrange("p kt w -> p (kt w)")
                            ).then_inc(s_pk[c % 2], 16)
                e.dma_start(out=nz["g"][:, c % 2, :], in_=nzg_dram[:, c * 256:(c + 1) * 256]
                            ).then_inc(s_pk[c % 2], 16)
                e.dma_start(out=nz["u"][:, c % 2, :], in_=nzu_dram[:, c * 256:(c + 1) * 256]
                            ).then_inc(s_pk[c % 2], 16)
                if c >= 1:
                    e.wait_ge(s_dqg, c)
                e.dma_start(out=srep["g"][:],
                            in_=sgcm_dram[c].rearrange("g w -> (g w)")
                            .unsqueeze(0).partition_broadcast(128).squeeze(1)
                            ).then_inc(s_srg, 16)
                if c >= 1:
                    e.wait_ge(s_dqu, c)
                e.dma_start(out=srep["u"][:],
                            in_=sucm_dram[c].rearrange("g w -> (g w)")
                            .unsqueeze(0).partition_broadcast(128).squeeze(1)
                            ).then_inc(s_sru, 16)

        @blk.scalar
        def _(e):
            for c in range(NCH):
                for k in range(4):
                    G = c * 4 + k
                    e.wait_ge(s_peg, G + 1)
                    if G >= 2:
                        e.wait_ge(s_ht, G - 1)
                    e.activation(out=hg[:, k % 2, :], in_=ps_g[k][:],
                                 func=AF.Silu).then_inc(s_sig, 1)
                    e.wait_ge(s_peu, G + 1)
                    e.copy(hu[:, k % 2, :], ps_u[k % 2][:]).then_inc(s_cu, 1)

        @blk.tensor
        def _(e):
            for c in range(NCH):
                e.wait_ge(s_dqg, c + 1)
                for k in range(4):
                    mc, nt = divmod(k, 2)
                    G = c * 4 + k
                    if c >= 1:
                        e.wait_ge(s_sig, 4 * (c - 1) + k + 1)
                    for kt in range(kt_n):
                        e.matmul(ps_g[k][:],
                                 qs["g"][:, kt * 256 + nt * 128: kt * 256 + (nt + 1) * 128],
                                 y_sb[:, kt * M + mc * MC: kt * M + (mc + 1) * MC],
                                 start=(kt == 0), stop=False)
                    e.matmul(ps_g[k][:], nz["g"][:, c % 2, nt * 128:(nt + 1) * 128],
                             xsum_sb[:, mc * MC:(mc + 1) * MC], start=False, stop=True
                             ).then_inc(s_peg, 1)
                e.wait_ge(s_dqu, c + 1)
                for k in range(4):
                    mc, nt = divmod(k, 2)
                    G = c * 4 + k
                    if G >= 2:
                        e.wait_ge(s_cu, G - 1)
                    for kt in range(kt_n):
                        e.matmul(ps_u[k % 2][:],
                                 qs["u"][:, kt * 256 + nt * 128: kt * 256 + (nt + 1) * 128],
                                 y_sb[:, kt * M + mc * MC: kt * M + (mc + 1) * MC],
                                 start=(kt == 0), stop=False)
                    e.matmul(ps_u[k % 2][:], nz["u"][:, c % 2, nt * 128:(nt + 1) * 128],
                             xsum_sb[:, mc * MC:(mc + 1) * MC], start=False, stop=True
                             ).then_inc(s_peu, 1)
                e.wait_ge(s_ht, 4 * (c + 1))
                for k in range(4):
                    mc, nt = divmod(k, 2)
                    e.matmul(ps_hs[:, mc * MC:(mc + 1) * MC],
                             stair[:, 85 - (2 * c + nt): 171 - (2 * c + nt)][:, 0:86],
                             htile[:, k, :],
                             start=(c == 0 and nt == 0), stop=(c == NCH - 1 and nt == 1))
                e.nop().then_inc(s_hsm, 1)

        @blk.gpsimd
        def _(e):
            for c in range(NCH):
                for k in range(4):
                    mc, nt = divmod(k, 2)
                    G = c * 4 + k
                    e.wait_ge(s_sig, G + 1)
                    e.wait_ge(s_cu, G + 1)
                    if c >= 1:
                        e.wait_ge(s_hsm, c)
                        e.wait_ge(s_hst[k], 16 * c)
                    e.tensor_tensor(out=htile[:, k, :], in0=hg[:, k % 2, :],
                                    in1=hu[:, k % 2, :], op=ALU.mult).then_inc(s_ht, 1)
                    e.wait_ge(s_ht, c * 4 + k + 1)
                    e.dma_start(out=h_dram[(2 * c + nt) * 128:(2 * c + nt + 1) * 128,
                                           mc * MC:(mc + 1) * MC],
                                in_=htile[:, k, :]).then_inc(s_hst[k], 16)
            for k in range(4):
                e.wait_ge(s_hst[k], 16 * NCH)

    with nc.Block() as blk:
        @blk.scalar
        def _(e):
            e.copy(hsum_sb[:], ps_hs[:]).then_inc(s_fin, 1)

        @blk.gpsimd
        def _(e):
            e.wait_ge(s_fin, 1)
            e.dma_start(out=hsum_dram[:], in_=hsum_sb[:]).then_inc(s_fin, 16)
            e.wait_ge(s_fin, 17)


def build_gateup_prog():
    P = Prog("gateup")
    with P.es:
        y = P.din("y_in", [D, M], BF16)
        pkg = P.din("pk_g", [43, 128, 32, 32], I32)
        pku = P.din("pk_u", [43, 128, 32, 32], I32)
        scg = P.din("sc_g", [32, II], F32)
        scu = P.din("sc_u", [32, II], F32)
        qzg = P.din("qz_g", [32, 1376], I32)
        qzu = P.din("qz_u", [32, 1376], I32)
        h = P.dout("h_out", [II, M], BF16)
        hsum = P.dout("hsum_out", [86, M], BF16)
        sgcm = P.dint("sg_cm", [43, 32, 256], BF16)
        sucm = P.dint("su_cm", [43, 32, 256], BF16)
        nzg = P.dint("nzg", [32, II], BF16)
        nzu = P.dint("nzu", [32, II], BF16)

        y_sb = P.sb("y", [128, 32 * M], BF16)
        xsum_sb = P.sb("xs", [32, M], BF16)
        stair32 = P.sb("st32", [128, 63], BF16)
        xs_es = contextlib.ExitStack()
        ps_xs = xs_es.enter_context(P.nc.psum_tensor("pxs", [32, M], F32))
        ld = P.sem("yld")
        xs = P.sem("xsem")

        with P.nc.Block() as blk:
            @blk.sync
            def _(e):
                load_rows(e, y_sb, y, 32, ld)

            @blk.vector
            def _(e):
                e.memset(stair32[:], 0.0)
                e.drain()
                e.memset(stair32[:, 31:32], 1.0).then_inc(xs, 1)

            @blk.tensor
            def _(e):
                e.wait_ge(xs, 1)
                e.wait_ge(ld, 16 * 4)
                for kt in range(32):
                    for mc in range(2):
                        mi = e.matmul(ps_xs[:, mc * MC:(mc + 1) * MC],
                                      stair32[:, 31 - kt: 63 - kt][:, 0:32],
                                      y_sb[:, kt * M + mc * MC: kt * M + (mc + 1) * MC],
                                      start=(kt == 0), stop=(kt == 31))
                mi.then_inc(xs, 1)

            @blk.scalar
            def _(e):
                e.wait_ge(xs, 2)
                e.copy(xsum_sb[:], ps_xs[:]).then_inc(xs, 1)
        xs_es.close()

        emit_ffn_prep(P, scg, qzg, sgcm, nzg, 32, 43, 1376, "spg")
        emit_ffn_prep(P, scu, qzu, sucm, nzu, 32, 43, 1376, "spu")
        emit_gateup(P, y_sb, pkg, pku, sgcm, sucm, nzg, nzu, xsum_sb, h, hsum)
    return P



def emit_down(P, h_dram, hsum_sb, pkd_dram, sdcm_dram, nzd_dram, z_dram, tag="dn"):
    nc = P.nc
    NDC = 16
    kt_n = 86
    KH = 43

    hh_sb = P.sb(tag + "_hh", [128, kt_n * MC], BF16)
    pk_sb = P.sb(tag + "_pk", [128, kt_n * 32], I32)
    hi = P.sb(tag + "_hi", [128, KH * 32], I32)
    srep = P.sb(tag + "_sr", [128, kt_n * 256], BF16)
    qs = P.sb(tag + "_qs", [128, kt_n * 256], BF16)
    nzd = P.sb(tag + "_nz", [86, 256], BF16)
    out_sb = P.sb(tag + "_o", [128, 4, MC], BF16)

    psum = [P.ps(tag + f"_p{i}", [128, MC]) for i in range(4)]

    s_hl = P.sem(tag + "_shl")
    s_pkh = [P.sem(tag + f"_spk{i}") for i in range(2)]
    s_nz = P.sem(tag + "_snz")
    s_srh = [P.sem(tag + f"_ssr{i}") for i in range(2)]
    s_dq = P.sem(tag + "_sdq")
    s_ph = P.sem(tag + "_sph")
    s_pe = P.sem(tag + "_spe")
    s_ev = P.sem(tag + "_sev")
    s_st = [P.sem(tag + f"_sst{i}") for i in range(4)]

    def deq_half(e, i, h):
        e.wait_ge(s_srh[h], 16 * (i + 1))
        e.wait_ge(s_pkh[h], 16 * (i + 1))
        if h == 1:
            e.wait_ge(s_nz, 16 * (i + 1))
        if i >= 1:
            if h == 0:
                e.wait_ge(s_ph, i)
            else:
                e.wait_ge(s_pe, 2 * i)
        pkh = pk_sb[:, h * KH * 32:(h + 1) * KH * 32]
        e.tensor_scalar(out=hi[:], in0=pkh, scalar1=4, scalar2=MASK32,
                        op0=ALU.logical_shift_right, op1=ALU.bitwise_and)
        e.drain()
        e.tensor_single_scalar(out=pkh, in_=pkh, scalar=MASK32, op=ALU.bitwise_and)
        e.drain()
        qv = qs[:, h * KH * 256:(h + 1) * KH * 256].rearrange(
            "p (kt i two) -> p kt i two", kt=KH, two=2)
        sv = srep[:, h * KH * 256:(h + 1) * KH * 256].rearrange(
            "p (kt i two) -> p kt i two", kt=KH, two=2)
        e.tensor_tensor(out=qv[:, :, :, 0],
                        in0=pkh.bitcast(U8).rearrange("p (kt i) -> p kt i", kt=KH),
                        in1=sv[:, :, :, 0], op=ALU.mult)
        e.tensor_tensor(out=qv[:, :, :, 1],
                        in0=hi[:].bitcast(U8).rearrange("p (kt i) -> p kt i", kt=KH),
                        in1=sv[:, :, :, 1], op=ALU.mult).then_inc(s_dq, 1)

    with nc.Block() as blk:
        @blk.sync
        def _(e):
            for mc in range(2):
                if mc == 1:
                    e.wait_ge(s_pe, NDC * 2)
                for k0 in range(0, kt_n, 8):
                    k1 = min(kt_n, k0 + 8)
                    e.dma_start(
                        out=hh_sb[:, k0 * MC:k1 * MC],
                        in_=h_dram.rearrange("(kt p) m -> p kt m", p=128)[
                            :, k0:k1, mc * MC:(mc + 1) * MC],
                    ).then_inc(s_hl, 16)
                for dc in range(NDC):
                    i = mc * NDC + dc
                    if i >= 1:
                        e.wait_ge(s_dq, 2 * i)
                    for h in range(2):
                        e.dma_start(out=pk_sb[:, h * KH * 32:(h + 1) * KH * 32],
                                    in_=pkd_dram[dc][:, h * KH:(h + 1) * KH, :]
                                    .rearrange("p kt w -> p (kt w)")).then_inc(s_pkh[h], 16)
                    for h in range(2):
                        e.dma_start(
                            out=srep[:, h * KH * 256:(h + 1) * KH * 256],
                            in_=sdcm_dram[dc][h * KH:(h + 1) * KH, :]
                            .rearrange("g w -> (g w)")
                            .unsqueeze(0).partition_broadcast(128).squeeze(1)
                        ).then_inc(s_srh[h], 16)
                    if i >= 1:
                        e.wait_ge(s_pe, 2 * i)
                    e.dma_start(out=nzd[:], in_=nzd_dram[:, dc * 256:(dc + 1) * 256]
                                ).then_inc(s_nz, 16)

        @blk.scalar
        def _(e):
            for mc in range(2):
                for dc in range(NDC):
                    i = mc * NDC + dc
                    for nt in range(2):
                        G = i * 2 + nt
                        e.wait_ge(s_pe, G + 1)
                        if G >= 4:
                            e.wait_ge(s_st[G % 4], 16 * (G // 4))
                        e.copy(out_sb[:, G % 4, :], psum[G % 4][:]).then_inc(s_ev, 1)

        @blk.vector
        def _(e):
            for mc in range(2):
                for dc in range(NDC):
                    i = mc * NDC + dc
                    deq_half(e, i, 0)
                    deq_half(e, i, 1)

        @blk.tensor
        def _(e):
            for mc in range(2):
                for dc in range(NDC):
                    i = mc * NDC + dc
                    e.wait_ge(s_dq, 2 * i + 1)
                    if mc == 0 and dc == 0:
                        e.wait_ge(s_hl, 16 * 11)
                    if mc == 1 and dc == 0:
                        e.wait_ge(s_hl, 16 * 22)
                    for nt in range(2):
                        G = i * 2 + nt
                        if G >= 4:
                            e.wait_ge(s_ev, G - 3)
                        for kt in range(KH):
                            mi = e.matmul(psum[G % 4][:],
                                          qs[:, kt * 256 + nt * 128: kt * 256 + (nt + 1) * 128],
                                          hh_sb[:, kt * MC:(kt + 1) * MC],
                                          start=(kt == 0), stop=False)
                        if nt == 1:
                            mi.then_inc(s_ph, 1)
                    e.wait_ge(s_dq, 2 * i + 2)
                    for nt in range(2):
                        G = i * 2 + nt
                        for kt in range(KH, kt_n):
                            e.matmul(psum[G % 4][:],
                                     qs[:, kt * 256 + nt * 128: kt * 256 + (nt + 1) * 128],
                                     hh_sb[:, kt * MC:(kt + 1) * MC],
                                     start=False, stop=False)
                        e.matmul(psum[G % 4][:],
                                 nzd[:, nt * 128:(nt + 1) * 128],
                                 hsum_sb[:, mc * MC:(mc + 1) * MC], start=False, stop=True
                                 ).then_inc(s_pe, 1)

        @blk.gpsimd
        def _(e):
            for mc in range(2):
                for dc in range(NDC):
                    for nt in range(2):
                        G = (mc * NDC + dc) * 2 + nt
                        e.wait_ge(s_ev, G + 1)
                        r0 = (2 * dc + nt) * 128
                        e.dma_start(out=z_dram[r0:r0 + 128, mc * MC:(mc + 1) * MC],
                                    in_=out_sb[:, G % 4, :]).then_inc(s_st[G % 4], 16)
            for i in range(4):
                e.wait_ge(s_st[i], 16 * NDC)


def build_down_prog():
    P = Prog("down")
    with P.es:
        h = P.din("h_in", [II, M], BF16)
        hsum = P.din("hsum_in", [86, M], BF16)
        pkd = P.din("pk_d", [16, 128, 86, 32], I32)
        scd = P.din("sc_d", [86, D], F32)
        qzd = P.din("qz_d", [86, 512], I32)
        z = P.dout("z_out", [D, M], BF16)
        sdcm = P.dint("sd_cm", [16, 86, 256], BF16)
        nzd = P.dint("nzd", [86, D], BF16)

        hsum_sb = P.sb("hs", [86, M], BF16)
        ld = P.sem("hld")
        with P.nc.Block() as blk:
            @blk.sync
            def _(e):
                e.dma_start(out=hsum_sb[:], in_=hsum[:]).then_inc(ld, 16)

            @blk.vector
            def _(e):
                e.wait_ge(ld, 16)
        emit_ffn_prep(P, scd, qzd, sdcm, nzd, 86, 16, 512, "spd")
        emit_down(P, h, hsum_sb, pkd, sdcm, nzd, z)
    return P



_PROGS = {}


def _prog(name, builder):
    if name not in _PROGS:
        _PROGS[name] = builder()
    return _PROGS[name]


def _run(P, per_core_inputs):
    res = run_bass_kernel_spmd(P.nc, per_core_inputs, core_ids=list(range(NCORES)))
    return res.results


def kernel_multilaunch(**inputs):
    sh, pc = prep_inputs(inputs)
    x = [pc[c]["x_in"] for c in range(NCORES)]

    p_ln = _prog("preln", lambda: build_ln_prog(False))
    p_post = _prog("postln", lambda: build_ln_prog(True))
    p_qkv = _prog("qkv", build_qkv_prog)
    p_attn = _prog("attn", build_attn_prog)
    p_wo = _prog("wo", build_wo_prog)
    p_gu = _prog("gateup", build_gateup_prog)
    p_dn = _prog("down", build_down_prog)

    for L in range(2):
        pre = f"L{L}_"
        r = _run(p_ln, [{"x_in": x[c]} for c in range(NCORES)])
        y = [r[c]["y_out"] for c in range(NCORES)]
        r = _run(p_qkv, [{"y_in": y[c], "w_in": sh[pre + "wqkv"]} for c in range(NCORES)])
        qk = [r[c]["qk_out"] for c in range(NCORES)]
        ins = []
        for c in range(NCORES):
            part = c ^ 1
            ins.append({
                "qk_in": qk[c],
                "kp_in": np.ascontiguousarray(qk[part][32 * 128:33 * 128, :]),
                "vp_in": np.ascontiguousarray(qk[part][33 * 128:34 * 128, :]),
                "bias_tab": pc[c]["bias_tab"],
                "ident": pc[c]["ident"],
            })
        r = _run(p_attn, ins)
        at = [r[c]["at_out"] for c in range(NCORES)]
        r = _run(p_wo, [{"at_in": at[c], "w_in": sh[pre + "wo"]} for c in range(NCORES)])
        z = [r[c]["z_out"] for c in range(NCORES)]
        r = _run(p_post, [{"z_in": z[c], "res_in": x[c]} for c in range(NCORES)])
        x = [r[c]["x_out"] for c in range(NCORES)]
        r = _run(p_ln, [{"x_in": x[c]} for c in range(NCORES)])
        y2 = [r[c]["y_out"] for c in range(NCORES)]
        r = _run(p_gu, [{"y_in": y2[c], "pk_g": sh[pre + "pk_g"], "pk_u": sh[pre + "pk_u"],
                         "sc_g": sh[pre + "sc_g"], "sc_u": sh[pre + "sc_u"],
                         "qz_g": sh[pre + "qz_g"], "qz_u": sh[pre + "qz_u"]}
                        for c in range(NCORES)])
        h = [r[c]["h_out"] for c in range(NCORES)]
        hs = [r[c]["hsum_out"] for c in range(NCORES)]
        r = _run(p_dn, [{"h_in": h[c], "hsum_in": hs[c], "pk_d": sh[pre + "pk_d"],
                         "sc_d": sh[pre + "sc_d"], "qz_d": sh[pre + "qz_d"]}
                        for c in range(NCORES)])
        z2 = [r[c]["z_out"] for c in range(NCORES)]
        r = _run(p_post, [{"z_in": z2[c], "res_in": x[c]} for c in range(NCORES)])
        x = [r[c]["x_out"] for c in range(NCORES)]

    return unprep_output(x)



MAGIC0 = 0x5EED0001
MAGIC1 = 0x5EED0002


def stage_preln(P, src_dram, dst_dram, tag):
    with P.stage():
        emit_ln(P, src_dram, dst_dram, tag=tag)


def stage_postln(P, z_dram, res_dram, dst_dram, tag):
    with P.stage():
        emit_ln(P, z_dram, dst_dram, res_dram=res_dram, tag=tag)


def stage_proj(P, rhs_dram, w_dram, out_dram, nt_n, qscale_tiles, tag):
    with P.stage():
        rhs_sb = P.sb(tag + "_rhs", [128, 32 * M], BF16)
        ld = P.sem(tag + "_rld")
        with P.nc.Block() as blk:
            @blk.sync
            def _(e):
                load_rows(e, rhs_sb, rhs_dram, 32, ld)

            @blk.vector
            def _(e):
                e.wait_ge(ld, 16 * 4)
        sc_ap = None
        if qscale_tiles:
            cst = make_consts(P, [SCALE_Q], tag=tag + "_c")
            sc_ap = cst[SCALE_Q]
        emit_proj_f32w(P, w_dram, rhs_sb, out_dram, 32, nt_n,
                       qscale_tiles=qscale_tiles, tag=tag, scale_ap=sc_ap)


def stage_exchange(P, qk_dram, kp_dram, vp_dram, shkv, shfl, magic, tag):
    nc = P.nc
    with P.stage():
        kv_sb = P.sb(tag + "_kv", [128, 2048], BF16)
        fl_sb = P.sb(tag + "_fl", [1, 8], I32)
        fr_sb = P.sb(tag + "_fr", [1, 8], I32)
        ld = P.sem(tag + "_ld")
        wr = P.sem(tag + "_wr")
        ps_ = P.sem(tag + "_ps")
        with nc.Block() as blk:
            @blk.sync
            def _(e):
                e.dma_start(out=kv_sb[:].rearrange("p (kt m) -> p kt m", kt=2),
                            in_=qk_dram[32 * 128:34 * 128, :]
                            .rearrange("(kt p) m -> p kt m", p=128)).then_inc(ld, 16)

            @blk.vector
            def _(e):
                e.memset(fl_sb[:], magic)
                e.wait_ge(ld, 16)
                e.tensor_copy(kv_sb[0:1, 0:1], kv_sb[0:1, 0:1]).then_inc(ld, 1)

            @blk.scalar
            def _(e):
                e.wait_ge(ld, 17)
                for arm in nc.switch_core_id(engines=[e], n=128):
                    s = int(arm) % 2
                    e.dma_start(out=shkv[s], in_=kv_sb[:]).then_inc(wr, 16)
                    e.wait_ge(wr, 16)
                    e.dma_start(out=shfl[s], in_=fl_sb[:]).then_inc(wr, 16)
                    e.wait_ge(wr, 32)
                    with e.register(f"{tag}acc{int(arm)}") as racc, \
                         e.register(f"{tag}fl{int(arm)}") as rflag, \
                         e.register(f"{tag}cd{int(arm)}") as rcond:
                        e.reg_mov(racc, 0)

                        def cond(e=e, racc=racc, rflag=rflag, rcond=rcond, s=s):
                            e.reg_alu(racc, racc, 16, mybir.AluOpType.add)
                            e.dma_start(out=fr_sb[:], in_=shfl[1 - s]).then_inc(ps_, 16)
                            e.wait_ge(ps_, racc)
                            e.load(rflag, fr_sb[0:1, 0:1])
                            e.reg_alu(rcond, rflag, magic, mybir.AluOpType.not_equal)
                            return rcond
                        with e.While(cond):
                            e.nop()
                    e.dma_start(out=kp_dram[:],
                                in_=shkv[1 - s][:, 0:1024]).then_inc(wr, 16)
                    e.dma_start(out=vp_dram[:],
                                in_=shkv[1 - s][:, 1024:2048]).then_inc(wr, 16)
                    e.wait_ge(wr, 64)


def stage_attn(P, qk_dram, kp_dram, vp_dram, bias_dram, id_dram, out_dram, tag):
    with P.stage():
        emit_attn(P, qk_dram, kp_dram, vp_dram, bias_dram, id_dram, out_dram, tag=tag)


def stage_gateup(P, y_dram, pkg, pku, scg, scu, qzg, qzu, h_dram, hsum_dram,
                 sgcm, sucm, nzg, nzu, tag):
    with P.stage():
        y_sb = P.sb(tag + "_y", [128, 32 * M], BF16)
        xsum_sb = P.sb(tag + "_xs", [32, M], BF16)
        stair32 = P.sb(tag + "_s32", [128, 63], BF16)
        xs_es = contextlib.ExitStack()
        ps_xs = xs_es.enter_context(P.nc.psum_tensor(tag + "_pxs", [32, M], F32))
        ld = P.sem(tag + "_yld")
        xs = P.sem(tag + "_xsem")
        with P.nc.Block() as blk:
            @blk.sync
            def _(e):
                load_rows(e, y_sb, y_dram, 32, ld)

            @blk.vector
            def _(e):
                e.memset(stair32[:], 0.0)
                e.drain()
                e.memset(stair32[:, 31:32], 1.0).then_inc(xs, 1)

            @blk.tensor
            def _(e):
                e.wait_ge(xs, 1)
                e.wait_ge(ld, 16 * 4)
                for kt in range(32):
                    for mc in range(2):
                        mi = e.matmul(ps_xs[:, mc * MC:(mc + 1) * MC],
                                      stair32[:, 31 - kt: 63 - kt][:, 0:32],
                                      y_sb[:, kt * M + mc * MC: kt * M + (mc + 1) * MC],
                                      start=(kt == 0), stop=(kt == 31))
                mi.then_inc(xs, 1)

            @blk.scalar
            def _(e):
                e.wait_ge(xs, 2)
                e.copy(xsum_sb[:], ps_xs[:]).then_inc(xs, 1)
        xs_es.close()

        emit_ffn_prep(P, scg, qzg, sgcm, nzg, 32, 43, 1376, tag + "g")
        emit_ffn_prep(P, scu, qzu, sucm, nzu, 32, 43, 1376, tag + "u")
        emit_gateup(P, y_sb, pkg, pku, sgcm, sucm, nzg, nzu, xsum_sb, h_dram,
                    hsum_dram, tag=tag)


def stage_down(P, h_dram, hsum_dram, pkd, scd, qzd, z_dram, sdcm, nzd, tag):
    with P.stage():
        hsum_sb = P.sb(tag + "_hs", [86, M], BF16)
        ld = P.sem(tag + "_hld")
        with P.nc.Block() as blk:
            @blk.sync
            def _(e):
                e.dma_start(out=hsum_sb[:], in_=hsum_dram[:]).then_inc(ld, 16)

            @blk.vector
            def _(e):
                e.wait_ge(ld, 16)
        emit_ffn_prep(P, scd, qzd, sdcm, nzd, 86, 16, 512, tag + "d")
        emit_down(P, h_dram, hsum_sb, pkd, sdcm, nzd, z_dram, tag=tag)


def build_mega():
    P = Prog("mega")
    with P.es:
        nc = P.nc
        x_in = P.din("x_in", [D, M], F32)
        bias = P.din("bias_tab", [128, 9216], F32)
        ident = P.din("ident", [128, 128], BF16)
        x_out = P.dout("x_out", [D, M], F32)

        w = {}
        for L in range(2):
            pre = f"L{L}_"
            w[pre + "wqkv"] = P.din(pre + "wqkv", [D, 4352], F32)
            w[pre + "wo"] = P.din(pre + "wo", [D, D], F32)
            w[pre + "pk_g"] = P.din(pre + "pk_g", [43, 128, 32, 32], I32)
            w[pre + "pk_u"] = P.din(pre + "pk_u", [43, 128, 32, 32], I32)
            w[pre + "pk_d"] = P.din(pre + "pk_d", [16, 128, 86, 32], I32)
            w[pre + "sc_g"] = P.din(pre + "sc_g", [32, II], F32)
            w[pre + "sc_u"] = P.din(pre + "sc_u", [32, II], F32)
            w[pre + "sc_d"] = P.din(pre + "sc_d", [86, D], F32)
            w[pre + "qz_g"] = P.din(pre + "qz_g", [32, 1376], I32)
            w[pre + "qz_u"] = P.din(pre + "qz_u", [32, 1376], I32)
            w[pre + "qz_d"] = P.din(pre + "qz_d", [86, 512], I32)

        x_st = P.dint("x_st", [D, M], F32)
        y_d = P.dint("y_d", [D, M], BF16)
        qk_d = P.dint("qk_d", [4352, M], BF16)
        kp_d = P.dint("kp_d", [128, M], BF16)
        vp_d = P.dint("vp_d", [128, M], BF16)
        at_d = P.dint("at_d", [D, M], BF16)
        z_d = P.dint("z_d", [D, M], BF16)
        h_d = P.dint("h_d", [II, M], BF16)
        hs_d = P.dint("hs_d", [86, M], BF16)
        sgcm = P.dint("sgcm", [43, 32, 256], BF16)
        sucm = P.dint("sucm", [43, 32, 256], BF16)
        sdcm = P.dint("sdcm", [16, 86, 256], BF16)
        nzg = P.dint("nzg", [32, II], BF16)
        nzu = P.dint("nzu", [32, II], BF16)
        nzd = P.dint("nzd", [86, D], BF16)
        shkv = [[P.dint(f"shkv{L}_{s}", [128, 2048], BF16, shared=True)
                 for s in range(2)] for L in range(2)]
        shfl = [[P.dint(f"shfl{L}_{s}", [1, 8], I32, shared=True)
                 for s in range(2)] for L in range(2)]

        for L in range(2):
            pre = f"L{L}_"
            xsrc = x_in if L == 0 else x_st
            stage_preln(P, xsrc, y_d, tag=f"ln1_{L}")
            stage_proj(P, y_d, w[pre + "wqkv"], qk_d, 34, 32, tag=f"qkv_{L}")
            stage_exchange(P, qk_d, kp_d, vp_d, shkv[L], shfl[L],
                           MAGIC0 + L, tag=f"xc_{L}")
            stage_attn(P, qk_d, kp_d, vp_d, bias, ident, at_d, tag=f"at_{L}")
            stage_proj(P, at_d, w[pre + "wo"], z_d, 32, 0, tag=f"wo_{L}")
            stage_postln(P, z_d, xsrc, x_st, tag=f"pl1_{L}")
            stage_preln(P, x_st, y_d, tag=f"ln3_{L}")
            stage_gateup(P, y_d, w[pre + "pk_g"], w[pre + "pk_u"],
                         w[pre + "sc_g"], w[pre + "sc_u"],
                         w[pre + "qz_g"], w[pre + "qz_u"],
                         h_d, hs_d, sgcm, sucm, nzg, nzu, tag=f"gu_{L}")
            stage_down(P, h_d, hs_d, w[pre + "pk_d"], w[pre + "sc_d"],
                       w[pre + "qz_d"], z_d, sdcm, nzd, tag=f"dn_{L}")
            dst = x_st if L == 0 else x_out
            stage_postln(P, z_d, x_st, dst, tag=f"pl2_{L}")
    return P


_MEGA = [None]


def kernel(**inputs):
    sh, pc = prep_inputs(inputs)
    if _MEGA[0] is None:
        _MEGA[0] = build_mega()
    P = _MEGA[0]
    in_maps = []
    for c in range(NCORES):
        m = {"x_in": pc[c]["x_in"], "bias_tab": pc[c]["bias_tab"],
             "ident": pc[c]["ident"]}
        for L in range(2):
            pre = f"L{L}_"
            for k in ["wqkv", "wo", "pk_g", "pk_u", "pk_d", "sc_g", "sc_u",
                      "sc_d", "qz_g", "qz_u", "qz_d"]:
                m[pre + k] = sh[pre + k]
        in_maps.append(m)
    res = run_bass_kernel_spmd(P.nc, in_maps, core_ids=list(range(NCORES)))
    return unprep_output([res.results[c]["x_out"] for c in range(NCORES)])


# version 11 (final)
# speedup vs baseline: 1.2121x; 1.0436x over previous
import math
import contextlib
import numpy as np
import ml_dtypes

import concourse.bass as bass
import concourse.mybir as mybir
from concourse.bass_utils import run_bass_kernel_spmd

AF = mybir.ActivationFunctionType
ALU = mybir.AluOpType
F32 = mybir.dt.float32
BF16 = mybir.dt.bfloat16
U8 = mybir.dt.uint8
I32 = mybir.dt.int32
BF = ml_dtypes.bfloat16

D = 4096
II = 11008
NH = 32
HD = 128
S = 2048
BATCH = 4
M = 1024
MC = 512
NCORES = 8
NEG = -60000.0
SCALE_Q = float(HD) ** -0.5
EPS = 1e-5
MASK32 = 0x0F0F0F0F
TAU = [0, 2, 4, 6, 1, 3, 5, 7]


def _pi(n):
    q = np.arange(n)
    return 8 * (q // 8) + np.array(TAU)[q % 8]


PI_D = _pi(D)
PI_I = _pi(II)
SLOPES = [(2.0 ** (-2.0 ** (-(math.log2(NH) - 3)))) ** (i + 1) for i in range(NH)]



def _qtiles(parity):
    return [2 * t + parity for t in range(8)]


def _bias_table(parity):
    out = np.full((128, 9216), NEG, np.float32)
    myt = _qtiles(parity)
    pat = _qtiles(1 - parity)
    for t in range(8):
        off = 128 * t * (t + 1)
        w = (t + 1) * 128
        posq = myt[t] * 128 + np.arange(128)
        for j in range(t + 1):
            dA = (myt[j] * 128 + np.arange(128))[None, :] - posq[:, None]
            out[:, off + j * 128:off + (j + 1) * 128] = np.where(dA <= 0, dA, NEG)
            dB = (pat[j] * 128 + np.arange(128))[None, :] - posq[:, None]
            out[:, off + w + j * 128:off + w + (j + 1) * 128] = np.where(dB <= 0, dB, NEG)
    return out


def _pretile_packed(qw, kt):
    r, w = qw.shape
    assert r == kt * 128 and w % 32 == 0
    nchunks = w // 32
    a = qw.reshape(kt, 128, nchunks, 32)
    return np.ascontiguousarray(a.transpose(2, 1, 0, 3))


def prep_inputs(inp):
    pc = [{} for _ in range(NCORES)]
    x = np.asarray(inp["L0_attn_hidden_states"])
    ident = np.eye(128, dtype=BF)
    for c in range(NCORES):
        p, j = c // 2, c % 2
        toks = np.concatenate([np.arange(g * 128, (g + 1) * 128) for g in _qtiles(j)])
        xc = x[p][toks][:, PI_D]
        pc[c]["x_in"] = np.ascontiguousarray(xc.T.astype(np.float32))
        pc[c]["bias_tab"] = np.ascontiguousarray(_bias_table(j))
        pc[c]["ident"] = ident

    sh = {}
    for L in range(2):
        pre = f"L{L}_"
        if L == 0:
            wqkv = np.asarray(inp["L0_attn_Wqkv"])
            wo = np.asarray(inp["L0_attn_Wo"])
        else:
            wqkv = np.concatenate(
                [np.asarray(inp["L1_attn_Wq"]), np.asarray(inp["L1_attn_Wkv"])], axis=1)
            wo = np.asarray(inp["L1_attn_Wo"])
        sh[pre + "wqkv"] = np.ascontiguousarray(wqkv[PI_D, :].astype(np.float32))
        sh[pre + "wo"] = np.ascontiguousarray(wo[:, PI_D].astype(np.float32))
        f = f"L{L}_ffn_"
        sh[pre + "pk_g"] = _pretile_packed(np.asarray(inp[f + "gate_qweight"])[PI_D, :], 32)
        sh[pre + "pk_u"] = _pretile_packed(np.asarray(inp[f + "up_qweight"])[PI_D, :], 32)
        sh[pre + "pk_d"] = _pretile_packed(np.asarray(inp[f + "down_qweight"])[PI_I, :], 86)
        sh[pre + "sc_g"] = np.ascontiguousarray(np.asarray(inp[f + "gate_scales"])[:, PI_I].astype(np.float32))
        sh[pre + "sc_u"] = np.ascontiguousarray(np.asarray(inp[f + "up_scales"])[:, PI_I].astype(np.float32))
        sh[pre + "sc_d"] = np.ascontiguousarray(np.asarray(inp[f + "down_scales"])[:, PI_D].astype(np.float32))
        sh[pre + "qz_g"] = np.ascontiguousarray(np.asarray(inp[f + "gate_qzeros"], np.int32))
        sh[pre + "qz_u"] = np.ascontiguousarray(np.asarray(inp[f + "up_qzeros"], np.int32))
        sh[pre + "qz_d"] = np.ascontiguousarray(np.asarray(inp[f + "down_qzeros"], np.int32))
    return sh, pc


def unprep_output(outs_by_core):
    res = np.empty((BATCH, S, D), np.float32)
    inv = np.argsort(PI_D)
    for c in range(NCORES):
        p, j = c // 2, c % 2
        toks = np.concatenate([np.arange(g * 128, (g + 1) * 128) for g in _qtiles(j)])
        res[p][toks] = np.asarray(outs_by_core[c], np.float32).T[:, inv]
    return res



class Prog:
    def __init__(self, name):
        self.nc = bass.Bass("TRN2", target_bir_lowering=False)
        self.es = contextlib.ExitStack()
        self.cur = self.es
        self.name = name

    def sb(self, name, shape, dt):
        return self.cur.enter_context(self.nc.sbuf_tensor(name, shape, dt))

    def ps(self, name, shape, dt=F32):
        return self.cur.enter_context(self.nc.psum_tensor(name, shape, dt))

    def sem(self, name):
        return self.cur.enter_context(self.nc.semaphore(name))

    @contextlib.contextmanager
    def stage(self):
        old = self.cur
        with contextlib.ExitStack() as st_es:
            self.cur = st_es
            yield
        self.cur = old
        self.nc.reset()

    def din(self, name, shape, dt):
        return self.nc.dram_tensor(name, list(shape), dt, kind="ExternalInput").ap()

    def dout(self, name, shape, dt):
        return self.nc.dram_tensor(name, list(shape), dt, kind="ExternalOutput").ap()

    def dint(self, name, shape, dt, shared=False):
        kw = {"addr_space": "Shared"} if shared else {}
        return self.nc.dram_tensor(name, list(shape), dt, kind="Internal", **kw).ap()


def make_consts(P, values, tag="cst"):
    tiles = {v: P.sb(f"{tag}_{i}", [128, 1], F32) for i, v in enumerate(values)}
    with P.nc.Block() as blk:
        @blk.vector
        def _(e):
            for v, t in tiles.items():
                e.memset(t[:], float(v))
    return tiles


def load_rows(e, dst_sb, src_dram, kt_n, sem, grp=8):
    n = 0
    for k0 in range(0, kt_n, grp):
        k1 = min(kt_n, k0 + grp)
        e.dma_start(
            out=dst_sb[:, k0 * M:k1 * M],
            in_=src_dram.rearrange("(kt p) m -> p kt m", p=128)[:, k0:k1, :],
        ).then_inc(sem, 16)
        n += 1
    return n



def emit_ln(P, src_dram, dst_dram, res_dram=None, tag="ln"):
    nc = P.nc
    kt_n = 32
    src_is_bf = (src_dram.dtype == BF16)
    inv_n = 1.0 / float(kt_n * 128)
    out_f32 = res_dram is not None

    x_sb = P.sb(tag + "_x", [128, kt_n * M], BF16 if src_is_bf else F32)
    xb_sb = None if src_is_bf else P.sb(tag + "_xb", [128, 2, M], BF16)
    sq_sb = P.sb(tag + "_sq", [128, 2, M], BF16)
    ones_col = P.sb(tag + "_oc", [128, 1], BF16)
    ones_row = P.sb(tag + "_or", [1, 128], F32)
    vq = P.sb(tag + "_vq", [1, M], F32)
    vmean = P.sb(tag + "_vm", [1, M], F32)
    vrstd = P.sb(tag + "_vr", [1, M], F32)
    vmr = P.sb(tag + "_vmr", [1, M], F32)
    vtmp = P.sb(tag + "_vt", [1, M], F32)
    rstd_rep = P.sb(tag + "_rr", [128, M], F32)
    mr_rep = P.sb(tag + "_mrr", [128, M], F32)
    res_sb = P.sb(tag + "_res", [128, 2, M], F32) if res_dram is not None else None
    out_sb = P.sb(tag + "_out", [128, 2, M], F32 if out_f32 else BF16)
    tmp_sb = P.sb(tag + "_tmp", [128, M], F32)

    ps_s = P.ps(tag + "_ps", [1, M])
    ps_q = P.ps(tag + "_pq", [1, M])
    ps_r1 = P.ps(tag + "_pr1", [128, M])
    ps_r2 = P.ps(tag + "_pr2", [128, M])

    ld = P.sem(tag + "_ld")
    cv = P.sem(tag + "_cv")
    cq = P.sem(tag + "_cq")
    mm = P.sem(tag + "_mm")
    ev = P.sem(tag + "_ev")
    st = P.sem(tag + "_st")
    rl = P.sem(tag + "_rl")
    nv = P.sem(tag + "_nv")

    with nc.Block() as blk:
        @blk.sync
        def _(e):
            load_rows(e, x_sb, src_dram, kt_n, ld)

        if not src_is_bf:
            @blk.vector
            def _(e):
                e.memset(ones_col[:], 1.0)
                e.memset(ones_row[:], 1.0)
                for kt in range(kt_n):
                    e.wait_ge(ld, 16 * (kt // 8 + 1))
                    if kt >= 2:
                        e.wait_ge(mm, kt - 1)
                    e.tensor_copy(xb_sb[:, kt % 2, :],
                                  x_sb[:, kt * M:(kt + 1) * M]).then_inc(cv, 1)
        else:
            @blk.vector
            def _(e):
                e.memset(ones_col[:], 1.0)
                e.memset(ones_row[:], 1.0).then_inc(cv, 1)

        @blk.scalar
        def _(e):
            for kt in range(kt_n):
                e.wait_ge(ld, 16 * (kt // 8 + 1))
                if kt >= 2:
                    e.wait_ge(mm, kt - 1)
                e.activation(out=sq_sb[:, kt % 2, :], in_=x_sb[:, kt * M:(kt + 1) * M],
                             func=AF.Square).then_inc(cq, 1)

        @blk.tensor
        def _(e):
            for kt in range(kt_n):
                e.wait_ge(cv, 1 if src_is_bf else kt + 1)
                e.wait_ge(cq, kt + 1)
                xsrc = (x_sb[:, kt * M:(kt + 1) * M] if src_is_bf
                        else xb_sb[:, kt % 2, :])
                for mc in range(2):
                    e.matmul(ps_s[:, mc * MC:(mc + 1) * MC], ones_col[:],
                             xsrc[:, mc * MC:(mc + 1) * MC],
                             start=(kt == 0), stop=(kt == kt_n - 1))
                    i2 = e.matmul(ps_q[:, mc * MC:(mc + 1) * MC], ones_col[:],
                                  sq_sb[:, kt % 2, mc * MC:(mc + 1) * MC],
                                  start=(kt == 0), stop=(kt == kt_n - 1))
                i2.then_inc(mm, 1)

    with nc.Block() as blk:
        @blk.scalar
        def _(e):
            e.copy(vmean[:], ps_s[:])
            e.copy(vq[:], ps_q[:]).then_inc(ev, 1)
            e.wait_ge(ev, 2)
            e.activation(out=vq[:], in_=vq[:], func=AF.Sqrt).then_inc(ev, 1)

        @blk.vector
        def _(e):
            e.wait_ge(ev, 1)
            e.tensor_scalar_mul(vmean[:], vmean[:], inv_n)
            e.tensor_scalar_mul(vq[:], vq[:], inv_n)
            e.drain()
            e.tensor_tensor(out=vtmp[:], in0=vmean[:], in1=vmean[:], op=ALU.mult)
            e.drain()
            e.tensor_sub(vq[:], vq[:], vtmp[:])
            e.drain()
            e.tensor_scalar_add(vq[:], vq[:], EPS).then_inc(ev, 1)
            e.wait_ge(ev, 3)
            e.reciprocal(vrstd[:], vq[:])
            e.drain()
            e.tensor_mul(vmr[:], vmean[:], vrstd[:]).then_inc(ev, 1)

        @blk.tensor
        def _(e):
            e.wait_ge(ev, 4)
            for mc in range(2):
                e.matmul(ps_r1[:, mc * MC:(mc + 1) * MC], ones_row[:],
                         vrstd[:, mc * MC:(mc + 1) * MC], start=True, stop=True)
                i2 = e.matmul(ps_r2[:, mc * MC:(mc + 1) * MC], ones_row[:],
                              vmr[:, mc * MC:(mc + 1) * MC], start=True, stop=True)
            i2.then_inc(mm, 1)

    with nc.Block() as blk:
        @blk.scalar
        def _(e):
            e.copy(rstd_rep[:], ps_r1[:])
            e.copy(mr_rep[:], ps_r2[:]).then_inc(ev, 1)

        @blk.sync
        def _(e):
            if res_dram is not None:
                for kt in range(kt_n):
                    if kt >= 2:
                        e.wait_ge(nv, kt - 1)
                    e.dma_start(out=res_sb[:, kt % 2, :],
                                in_=res_dram[kt * 128:(kt + 1) * 128, :]).then_inc(rl, 16)

        @blk.vector
        def _(e):
            e.wait_ge(ev, 5)
            for kt in range(kt_n):
                xt = x_sb[:, kt * M:(kt + 1) * M]
                if kt >= 2:
                    e.wait_ge(st, 16 * (kt - 1))
                e.tensor_mul(tmp_sb[:], xt, rstd_rep[:])
                e.drain()
                if res_dram is not None:
                    e.wait_ge(rl, 16 * (kt + 1))
                    e.tensor_sub(tmp_sb[:], tmp_sb[:], mr_rep[:])
                    e.drain()
                    e.tensor_add(out_sb[:, kt % 2, :], tmp_sb[:],
                                 res_sb[:, kt % 2, :]).then_inc(nv, 1)
                else:
                    e.tensor_sub(out_sb[:, kt % 2, :], tmp_sb[:],
                                 mr_rep[:]).then_inc(nv, 1)

        @blk.gpsimd
        def _(e):
            for kt in range(kt_n):
                e.wait_ge(nv, kt + 1)
                e.dma_start(out=dst_dram[kt * 128:(kt + 1) * 128, :],
                            in_=out_sb[:, kt % 2, :]).then_inc(st, 16)
            e.wait_ge(st, 16 * kt_n)



def emit_proj_f32w(P, w_dram, rhs_sb, out_dram, kt_n, nt_n, qscale_tiles=0, tag="pj",
                   scale_ap=None):
    nc = P.nc
    NCB = 256
    ncb_n = nt_n * 128 // NCB

    wland = P.sb(tag + "_wl", [128, 2 * NCB], F32)
    w_sb = P.sb(tag + "_w", [128, 2, kt_n * NCB], BF16)
    out_sb = P.sb(tag + "_o", [128, 4, MC], BF16)
    psum = [P.ps(tag + f"_p{i}", [128, MC]) for i in range(8)]

    wld = [P.sem(tag + f"_wld{i}") for i in range(2)]
    wcv = P.sem(tag + "_wcv")
    mmg = P.sem(tag + "_mmg")
    ev = P.sem(tag + "_ev")
    st = [P.sem(tag + f"_st{i}") for i in range(4)]

    G_N = ncb_n * 4

    with nc.Block() as blk:
        @blk.sync
        def _(e):
            for c in range(ncb_n):
                for kt in range(kt_n):
                    i = c * kt_n + kt
                    if i >= 2:
                        e.wait_ge(wcv, i - 1)
                    e.dma_start(
                        out=wland[:, (i % 2) * NCB:(i % 2 + 1) * NCB],
                        in_=w_dram[kt * 128:(kt + 1) * 128, c * NCB:(c + 1) * NCB],
                    ).then_inc(wld[i % 2], 16)

        @blk.vector
        def _(e):
            for c in range(ncb_n):
                buf = c % 2
                if c >= 2:
                    e.wait_ge(mmg, (c - 1) * 4)
                for kt in range(kt_n):
                    i = c * kt_n + kt
                    e.wait_ge(wld[i % 2], 16 * (i // 2 + 1))
                    e.tensor_copy(w_sb[:, buf, kt * NCB:(kt + 1) * NCB],
                                  wland[:, (i % 2) * NCB:(i % 2 + 1) * NCB]).then_inc(wcv, 1)

        @blk.tensor
        def _(e):
            for c in range(ncb_n):
                buf = c % 2
                for mc in range(2):
                    for nt in range(2):
                        G = c * 4 + mc * 2 + nt
                        bank = G % 8
                        if G >= 8:
                            e.wait_ge(ev, G - 7)
                        for kt in range(kt_n):
                            if mc == 0 and nt == 0:
                                e.wait_ge(wcv, c * kt_n + kt + 1)
                            mi = e.matmul(
                                psum[bank][:],
                                w_sb[:, buf, kt * NCB + nt * 128: kt * NCB + (nt + 1) * 128],
                                rhs_sb[:, kt * M + mc * MC: kt * M + (mc + 1) * MC],
                                start=(kt == 0), stop=(kt == kt_n - 1))
                        mi.then_inc(mmg, 1)

        @blk.scalar
        def _(e):
            for c in range(ncb_n):
                for mc in range(2):
                    for nt in range(2):
                        G = c * 4 + mc * 2 + nt
                        bank = G % 8
                        e.wait_ge(mmg, G + 1)
                        if G >= 4:
                            e.wait_ge(st[G % 4], 16 * (G // 4))
                        if (2 * c + nt) < qscale_tiles:
                            e.activation(out=out_sb[:, G % 4, :], in_=psum[bank][:],
                                         func=AF.Copy, scale=scale_ap[:]).then_inc(ev, 1)
                        else:
                            e.copy(out_sb[:, G % 4, :], psum[bank][:]).then_inc(ev, 1)

        @blk.gpsimd
        def _(e):
            for c in range(ncb_n):
                for mc in range(2):
                    for nt in range(2):
                        G = c * 4 + mc * 2 + nt
                        e.wait_ge(ev, G + 1)
                        r0 = (2 * c + nt) * 128
                        e.dma_start(out=out_dram[r0:r0 + 128, mc * MC:(mc + 1) * MC],
                                    in_=out_sb[:, G % 4, :]).then_inc(st[G % 4], 16)
            for i in range(4):
                e.wait_ge(st[i], 16 * (G_N // 4 + (1 if i < G_N % 4 else 0)))


def build_ln_prog(post):
    P = Prog("postln" if post else "preln")
    with P.es:
        if post:
            z = P.din("z_in", [D, M], BF16)
            res = P.din("res_in", [D, M], F32)
            xo = P.dout("x_out", [D, M], F32)
            emit_ln(P, z, xo, res_dram=res)
        else:
            x = P.din("x_in", [D, M], F32)
            y = P.dout("y_out", [D, M], BF16)
            emit_ln(P, x, y)
    return P


def build_qkv_prog():
    P = Prog("qkv")
    with P.es:
        y = P.din("y_in", [D, M], BF16)
        w = P.din("w_in", [D, 4352], F32)
        qk = P.dout("qk_out", [4352, M], BF16)
        y_sb = P.sb("y", [128, 32 * M], BF16)
        ld = P.sem("yld")
        with P.nc.Block() as blk:
            @blk.sync
            def _(e):
                load_rows(e, y_sb, y, 32, ld)

            @blk.vector
            def _(e):
                e.wait_ge(ld, 16 * 4)
        cst = make_consts(P, [SCALE_Q], tag="qc")
        emit_proj_f32w(P, w, y_sb, qk, 32, 34, qscale_tiles=32, scale_ap=cst[SCALE_Q])
    return P


def build_wo_prog():
    P = Prog("wo")
    with P.es:
        at = P.din("at_in", [D, M], BF16)
        w = P.din("w_in", [D, D], F32)
        z = P.dout("z_out", [D, M], BF16)
        a_sb = P.sb("a", [128, 32 * M], BF16)
        ld = P.sem("ald")
        with P.nc.Block() as blk:
            @blk.sync
            def _(e):
                load_rows(e, a_sb, at, 32, ld)

            @blk.vector
            def _(e):
                e.wait_ge(ld, 16 * 4)
        emit_proj_f32w(P, w, a_sb, z, 32, 32)
    return P



OFF_T = [128 * t * (t + 1) for t in range(8)]


def emit_attn(P, qk_dram, kp_dram, vp_dram, bias_dram, id_dram, out_dram, tag="at"):
    nc = P.nc
    qk_sb = P.sb(tag + "_qk", [128, 34 * M], BF16)
    kp_sb = P.sb(tag + "_kp", [128, M], BF16)
    vp_sb = P.sb(tag + "_vp", [128, M], BF16)
    id_sb = P.sb(tag + "_id", [128, 128], BF16)
    vt_sb = P.sb(tag + "_vt", [128, 16 * 128], BF16)
    btab = P.sb(tag + "_bt", [128, 9216], F32)
    srow = P.sb(tag + "_sr", [128, 4, 2048], F32)
    probs = P.sb(tag + "_pb", [128, 4, 2048], BF16)
    ptb = P.sb(tag + "_pt", [128, 4, 2048], BF16)
    mA = P.sb(tag + "_mA", [128, 1], F32)
    negm = P.sb(tag + "_nm", [128, 4, 1], F32)
    sA = P.sb(tag + "_sA", [128, 4, 1], F32)
    rsum = P.sb(tag + "_rs", [128, 1], F32)
    out_sb = P.sb(tag + "_ao", [128, 4, 128], BF16)

    ps_sc = P.ps(tag + "_psc", [128, 2048])
    ps_tp = P.ps(tag + "_ptp", [128, 2048], BF16)
    ps_pv = P.ps(tag + "_ppv", [128, 128])

    ld = P.sem(tag + "_ld")
    s_mm = P.sem(tag + "_smm")
    s_sr = P.sem(tag + "_ssr")
    s_ex = P.sem(tag + "_sex")
    s_no = P.sem(tag + "_sno")
    s_tp = P.sem(tag + "_stp")
    s_pt = P.sem(tag + "_spt")
    s_pv = P.sem(tag + "_spv")
    s_ae = P.sem(tag + "_sae")
    s_ost = [P.sem(tag + f"_so{i}") for i in range(4)]

    with nc.Block() as blk:
        @blk.sync
        def _(e):
            e.dma_start(out=id_sb[:], in_=id_dram[:]).then_inc(ld, 16)
            e.dma_start(out=qk_sb[:, 32 * M:34 * M],
                        in_=qk_dram.rearrange("(kt p) m -> p kt m", p=128)[:, 32:34, :]
                        ).then_inc(ld, 16)
            e.dma_start(out=kp_sb[:], in_=kp_dram[:]).then_inc(ld, 16)
            e.dma_start(out=vp_sb[:], in_=vp_dram[:]).then_inc(ld, 16)
            e.dma_start(out=btab[:], in_=bias_dram[:]).then_inc(ld, 16)
            for g in range(4):
                e.dma_start(out=qk_sb[:, g * 8 * M:(g + 1) * 8 * M],
                            in_=qk_dram.rearrange("(kt p) m -> p kt m", p=128)
                            [:, g * 8:(g + 1) * 8, :]).then_inc(ld, 16)

        @blk.tensor
        def _(e):
            e.wait_ge(ld, 16 * 4)
            mi = None
            for j in range(8):
                mi = e.transpose(ps_tp[:, j * 128:(j + 1) * 128],
                                 qk_sb[:, 33 * M + j * 128: 33 * M + (j + 1) * 128], id_sb[:])
            mi.then_inc(s_tp, 1)
            for j in range(8):
                mi = e.transpose(ps_tp[:, 1024 + j * 128: 1024 + (j + 1) * 128],
                                 vp_sb[:, j * 128:(j + 1) * 128], id_sb[:])
            mi.then_inc(s_tp, 1)

        @blk.scalar
        def _(e):
            e.wait_ge(s_tp, 1)
            e.copy(vt_sb[:, 0:1024], ps_tp[:, 0:1024])
            e.wait_ge(s_tp, 2)
            e.copy(vt_sb[:, 1024:2048], ps_tp[:, 1024:2048]).then_inc(s_tp, 1)

    with nc.Block() as blk:
        def emit_transp(e, j):
            tj = j % 8
            e.wait_ge(s_no, j + 1)
            if j > 0:
                e.wait_ge(s_pt, j)
            pb = probs[:, j % 4, :]
            mi = None
            for jj in range(2 * (tj + 1)):
                mi = e.transpose(ps_tp[:, jj * 128:(jj + 1) * 128],
                                 pb[:, jj * 128:(jj + 1) * 128], id_sb[:])
            mi.then_inc(s_tp, 1)

        def emit_pv(e, j):
            tj = j % 8
            e.wait_ge(s_pt, j + 1)
            if j > 0:
                e.wait_ge(s_ae, j)
            pt = ptb[:, j % 4, :]
            for jj in range(tj + 1):
                e.matmul(ps_pv[:], vt_sb[:, jj * 128:(jj + 1) * 128],
                         pt[:, jj * 128:(jj + 1) * 128],
                         start=(jj == 0), stop=False)
                mi = e.matmul(ps_pv[:], vt_sb[:, 1024 + jj * 128:1024 + (jj + 1) * 128],
                              pt[:, (tj + 1 + jj) * 128:(tj + 2 + jj) * 128],
                              start=False, stop=(jj == tj))
            mi.then_inc(s_pv, 1)

        @blk.tensor
        def _(e):
            for k in range(260):
                if k <= 255:
                    h, t = divmod(k, 8)
                    w = (t + 1) * 128
                    if t == 0 and h % 8 == 0:
                        e.wait_ge(ld, 16 * (6 + h // 8))
                    if k > 0:
                        e.wait_ge(s_sr, k)
                    q_ap = qk_sb[:, h * M + t * 128: h * M + (t + 1) * 128]
                    for c0 in range(0, w, MC):
                        cw = min(MC, w - c0)
                        e.matmul(ps_sc[:, c0:c0 + cw], q_ap,
                                 qk_sb[:, 32 * M + c0: 32 * M + c0 + cw],
                                 start=True, stop=True)
                    a = w
                    while a < 2 * w:
                        b = min(2 * w, (a // MC + 1) * MC)
                        mi = e.matmul(ps_sc[:, a:b], q_ap,
                                      kp_sb[:, a - w: b - w], start=True, stop=True)
                        a = b
                    mi.then_inc(s_mm, 1)
                if 2 <= k <= 257:
                    emit_transp(e, k - 2)
                if 4 <= k <= 259:
                    emit_pv(e, k - 4)

        def sm_front(e, it):
            h, t = divmod(it, 8)
            w = (t + 1) * 128
            e.wait_ge(s_mm, it + 1)
            if it == 0:
                e.wait_ge(ld, 16 * 5)
            sr = srow[:, it % 4, :]
            off = OFF_T[t]
            sl = float(SLOPES[h])
            e.scalar_tensor_tensor(out=sr[:, 0:2 * w], in0=btab[:, off:off + 2 * w],
                                   scalar=sl, in1=ps_sc[:, 0:2 * w],
                                   op0=ALU.mult, op1=ALU.add)
            e.drain()
            e.reduce_max(mA[:], sr[:, 0:2 * w], axis=mybir.AxisListType.X)
            e.drain()
            e.tensor_scalar_mul(negm[:, it % 4, :], mA[:], -1.0).then_inc(s_sr, 1)

        def sm_back(e, it):
            h, t = divmod(it, 8)
            w = (t + 1) * 128
            e.wait_ge(s_ex, it + 1)
            e.reciprocal(rsum[:], sA[:, it % 4, :])
            e.drain()
            pb = probs[:, it % 4, :]
            e.tensor_scalar(out=pb[:, 0:2 * w], in0=pb[:, 0:2 * w],
                            scalar1=rsum[:], scalar2=None, op0=ALU.mult).then_inc(s_no, 1)

        @blk.vector
        def _(e):
            for k in range(258):
                if k >= 2:
                    sm_back(e, k - 2)
                if k <= 255:
                    sm_front(e, k)

        @blk.scalar
        def _(e):
            for k in range(260):
                if k <= 255:
                    it = k
                    h, t = divmod(it, 8)
                    w = (t + 1) * 128
                    e.wait_ge(s_sr, it + 1)
                    if it >= 4:
                        e.wait_ge(s_tp, it)
                        e.wait_ge(s_no, it - 3)
                    pb = probs[:, it % 4, :]
                    sr = srow[:, it % 4, :]
                    e.activation(out=pb[:, 0:2 * w], in_=sr[:, 0:2 * w], func=AF.Exp,
                                 bias=negm[:, it % 4, :],
                                 accum_out=sA[:, it % 4, :]).then_inc(s_ex, 1)
                if 2 <= k <= 257:
                    j = k - 2
                    tj = j % 8
                    wj2 = 2 * (tj + 1) * 128
                    e.wait_ge(s_tp, 3 + j + 1)
                    if j >= 4:
                        e.wait_ge(s_pv, j - 3)
                    pt = ptb[:, j % 4, :]
                    e.copy(pt[:, 0:wj2], ps_tp[:, 0:wj2]).then_inc(s_pt, 1)
                if 4 <= k <= 259:
                    j = k - 4
                    e.wait_ge(s_pv, j + 1)
                    if j >= 4:
                        e.wait_ge(s_ost[j % 4], 16 * (j // 4))
                    e.copy(out_sb[:, j % 4, :], ps_pv[:]).then_inc(s_ae, 1)

        @blk.gpsimd
        def _(e):
            for j in range(256):
                hj, tj = divmod(j, 8)
                e.wait_ge(s_ae, j + 1)
                e.dma_start(out=out_dram[hj * 128:(hj + 1) * 128,
                                         tj * 128:(tj + 1) * 128],
                            in_=out_sb[:, j % 4, :]).then_inc(s_ost[j % 4], 16)
            for i in range(4):
                e.wait_ge(s_ost[i], 16 * 64)


def build_attn_prog():
    P = Prog("attn")
    with P.es:
        qk = P.din("qk_in", [4352, M], BF16)
        kp = P.din("kp_in", [128, M], BF16)
        vp = P.din("vp_in", [128, M], BF16)
        bias = P.din("bias_tab", [128, 9216], F32)
        ident = P.din("ident", [128, 128], BF16)
        ao = P.dout("at_out", [D, M], BF16)
        emit_attn(P, qk, kp, vp, bias, ident, ao)
    return P



def emit_ffn_prep(P, sc_dram, qz_dram, cm_dram, nzs_dram, groups, nchunks, words, tag):
    nc = P.nc
    cols = nchunks * 256
    with contextlib.ExitStack() as tes:
        s_sb = tes.enter_context(nc.sbuf_tensor(tag + "_s", [groups, cols], F32))
        sb_bf = tes.enter_context(nc.sbuf_tensor(tag + "_sb", [groups, cols], BF16))
        qz_sb = tes.enter_context(nc.sbuf_tensor(tag + "_qz", [groups, words], I32))
        hi32 = tes.enter_context(nc.sbuf_tensor(tag + "_hi", [groups, words], I32))
        nzs = tes.enter_context(nc.sbuf_tensor(tag + "_z", [groups, cols], BF16))
        ld = P.sem(tag + "_ld")
        dn = P.sem(tag + "_dn")
        with nc.Block() as blk:
            @blk.sync
            def _(e):
                e.dma_start(out=s_sb[:], in_=sc_dram[:]).then_inc(ld, 16)
                e.dma_start(out=qz_sb[:], in_=qz_dram[:]).then_inc(ld, 16)

            @blk.vector
            def _(e):
                e.wait_ge(ld, 32)
                e.tensor_copy(sb_bf[:], s_sb[:]).then_inc(dn, 1)
                e.drain()
                e.tensor_scalar(out=hi32[:], in0=qz_sb[:], scalar1=4, scalar2=MASK32,
                                op0=ALU.logical_shift_right, op1=ALU.bitwise_and)
                e.tensor_single_scalar(out=qz_sb[:], in_=qz_sb[:], scalar=MASK32,
                                       op=ALU.bitwise_and)
                e.drain()
                e.tensor_tensor(out=nzs[:].rearrange("g (w two) -> g w two", two=2)[:, :, 0],
                                in0=qz_sb[:].bitcast(U8),
                                in1=sb_bf[:].rearrange("g (w two) -> g w two", two=2)[:, :, 0],
                                op=ALU.mult)
                e.tensor_tensor(out=nzs[:].rearrange("g (w two) -> g w two", two=2)[:, :, 1],
                                in0=hi32[:].bitcast(U8),
                                in1=sb_bf[:].rearrange("g (w two) -> g w two", two=2)[:, :, 1],
                                op=ALU.mult)
                e.drain()
                e.tensor_scalar_mul(nzs[:], nzs[:], -1.0).then_inc(dn, 1)

            @blk.gpsimd
            def _(e):
                e.wait_ge(dn, 1)
                e.dma_start(
                    out=cm_dram.rearrange("nc g w -> g nc w"),
                    in_=sb_bf[:].rearrange("g (nc w) -> g nc w", w=256),
                ).then_inc(dn, 16)
                e.wait_ge(dn, 18)
                e.dma_start(out=nzs_dram[:], in_=nzs[:]).then_inc(dn, 16)
                e.wait_ge(dn, 34)


def emit_gateup(P, y_sb, pkg_dram, pku_dram, sgcm_dram, sucm_dram, nzg_dram, nzu_dram,
                xsum_sb, h_dram, hsum_dram, tag="gu"):
    nc = P.nc
    NCH = 43
    kt_n = 32

    pk = {m: P.sb(tag + f"_pk{m}", [128, 2, 1024], I32) for m in "gu"}
    hi = {m: P.sb(tag + f"_hi{m}", [128, 1024], I32) for m in "gu"}
    nz = {m: P.sb(tag + f"_nz{m}", [32, 2, 256], BF16) for m in "gu"}
    srep = {m: P.sb(tag + f"_sr{m}", [128, 8192], BF16) for m in "gu"}
    qs = {m: P.sb(tag + f"_qs{m}", [128, 8192], BF16) for m in "gu"}
    hg = P.sb(tag + "_hg", [128, 2, MC], F32)
    hu = P.sb(tag + "_hu", [128, 2, MC], BF16)
    htile = P.sb(tag + "_ht", [128, 4, MC], BF16)
    stair = P.sb(tag + "_stair", [128, 171], BF16)
    hsum_sb = P.sb(tag + "_hs", [86, M], BF16)

    ps_g = [P.ps(tag + f"_pg{i}", [128, MC]) for i in range(4)]
    ps_u = [P.ps(tag + f"_pu{i}", [128, MC]) for i in range(2)]
    ps_hs = P.ps(tag + "_phs", [86, M])

    s_pk = [P.sem(tag + f"_spk{i}") for i in range(2)]
    s_srg = P.sem(tag + "_ssrg")
    s_sru = P.sem(tag + "_ssru")
    s_dqg = P.sem(tag + "_sdqg")
    s_dqu = P.sem(tag + "_sdqu")
    s_peg = P.sem(tag + "_speg")
    s_peu = P.sem(tag + "_speu")
    s_sig = P.sem(tag + "_ssig")
    s_cu = P.sem(tag + "_scu")
    s_ht = P.sem(tag + "_sht")
    s_hsm = P.sem(tag + "_shsm")
    s_hst = [P.sem(tag + f"_shst{i}") for i in range(4)]
    s_fin = P.sem(tag + "_sfin")

    with nc.Block() as blk:
        @blk.vector
        def _(e):
            e.memset(stair[:], 0.0)
            e.drain()
            e.memset(stair[:, 85:86], 1.0)
            e.drain()
            for c in range(NCH):
                e.wait_ge(s_srg, 16 * (c + 1))
                e.wait_ge(s_pk[c % 2], 64 * (c // 2 + 1))
                if c >= 1:
                    e.wait_ge(s_peg, 4 * c)
                for m, s_sr_sem, s_dq_sem, s_pe_sem in (
                        ("g", s_srg, s_dqg, s_peg), ("u", s_sru, s_dqu, s_peu)):
                    if m == "u":
                        e.wait_ge(s_sru, 16 * (c + 1))
                        if c >= 1:
                            e.wait_ge(s_peu, 4 * c)
                    e.tensor_scalar(out=hi[m][:], in0=pk[m][:, c % 2, :], scalar1=4,
                                    scalar2=MASK32, op0=ALU.logical_shift_right,
                                    op1=ALU.bitwise_and)
                    e.drain()
                    e.tensor_single_scalar(out=pk[m][:, c % 2, :], in_=pk[m][:, c % 2, :],
                                           scalar=MASK32, op=ALU.bitwise_and)
                    e.drain()
                    qv = qs[m][:].rearrange("p (kt i two) -> p kt i two", kt=32, two=2)
                    sv = srep[m][:].rearrange("p (kt i two) -> p kt i two", kt=32, two=2)
                    e.tensor_tensor(out=qv[:, :, :, 0],
                                    in0=pk[m][:, c % 2, :].bitcast(U8).rearrange(
                                        "p (kt i) -> p kt i", kt=32),
                                    in1=sv[:, :, :, 0], op=ALU.mult)
                    e.tensor_tensor(out=qv[:, :, :, 1],
                                    in0=hi[m][:].bitcast(U8).rearrange(
                                        "p (kt i) -> p kt i", kt=32),
                                    in1=sv[:, :, :, 1], op=ALU.mult
                                    ).then_inc(s_dqg if m == "g" else s_dqu, 1)

        @blk.sync
        def _(e):
            for c in range(NCH):
                if c >= 2:
                    e.wait_ge(s_dqu, c - 1)
                    e.wait_ge(s_peu, 4 * (c - 1))
                e.dma_start(out=pk["g"][:, c % 2, :],
                            in_=pkg_dram[c].rearrange("p kt w -> p (kt w)")
                            ).then_inc(s_pk[c % 2], 16)
                e.dma_start(out=pk["u"][:, c % 2, :],
                            in_=pku_dram[c].rearrange("p kt w -> p (kt w)")
                            ).then_inc(s_pk[c % 2], 16)
                e.dma_start(out=nz["g"][:, c % 2, :], in_=nzg_dram[:, c * 256:(c + 1) * 256]
                            ).then_inc(s_pk[c % 2], 16)
                e.dma_start(out=nz["u"][:, c % 2, :], in_=nzu_dram[:, c * 256:(c + 1) * 256]
                            ).then_inc(s_pk[c % 2], 16)
                if c >= 1:
                    e.wait_ge(s_dqg, c)
                e.dma_start(out=srep["g"][:],
                            in_=sgcm_dram[c].rearrange("g w -> (g w)")
                            .unsqueeze(0).partition_broadcast(128).squeeze(1)
                            ).then_inc(s_srg, 16)
                if c >= 1:
                    e.wait_ge(s_dqu, c)
                e.dma_start(out=srep["u"][:],
                            in_=sucm_dram[c].rearrange("g w -> (g w)")
                            .unsqueeze(0).partition_broadcast(128).squeeze(1)
                            ).then_inc(s_sru, 16)

        @blk.scalar
        def _(e):
            for c in range(NCH):
                for k in range(4):
                    G = c * 4 + k
                    e.wait_ge(s_peg, G + 1)
                    if G >= 2:
                        e.wait_ge(s_ht, G - 1)
                    e.activation(out=hg[:, k % 2, :], in_=ps_g[k][:],
                                 func=AF.Silu).then_inc(s_sig, 1)
                    e.wait_ge(s_peu, G + 1)
                    e.copy(hu[:, k % 2, :], ps_u[k % 2][:]).then_inc(s_cu, 1)

        @blk.tensor
        def _(e):
            for c in range(NCH):
                e.wait_ge(s_dqg, c + 1)
                for k in range(4):
                    mc, nt = divmod(k, 2)
                    G = c * 4 + k
                    if c >= 1:
                        e.wait_ge(s_sig, 4 * (c - 1) + k + 1)
                    for kt in range(kt_n):
                        e.matmul(ps_g[k][:],
                                 qs["g"][:, kt * 256 + nt * 128: kt * 256 + (nt + 1) * 128],
                                 y_sb[:, kt * M + mc * MC: kt * M + (mc + 1) * MC],
                                 start=(kt == 0), stop=False)
                    e.matmul(ps_g[k][:], nz["g"][:, c % 2, nt * 128:(nt + 1) * 128],
                             xsum_sb[:, mc * MC:(mc + 1) * MC], start=False, stop=True
                             ).then_inc(s_peg, 1)
                e.wait_ge(s_dqu, c + 1)
                for k in range(4):
                    mc, nt = divmod(k, 2)
                    G = c * 4 + k
                    if G >= 2:
                        e.wait_ge(s_cu, G - 1)
                    for kt in range(kt_n):
                        e.matmul(ps_u[k % 2][:],
                                 qs["u"][:, kt * 256 + nt * 128: kt * 256 + (nt + 1) * 128],
                                 y_sb[:, kt * M + mc * MC: kt * M + (mc + 1) * MC],
                                 start=(kt == 0), stop=False)
                    e.matmul(ps_u[k % 2][:], nz["u"][:, c % 2, nt * 128:(nt + 1) * 128],
                             xsum_sb[:, mc * MC:(mc + 1) * MC], start=False, stop=True
                             ).then_inc(s_peu, 1)
                e.wait_ge(s_ht, 4 * (c + 1))
                for k in range(4):
                    mc, nt = divmod(k, 2)
                    e.matmul(ps_hs[:, mc * MC:(mc + 1) * MC],
                             stair[:, 85 - (2 * c + nt): 171 - (2 * c + nt)][:, 0:86],
                             htile[:, k, :],
                             start=(c == 0 and nt == 0), stop=(c == NCH - 1 and nt == 1))
                e.nop().then_inc(s_hsm, 1)

        @blk.gpsimd
        def _(e):
            for c in range(NCH):
                for k in range(4):
                    mc, nt = divmod(k, 2)
                    G = c * 4 + k
                    e.wait_ge(s_sig, G + 1)
                    e.wait_ge(s_cu, G + 1)
                    if c >= 1:
                        e.wait_ge(s_hsm, c)
                        e.wait_ge(s_hst[k], 16 * c)
                    e.tensor_tensor(out=htile[:, k, :], in0=hg[:, k % 2, :],
                                    in1=hu[:, k % 2, :], op=ALU.mult).then_inc(s_ht, 1)
                    e.wait_ge(s_ht, c * 4 + k + 1)
                    e.dma_start(out=h_dram[(2 * c + nt) * 128:(2 * c + nt + 1) * 128,
                                           mc * MC:(mc + 1) * MC],
                                in_=htile[:, k, :]).then_inc(s_hst[k], 16)
            for k in range(4):
                e.wait_ge(s_hst[k], 16 * NCH)

    with nc.Block() as blk:
        @blk.scalar
        def _(e):
            e.copy(hsum_sb[:], ps_hs[:]).then_inc(s_fin, 1)

        @blk.gpsimd
        def _(e):
            e.wait_ge(s_fin, 1)
            e.dma_start(out=hsum_dram[:], in_=hsum_sb[:]).then_inc(s_fin, 16)
            e.wait_ge(s_fin, 17)


def build_gateup_prog():
    P = Prog("gateup")
    with P.es:
        y = P.din("y_in", [D, M], BF16)
        pkg = P.din("pk_g", [43, 128, 32, 32], I32)
        pku = P.din("pk_u", [43, 128, 32, 32], I32)
        scg = P.din("sc_g", [32, II], F32)
        scu = P.din("sc_u", [32, II], F32)
        qzg = P.din("qz_g", [32, 1376], I32)
        qzu = P.din("qz_u", [32, 1376], I32)
        h = P.dout("h_out", [II, M], BF16)
        hsum = P.dout("hsum_out", [86, M], BF16)
        sgcm = P.dint("sg_cm", [43, 32, 256], BF16)
        sucm = P.dint("su_cm", [43, 32, 256], BF16)
        nzg = P.dint("nzg", [32, II], BF16)
        nzu = P.dint("nzu", [32, II], BF16)

        y_sb = P.sb("y", [128, 32 * M], BF16)
        xsum_sb = P.sb("xs", [32, M], BF16)
        stair32 = P.sb("st32", [128, 63], BF16)
        xs_es = contextlib.ExitStack()
        ps_xs = xs_es.enter_context(P.nc.psum_tensor("pxs", [32, M], F32))
        ld = P.sem("yld")
        xs = P.sem("xsem")

        with P.nc.Block() as blk:
            @blk.sync
            def _(e):
                load_rows(e, y_sb, y, 32, ld)

            @blk.vector
            def _(e):
                e.memset(stair32[:], 0.0)
                e.drain()
                e.memset(stair32[:, 31:32], 1.0).then_inc(xs, 1)

            @blk.tensor
            def _(e):
                e.wait_ge(xs, 1)
                e.wait_ge(ld, 16 * 4)
                for kt in range(32):
                    for mc in range(2):
                        mi = e.matmul(ps_xs[:, mc * MC:(mc + 1) * MC],
                                      stair32[:, 31 - kt: 63 - kt][:, 0:32],
                                      y_sb[:, kt * M + mc * MC: kt * M + (mc + 1) * MC],
                                      start=(kt == 0), stop=(kt == 31))
                mi.then_inc(xs, 1)

            @blk.scalar
            def _(e):
                e.wait_ge(xs, 2)
                e.copy(xsum_sb[:], ps_xs[:]).then_inc(xs, 1)
        xs_es.close()

        emit_ffn_prep(P, scg, qzg, sgcm, nzg, 32, 43, 1376, "spg")
        emit_ffn_prep(P, scu, qzu, sucm, nzu, 32, 43, 1376, "spu")
        emit_gateup(P, y_sb, pkg, pku, sgcm, sucm, nzg, nzu, xsum_sb, h, hsum)
    return P



def emit_down(P, h_dram, hsum_sb, pkd_dram, sdcm_dram, nzd_dram, z_dram, tag="dn"):
    nc = P.nc
    NDC = 16
    kt_n = 86
    KH = 43

    hh_sb = P.sb(tag + "_hh", [128, kt_n * MC], BF16)
    pk_sb = P.sb(tag + "_pk", [128, kt_n * 32], I32)
    hi = P.sb(tag + "_hi", [128, KH * 32], I32)
    srep = P.sb(tag + "_sr", [128, kt_n * 256], BF16)
    qs = P.sb(tag + "_qs", [128, kt_n * 256], BF16)
    nzd = P.sb(tag + "_nz", [86, 256], BF16)
    out_sb = P.sb(tag + "_o", [128, 4, MC], BF16)

    psum = [P.ps(tag + f"_p{i}", [128, MC]) for i in range(4)]

    s_hl = P.sem(tag + "_shl")
    s_pkh = [P.sem(tag + f"_spk{i}") for i in range(2)]
    s_nz = P.sem(tag + "_snz")
    s_srh = [P.sem(tag + f"_ssr{i}") for i in range(2)]
    s_dq = P.sem(tag + "_sdq")
    s_ph = P.sem(tag + "_sph")
    s_pe = P.sem(tag + "_spe")
    s_ev = P.sem(tag + "_sev")
    s_st = [P.sem(tag + f"_sst{i}") for i in range(4)]

    def deq_half(e, i, h):
        e.wait_ge(s_srh[h], 16 * (i + 1))
        e.wait_ge(s_pkh[h], 16 * (i + 1))
        if h == 1:
            e.wait_ge(s_nz, 16 * (i + 1))
        if i >= 1:
            if h == 0:
                e.wait_ge(s_ph, i)
            else:
                e.wait_ge(s_pe, 2 * i)
        pkh = pk_sb[:, h * KH * 32:(h + 1) * KH * 32]
        e.tensor_scalar(out=hi[:], in0=pkh, scalar1=4, scalar2=MASK32,
                        op0=ALU.logical_shift_right, op1=ALU.bitwise_and)
        e.drain()
        e.tensor_single_scalar(out=pkh, in_=pkh, scalar=MASK32, op=ALU.bitwise_and)
        e.drain()
        qv = qs[:, h * KH * 256:(h + 1) * KH * 256].rearrange(
            "p (kt i two) -> p kt i two", kt=KH, two=2)
        sv = srep[:, h * KH * 256:(h + 1) * KH * 256].rearrange(
            "p (kt i two) -> p kt i two", kt=KH, two=2)
        e.tensor_tensor(out=qv[:, :, :, 0],
                        in0=pkh.bitcast(U8).rearrange("p (kt i) -> p kt i", kt=KH),
                        in1=sv[:, :, :, 0], op=ALU.mult)
        e.tensor_tensor(out=qv[:, :, :, 1],
                        in0=hi[:].bitcast(U8).rearrange("p (kt i) -> p kt i", kt=KH),
                        in1=sv[:, :, :, 1], op=ALU.mult).then_inc(s_dq, 1)

    with nc.Block() as blk:
        @blk.sync
        def _(e):
            for mc in range(2):
                if mc == 1:
                    e.wait_ge(s_pe, NDC * 2)
                for k0 in range(0, kt_n, 8):
                    k1 = min(kt_n, k0 + 8)
                    e.dma_start(
                        out=hh_sb[:, k0 * MC:k1 * MC],
                        in_=h_dram.rearrange("(kt p) m -> p kt m", p=128)[
                            :, k0:k1, mc * MC:(mc + 1) * MC],
                    ).then_inc(s_hl, 16)
                for dc in range(NDC):
                    i = mc * NDC + dc
                    if i >= 1:
                        e.wait_ge(s_dq, 2 * i)
                    for h in range(2):
                        e.dma_start(out=pk_sb[:, h * KH * 32:(h + 1) * KH * 32],
                                    in_=pkd_dram[dc][:, h * KH:(h + 1) * KH, :]
                                    .rearrange("p kt w -> p (kt w)")).then_inc(s_pkh[h], 16)
                    for h in range(2):
                        e.dma_start(
                            out=srep[:, h * KH * 256:(h + 1) * KH * 256],
                            in_=sdcm_dram[dc][h * KH:(h + 1) * KH, :]
                            .rearrange("g w -> (g w)")
                            .unsqueeze(0).partition_broadcast(128).squeeze(1)
                        ).then_inc(s_srh[h], 16)
                    if i >= 1:
                        e.wait_ge(s_pe, 2 * i)
                    e.dma_start(out=nzd[:], in_=nzd_dram[:, dc * 256:(dc + 1) * 256]
                                ).then_inc(s_nz, 16)

        @blk.scalar
        def _(e):
            for mc in range(2):
                for dc in range(NDC):
                    i = mc * NDC + dc
                    for nt in range(2):
                        G = i * 2 + nt
                        e.wait_ge(s_pe, G + 1)
                        if G >= 4:
                            e.wait_ge(s_st[G % 4], 16 * (G // 4))
                        e.copy(out_sb[:, G % 4, :], psum[G % 4][:]).then_inc(s_ev, 1)

        @blk.vector
        def _(e):
            for mc in range(2):
                for dc in range(NDC):
                    i = mc * NDC + dc
                    deq_half(e, i, 0)
                    deq_half(e, i, 1)

        @blk.tensor
        def _(e):
            for mc in range(2):
                for dc in range(NDC):
                    i = mc * NDC + dc
                    e.wait_ge(s_dq, 2 * i + 1)
                    if mc == 0 and dc == 0:
                        e.wait_ge(s_hl, 16 * 11)
                    if mc == 1 and dc == 0:
                        e.wait_ge(s_hl, 16 * 22)
                    for nt in range(2):
                        G = i * 2 + nt
                        if G >= 4:
                            e.wait_ge(s_ev, G - 3)
                        for kt in range(KH):
                            mi = e.matmul(psum[G % 4][:],
                                          qs[:, kt * 256 + nt * 128: kt * 256 + (nt + 1) * 128],
                                          hh_sb[:, kt * MC:(kt + 1) * MC],
                                          start=(kt == 0), stop=False)
                        if nt == 1:
                            mi.then_inc(s_ph, 1)
                    e.wait_ge(s_dq, 2 * i + 2)
                    for nt in range(2):
                        G = i * 2 + nt
                        for kt in range(KH, kt_n):
                            e.matmul(psum[G % 4][:],
                                     qs[:, kt * 256 + nt * 128: kt * 256 + (nt + 1) * 128],
                                     hh_sb[:, kt * MC:(kt + 1) * MC],
                                     start=False, stop=False)
                        e.matmul(psum[G % 4][:],
                                 nzd[:, nt * 128:(nt + 1) * 128],
                                 hsum_sb[:, mc * MC:(mc + 1) * MC], start=False, stop=True
                                 ).then_inc(s_pe, 1)

        @blk.gpsimd
        def _(e):
            for mc in range(2):
                for dc in range(NDC):
                    for nt in range(2):
                        G = (mc * NDC + dc) * 2 + nt
                        e.wait_ge(s_ev, G + 1)
                        r0 = (2 * dc + nt) * 128
                        e.dma_start(out=z_dram[r0:r0 + 128, mc * MC:(mc + 1) * MC],
                                    in_=out_sb[:, G % 4, :]).then_inc(s_st[G % 4], 16)
            for i in range(4):
                e.wait_ge(s_st[i], 16 * NDC)


def build_down_prog():
    P = Prog("down")
    with P.es:
        h = P.din("h_in", [II, M], BF16)
        hsum = P.din("hsum_in", [86, M], BF16)
        pkd = P.din("pk_d", [16, 128, 86, 32], I32)
        scd = P.din("sc_d", [86, D], F32)
        qzd = P.din("qz_d", [86, 512], I32)
        z = P.dout("z_out", [D, M], BF16)
        sdcm = P.dint("sd_cm", [16, 86, 256], BF16)
        nzd = P.dint("nzd", [86, D], BF16)

        hsum_sb = P.sb("hs", [86, M], BF16)
        ld = P.sem("hld")
        with P.nc.Block() as blk:
            @blk.sync
            def _(e):
                e.dma_start(out=hsum_sb[:], in_=hsum[:]).then_inc(ld, 16)

            @blk.vector
            def _(e):
                e.wait_ge(ld, 16)
        emit_ffn_prep(P, scd, qzd, sdcm, nzd, 86, 16, 512, "spd")
        emit_down(P, h, hsum_sb, pkd, sdcm, nzd, z)
    return P



_PROGS = {}


def _prog(name, builder):
    if name not in _PROGS:
        _PROGS[name] = builder()
    return _PROGS[name]


def _run(P, per_core_inputs):
    res = run_bass_kernel_spmd(P.nc, per_core_inputs, core_ids=list(range(NCORES)))
    return res.results


def kernel_multilaunch(**inputs):
    sh, pc = prep_inputs(inputs)
    x = [pc[c]["x_in"] for c in range(NCORES)]

    p_ln = _prog("preln", lambda: build_ln_prog(False))
    p_post = _prog("postln", lambda: build_ln_prog(True))
    p_qkv = _prog("qkv", build_qkv_prog)
    p_attn = _prog("attn", build_attn_prog)
    p_wo = _prog("wo", build_wo_prog)
    p_gu = _prog("gateup", build_gateup_prog)
    p_dn = _prog("down", build_down_prog)

    for L in range(2):
        pre = f"L{L}_"
        r = _run(p_ln, [{"x_in": x[c]} for c in range(NCORES)])
        y = [r[c]["y_out"] for c in range(NCORES)]
        r = _run(p_qkv, [{"y_in": y[c], "w_in": sh[pre + "wqkv"]} for c in range(NCORES)])
        qk = [r[c]["qk_out"] for c in range(NCORES)]
        ins = []
        for c in range(NCORES):
            part = c ^ 1
            ins.append({
                "qk_in": qk[c],
                "kp_in": np.ascontiguousarray(qk[part][32 * 128:33 * 128, :]),
                "vp_in": np.ascontiguousarray(qk[part][33 * 128:34 * 128, :]),
                "bias_tab": pc[c]["bias_tab"],
                "ident": pc[c]["ident"],
            })
        r = _run(p_attn, ins)
        at = [r[c]["at_out"] for c in range(NCORES)]
        r = _run(p_wo, [{"at_in": at[c], "w_in": sh[pre + "wo"]} for c in range(NCORES)])
        z = [r[c]["z_out"] for c in range(NCORES)]
        r = _run(p_post, [{"z_in": z[c], "res_in": x[c]} for c in range(NCORES)])
        x = [r[c]["x_out"] for c in range(NCORES)]
        r = _run(p_ln, [{"x_in": x[c]} for c in range(NCORES)])
        y2 = [r[c]["y_out"] for c in range(NCORES)]
        r = _run(p_gu, [{"y_in": y2[c], "pk_g": sh[pre + "pk_g"], "pk_u": sh[pre + "pk_u"],
                         "sc_g": sh[pre + "sc_g"], "sc_u": sh[pre + "sc_u"],
                         "qz_g": sh[pre + "qz_g"], "qz_u": sh[pre + "qz_u"]}
                        for c in range(NCORES)])
        h = [r[c]["h_out"] for c in range(NCORES)]
        hs = [r[c]["hsum_out"] for c in range(NCORES)]
        r = _run(p_dn, [{"h_in": h[c], "hsum_in": hs[c], "pk_d": sh[pre + "pk_d"],
                         "sc_d": sh[pre + "sc_d"], "qz_d": sh[pre + "qz_d"]}
                        for c in range(NCORES)])
        z2 = [r[c]["z_out"] for c in range(NCORES)]
        r = _run(p_post, [{"z_in": z2[c], "res_in": x[c]} for c in range(NCORES)])
        x = [r[c]["x_out"] for c in range(NCORES)]

    return unprep_output(x)



MAGIC0 = 0x5EED0001
MAGIC1 = 0x5EED0002


def stage_preln(P, src_dram, dst_dram, tag):
    with P.stage():
        emit_ln(P, src_dram, dst_dram, tag=tag)


def stage_postln(P, z_dram, res_dram, dst_dram, tag):
    with P.stage():
        emit_ln(P, z_dram, dst_dram, res_dram=res_dram, tag=tag)


def stage_proj(P, rhs_dram, w_dram, out_dram, nt_n, qscale_tiles, tag):
    with P.stage():
        rhs_sb = P.sb(tag + "_rhs", [128, 32 * M], BF16)
        ld = P.sem(tag + "_rld")
        with P.nc.Block() as blk:
            @blk.sync
            def _(e):
                load_rows(e, rhs_sb, rhs_dram, 32, ld)

            @blk.vector
            def _(e):
                e.wait_ge(ld, 16 * 4)
        sc_ap = None
        if qscale_tiles:
            cst = make_consts(P, [SCALE_Q], tag=tag + "_c")
            sc_ap = cst[SCALE_Q]
        emit_proj_f32w(P, w_dram, rhs_sb, out_dram, 32, nt_n,
                       qscale_tiles=qscale_tiles, tag=tag, scale_ap=sc_ap)


def stage_exchange(P, qk_dram, kp_dram, vp_dram, shkv, shfl, magic, tag):
    nc = P.nc
    with P.stage():
        kv_sb = P.sb(tag + "_kv", [128, 2048], BF16)
        fl_sb = P.sb(tag + "_fl", [1, 8], I32)
        fr_sb = P.sb(tag + "_fr", [1, 8], I32)
        ld = P.sem(tag + "_ld")
        wr = P.sem(tag + "_wr")
        ps_ = P.sem(tag + "_ps")
        with nc.Block() as blk:
            @blk.sync
            def _(e):
                e.dma_start(out=kv_sb[:].rearrange("p (kt m) -> p kt m", kt=2),
                            in_=qk_dram[32 * 128:34 * 128, :]
                            .rearrange("(kt p) m -> p kt m", p=128)).then_inc(ld, 16)

            @blk.vector
            def _(e):
                e.memset(fl_sb[:], magic)
                e.wait_ge(ld, 16)
                e.tensor_copy(kv_sb[0:1, 0:1], kv_sb[0:1, 0:1]).then_inc(ld, 1)

            @blk.scalar
            def _(e):
                e.wait_ge(ld, 17)
                for arm in nc.switch_core_id(engines=[e], n=128):
                    s = int(arm) % 2
                    e.dma_start(out=shkv[s], in_=kv_sb[:]).then_inc(wr, 16)
                    e.wait_ge(wr, 16)
                    e.dma_start(out=shfl[s], in_=fl_sb[:]).then_inc(wr, 16)
                    e.wait_ge(wr, 32)
                    with e.register(f"{tag}acc{int(arm)}") as racc, \
                         e.register(f"{tag}fl{int(arm)}") as rflag, \
                         e.register(f"{tag}cd{int(arm)}") as rcond:
                        e.reg_mov(racc, 0)

                        def cond(e=e, racc=racc, rflag=rflag, rcond=rcond, s=s):
                            e.reg_alu(racc, racc, 16, mybir.AluOpType.add)
                            e.dma_start(out=fr_sb[:], in_=shfl[1 - s]).then_inc(ps_, 16)
                            e.wait_ge(ps_, racc)
                            e.load(rflag, fr_sb[0:1, 0:1])
                            e.reg_alu(rcond, rflag, magic, mybir.AluOpType.not_equal)
                            return rcond
                        with e.While(cond):
                            e.nop()
                    e.dma_start(out=kp_dram[:],
                                in_=shkv[1 - s][:, 0:1024]).then_inc(wr, 16)
                    e.dma_start(out=vp_dram[:],
                                in_=shkv[1 - s][:, 1024:2048]).then_inc(wr, 16)
                    e.wait_ge(wr, 64)


def stage_attn(P, qk_dram, kp_dram, vp_dram, bias_dram, id_dram, out_dram, tag):
    with P.stage():
        emit_attn(P, qk_dram, kp_dram, vp_dram, bias_dram, id_dram, out_dram, tag=tag)


def stage_gateup(P, y_dram, pkg, pku, scg, scu, qzg, qzu, h_dram, hsum_dram,
                 sgcm, sucm, nzg, nzu, tag):
    with P.stage():
        y_sb = P.sb(tag + "_y", [128, 32 * M], BF16)
        xsum_sb = P.sb(tag + "_xs", [32, M], BF16)
        stair32 = P.sb(tag + "_s32", [128, 63], BF16)
        xs_es = contextlib.ExitStack()
        ps_xs = xs_es.enter_context(P.nc.psum_tensor(tag + "_pxs", [32, M], F32))
        ld = P.sem(tag + "_yld")
        xs = P.sem(tag + "_xsem")
        with P.nc.Block() as blk:
            @blk.sync
            def _(e):
                load_rows(e, y_sb, y_dram, 32, ld)

            @blk.vector
            def _(e):
                e.memset(stair32[:], 0.0)
                e.drain()
                e.memset(stair32[:, 31:32], 1.0).then_inc(xs, 1)

            @blk.tensor
            def _(e):
                e.wait_ge(xs, 1)
                e.wait_ge(ld, 16 * 4)
                for kt in range(32):
                    for mc in range(2):
                        mi = e.matmul(ps_xs[:, mc * MC:(mc + 1) * MC],
                                      stair32[:, 31 - kt: 63 - kt][:, 0:32],
                                      y_sb[:, kt * M + mc * MC: kt * M + (mc + 1) * MC],
                                      start=(kt == 0), stop=(kt == 31))
                mi.then_inc(xs, 1)

            @blk.scalar
            def _(e):
                e.wait_ge(xs, 2)
                e.copy(xsum_sb[:], ps_xs[:]).then_inc(xs, 1)
        xs_es.close()

        emit_ffn_prep(P, scg, qzg, sgcm, nzg, 32, 43, 1376, tag + "g")
        emit_ffn_prep(P, scu, qzu, sucm, nzu, 32, 43, 1376, tag + "u")
        emit_gateup(P, y_sb, pkg, pku, sgcm, sucm, nzg, nzu, xsum_sb, h_dram,
                    hsum_dram, tag=tag)


def stage_down(P, h_dram, hsum_dram, pkd, scd, qzd, z_dram, sdcm, nzd, tag):
    with P.stage():
        hsum_sb = P.sb(tag + "_hs", [86, M], BF16)
        ld = P.sem(tag + "_hld")
        with P.nc.Block() as blk:
            @blk.sync
            def _(e):
                e.dma_start(out=hsum_sb[:], in_=hsum_dram[:]).then_inc(ld, 16)

            @blk.vector
            def _(e):
                e.wait_ge(ld, 16)
        emit_ffn_prep(P, scd, qzd, sdcm, nzd, 86, 16, 512, tag + "d")
        emit_down(P, h_dram, hsum_sb, pkd, sdcm, nzd, z_dram, tag=tag)


def build_mega():
    P = Prog("mega")
    with P.es:
        nc = P.nc
        x_in = P.din("x_in", [D, M], F32)
        bias = P.din("bias_tab", [128, 9216], F32)
        ident = P.din("ident", [128, 128], BF16)
        x_out = P.dout("x_out", [D, M], F32)

        w = {}
        for L in range(2):
            pre = f"L{L}_"
            w[pre + "wqkv"] = P.din(pre + "wqkv", [D, 4352], F32)
            w[pre + "wo"] = P.din(pre + "wo", [D, D], F32)
            w[pre + "pk_g"] = P.din(pre + "pk_g", [43, 128, 32, 32], I32)
            w[pre + "pk_u"] = P.din(pre + "pk_u", [43, 128, 32, 32], I32)
            w[pre + "pk_d"] = P.din(pre + "pk_d", [16, 128, 86, 32], I32)
            w[pre + "sc_g"] = P.din(pre + "sc_g", [32, II], F32)
            w[pre + "sc_u"] = P.din(pre + "sc_u", [32, II], F32)
            w[pre + "sc_d"] = P.din(pre + "sc_d", [86, D], F32)
            w[pre + "qz_g"] = P.din(pre + "qz_g", [32, 1376], I32)
            w[pre + "qz_u"] = P.din(pre + "qz_u", [32, 1376], I32)
            w[pre + "qz_d"] = P.din(pre + "qz_d", [86, 512], I32)

        x_st = P.dint("x_st", [D, M], F32)
        y_d = P.dint("y_d", [D, M], BF16)
        qk_d = P.dint("qk_d", [4352, M], BF16)
        kp_d = P.dint("kp_d", [128, M], BF16)
        vp_d = P.dint("vp_d", [128, M], BF16)
        at_d = P.dint("at_d", [D, M], BF16)
        z_d = P.dint("z_d", [D, M], BF16)
        h_d = P.dint("h_d", [II, M], BF16)
        hs_d = P.dint("hs_d", [86, M], BF16)
        sgcm = P.dint("sgcm", [43, 32, 256], BF16)
        sucm = P.dint("sucm", [43, 32, 256], BF16)
        sdcm = P.dint("sdcm", [16, 86, 256], BF16)
        nzg = P.dint("nzg", [32, II], BF16)
        nzu = P.dint("nzu", [32, II], BF16)
        nzd = P.dint("nzd", [86, D], BF16)
        shkv = [[P.dint(f"shkv{L}_{s}", [128, 2048], BF16, shared=True)
                 for s in range(2)] for L in range(2)]
        shfl = [[P.dint(f"shfl{L}_{s}", [1, 8], I32, shared=True)
                 for s in range(2)] for L in range(2)]

        for L in range(2):
            pre = f"L{L}_"
            xsrc = x_in if L == 0 else x_st
            stage_preln(P, xsrc, y_d, tag=f"ln1_{L}")
            stage_proj(P, y_d, w[pre + "wqkv"], qk_d, 34, 32, tag=f"qkv_{L}")
            stage_exchange(P, qk_d, kp_d, vp_d, shkv[L], shfl[L],
                           MAGIC0 + L, tag=f"xc_{L}")
            stage_attn(P, qk_d, kp_d, vp_d, bias, ident, at_d, tag=f"at_{L}")
            stage_proj(P, at_d, w[pre + "wo"], z_d, 32, 0, tag=f"wo_{L}")
            stage_postln(P, z_d, xsrc, x_st, tag=f"pl1_{L}")
            stage_preln(P, x_st, y_d, tag=f"ln3_{L}")
            stage_gateup(P, y_d, w[pre + "pk_g"], w[pre + "pk_u"],
                         w[pre + "sc_g"], w[pre + "sc_u"],
                         w[pre + "qz_g"], w[pre + "qz_u"],
                         h_d, hs_d, sgcm, sucm, nzg, nzu, tag=f"gu_{L}")
            stage_down(P, h_d, hs_d, w[pre + "pk_d"], w[pre + "sc_d"],
                       w[pre + "qz_d"], z_d, sdcm, nzd, tag=f"dn_{L}")
            dst = x_st if L == 0 else x_out
            stage_postln(P, z_d, x_st, dst, tag=f"pl2_{L}")
    return P


_MEGA = [None]


def kernel(**inputs):
    sh, pc = prep_inputs(inputs)
    if _MEGA[0] is None:
        _MEGA[0] = build_mega()
    P = _MEGA[0]
    in_maps = []
    for c in range(NCORES):
        m = {"x_in": pc[c]["x_in"], "bias_tab": pc[c]["bias_tab"],
             "ident": pc[c]["ident"]}
        for L in range(2):
            pre = f"L{L}_"
            for k in ["wqkv", "wo", "pk_g", "pk_u", "pk_d", "sc_g", "sc_u",
                      "sc_d", "qz_g", "qz_u", "qz_d"]:
                m[pre + k] = sh[pre + k]
        in_maps.append(m)
    res = run_bass_kernel_spmd(P.nc, in_maps, core_ids=list(range(NCORES)))
    return unprep_output([res.results[c]["x_out"] for c in range(NCORES)])
